# version 24
# baseline (speedup 1.0000x reference)
from contextlib import ExitStack

import numpy as np

import concourse.bass as bass
import concourse.mybir as mybir
from concourse.bass_utils import run_bass_kernel_spmd

F32 = mybir.dt.float32
BF16 = mybir.dt.bfloat16
F16 = mybir.dt.float16
I8 = mybir.dt.int8
AX = mybir.AxisListType.X
OP = mybir.AluOpType
AF = mybir.ActivationFunctionType

D = 2048
S = 2048
H = 16
NC_ = 16
NT = 16
QT = 4
INV_SQRT_HD = 1.0 / np.sqrt(128.0)
MAGIC = 12582912.0
LN_EPS = 1e-5
Q_EPS = 1e-7
INV_D = 1.0 / 2048.0
INV127 = 1.0 / 127.0


class ChainedEng:

    def __init__(self, eng, sem, last=None):
        self._e, self._s = eng, sem
        self.last = last

    def __getattr__(self, name):
        f = getattr(self._e, name)

        def g(*a, inc=None, **k):
            if self.last is not None:
                self._e.wait_ge(self.last[0], self.last[1])
            inst = f(*a, **k)
            if inc is None:
                if self.last is not None and self.last[0] is self._s:
                    nxt = self.last[1] + 1
                else:
                    nxt = self._n + 1
                inst.then_inc(self._s, 1)
                self._n = nxt
                self.last = (self._s, nxt)
            else:
                sem, kk, after = inc
                inst.then_inc(sem, kk)
                self.last = (sem, after)
            return inst

        return g

    @property
    def _n(self):
        return self.__dict__.setdefault("_nv", 0)

    @_n.setter
    def _n(self, v):
        self.__dict__["_nv"] = v

    def wait_ge(self, sem, v):
        return self._e.wait_ge(sem, v)


def build_program():
    nc = bass.Bass("TRN2", target_bir_lowering=False)

    xT_d = nc.dram_tensor("xT", [D, S], F32, kind="ExternalInput").ap()
    x_d = nc.dram_tensor("x", [S, D], F32, kind="ExternalInput").ap()
    wqkv_d = nc.dram_tensor("wqkv", [H, 128, 5 * 16 * 128], F32, kind="ExternalInput").ap()
    wo_d = nc.dram_tensor("wo", [D, D], F32, kind="ExternalInput").ap()
    wffn_d = nc.dram_tensor("wffn", [D, D], I8, kind="ExternalInput").ap()
    wsb_d = nc.dram_tensor("wsb", [128, D], F32, kind="ExternalInput").ap()
    bb_d = nc.dram_tensor("bb", [128, D], F32, kind="ExternalInput").ap()
    lam_d = nc.dram_tensor("lam", [128, H], F32, kind="ExternalInput").ap()
    out_d = nc.dram_tensor("out", [S, D], F32, kind="ExternalOutput").ap()
    osc_d = nc.dram_tensor("osc", [NT, H, 128, 128], BF16).ap()
    wob_d = nc.dram_tensor("wob", [NC_, 128, D], BF16).ap()
    wfb_d = nc.dram_tensor("wfb", [NC_, 128, D], BF16).ap()
    ysc_d = nc.dram_tensor("ysc", [NT, 128, D], F16).ap()

    with ExitStack() as top:
        E = top.enter_context
        ones_sb = E(nc.sbuf_tensor("ones", [128, 128], BF16))
        id_sb = E(nc.sbuf_tensor("ident", [128, 128], BF16))
        lam_sb = E(nc.sbuf_tensor("lam_sb", [128, H], F32))
        neglam_sb = E(nc.sbuf_tensor("neglam", [128, H], F32))
        scale_all = E(nc.sbuf_tensor("scale_all", [128, NT], F32))
        (sWQ, sXT, sPM, sPD, sS, sEXP, sPVD, sCB, sODMA, sAUX,
         sOI, sXI, sWO, sAR, sZD, sQ, sTR, sTRE, sYD, sYDMA,
         sVST, sSQT, sWF, sWC, sYI, sZ2, sPF, sOUTV, sOUT, sVS2, sSQ2, sID, sVC, sOC,
         sST, sYR, sAFF, sS2T, sAF2, sWO2, sWO3, sWO4, sTQ2, sTQ3, sOC2, sCV, sCS, sXT2, sXT3, sXT4) = [
            E(nc.semaphore(n)) for n in (
                "sWQ", "sXT", "sPM", "sPD", "sS", "sEXP", "sPVD", "sCB",
                "sODMA", "sAUX", "sOI", "sXI", "sWO", "sAR", "sZD", "sQ",
                "sTR", "sTRE", "sYD", "sYDMA", "sVST", "sSQT", "sWF", "sWC",
                "sYI", "sZ2", "sPF", "sOUTV", "sOUT", "sVS2", "sSQ2", "sID", "sVC", "sOC",
                "sST", "sYR", "sAFF", "sS2T", "sAF2", "sWO2", "sWO3", "sWO4", "sTQ2", "sTQ3", "sOC2", "sCV", "sCS", "sXT2", "sXT3", "sXT4")]
        with nc.Block() as blk0:
            @blk0.gpsimd
            def _(g):
                g.memset(ones_sb[:], 1.0)
                g.memset(id_sb[:], 0.0).then_inc(sID, 1)
                g.wait_ge(sID, 1)
                g.affine_select(
                    out=id_sb[:], in_=id_sb[:],
                    compare_op=OP.not_equal, fill=1.0,
                    base=0, pattern=[[-1, 128]], channel_multiplier=1,
                )

        blkM = E(nc.Block())
        with ExitStack() as phb:
            Eb = phb.enter_context
            xT_sb = Eb(nc.sbuf_tensor("xTb", [128, NC_ * S], BF16))
            wq_sb = Eb(nc.sbuf_tensor("wq", [128, 2 * 5 * 16 * 128], BF16))
            q1_sb = Eb(nc.sbuf_tensor("q1", [128, S], BF16))
            k1_sb = Eb(nc.sbuf_tensor("k1", [128, S], BF16))
            q2_sb = Eb(nc.sbuf_tensor("q2", [128, S], BF16))
            k2_sb = Eb(nc.sbuf_tensor("k2", [128, S], BF16))
            v_sb = Eb(nc.sbuf_tensor("vv", [128, S], BF16))
            v2_sb = Eb(nc.sbuf_tensor("v2", [128, S], BF16))
            pb_sb = Eb(nc.sbuf_tensor("pb", [128, 2048], BF16))
            oc_sb = Eb(nc.sbuf_tensor("oc", [128, 3072], F32))
            sc_sb = Eb(nc.sbuf_tensor("sc_", [128, 3072], F32))
            r1_sb = Eb(nc.sbuf_tensor("r1", [128, 512], F32))
            r2_sb = Eb(nc.sbuf_tensor("r2", [128, 512], F32))
            t1_sb = Eb(nc.sbuf_tensor("t1", [128, 512], F32))
            t2_sb = Eb(nc.sbuf_tensor("t2", [128, 512], F32))
            oout_sb = Eb(nc.sbuf_tensor("oout", [128, 1024], BF16))
            stg_sb = Eb(nc.sbuf_tensor("stg", [128, 2 * D], BF16))
            psA = Eb(nc.psum_tensor("psA", [128, 1024], F32))
            psB = Eb(nc.psum_tensor("psB", [128, 1024], F32))
            pO = Eb(nc.psum_tensor("pO", [128, 1024], F32))
            pS = Eb(nc.psum_tensor("pS", [128, 1024], F32))
            blk = blkM
            sbank = [psA, psB]
            projb = [psA, psB]

            sS_after = {}
            _cnt = 0
            for _grp in range(H * QT):
                for _c in range(NC_):
                    _cnt += 2
                    sS_after[(_grp, _c)] = _cnt

            @blk.gpsimd
            def _(g):
                xtsem = [sXT, sXT2, sXT3, sXT4]
                xtv = xT_sb[:].rearrange("p (c t) -> p c t", c=NC_)
                xdv = xT_d.rearrange("(c p) t -> p c t", p=128)
                g.dma_start(out=wq_sb[:, 0:4096], in_=wqkv_d[0, :, 0:4096]).then_inc(sWQ, 16)
                g.dma_start(out=xtv[:, 0:4, :], in_=xdv[:, 0:4, :]).then_inc(sXT, 16)
                g.dma_start(out=xtv[:, 4:8, :], in_=xdv[:, 4:8, :]).then_inc(sXT2, 16)
                g.wait_ge(sWQ, 16)
                g.dma_start(out=wq_sb[:, 4096:10240], in_=wqkv_d[0, :, 4096:10240]).then_inc(sWQ, 16)
                g.dma_start(out=xtv[:, 8:12, :], in_=xdv[:, 8:12, :]).then_inc(sXT3, 16)
                g.dma_start(out=xtv[:, 12:16, :], in_=xdv[:, 12:16, :]).then_inc(sXT4, 16)
                for h in range(1, H):
                    if h >= 2:
                        g.wait_ge(sPM, (h - 1) * 32)
                    g.wait_ge(sWQ, 16 * (h + 1))
                    g.dma_start(out=wq_sb[:, (h % 2) * 10240:(h % 2) * 10240 + 10240],
                                in_=wqkv_d[h]).then_inc(sWQ, 16)
                    if h == 1:
                        for i in range(2 * NC_):
                            src_ap = (wo_d[(i % NC_) * 128:(i % NC_) * 128 + 128, :]
                                      if i < NC_ else
                                      wffn_d[(i % NC_) * 128:(i % NC_) * 128 + 128, :])
                            g.wait_ge(sCV, 16 * i)
                            if i >= 2:
                                g.wait_ge(sCS, 16 * (i - 1))
                            g.dma_start(
                                out=stg_sb[:, (i % 2) * D:(i % 2) * D + D], in_=src_ap
                            ).then_inc(sCV, 16)

            @blk.sync
            def _(sy):
                sy.dma_start(out=lam_sb[:], in_=lam_d).then_inc(sAUX, 16)
                for i in range(2 * NC_):
                    dst = wob_d[i % NC_] if i < NC_ else wfb_d[i % NC_]
                    sy.wait_ge(sCV, 16 * (i + 1))
                    sy.wait_ge(sCS, 16 * i)
                    sy.dma_start(out=dst, in_=stg_sb[:, (i % 2) * D:(i % 2) * D + D]).then_inc(sCS, 16)
                for grp in range(61):
                    sy.wait_ge(sCB, grp + 1)
                    h, qt = grp // QT, grp % QT
                    for j in range(4):
                        sy.wait_ge(sODMA, 64 * grp + 16 * j)
                        sy.dma_start(
                            out=osc_d[qt * 4 + j, h],
                            in_=oout_sb[:, (grp % 2) * 512 + j * 128:(grp % 2) * 512 + (j + 1) * 128],
                        ).then_inc(sODMA, 16)

            @blk.tensor
            def _(te):
                for _s in (sXT, sXT2, sXT3, sXT4):
                    te.wait_ge(_s, 16)
                for h in range(H):
                    te.wait_ge(sWQ, 16 * (h + 2))
                    if h > 0:
                        te.wait_ge(sEXP, 64 * h)

                    def wslice(m, c):
                        base = (h % 2) * 10240 + (m * 16 + c) * 128
                        return wq_sb[:, base:base + 128]

                    def pdval(g):
                        return g + 1 if g < 16 else g + 2

                    g = 0
                    for tt in range(NT):
                        if h > 0 or g >= 2:
                            te.wait_ge(sPD, h * 33 + pdval(g - 2))
                        for c in range(NC_):
                            mm = te.matmul(
                                projb[g % 2][:, 0:128],
                                xT_sb[:, c * S + tt * 128: c * S + tt * 128 + 128],
                                wslice(0, c),
                                start=(c == 0), stop=(c == NC_ - 1),
                            )
                        mm.then_inc(sPM, 1)
                        g += 1
                    for m, _dst in enumerate((q1_sb, k1_sb, q2_sb, k2_sb)):
                        for qt in range(QT):
                            te.wait_ge(sPD, h * 33 + pdval(g - 2))
                            for c in range(NC_):
                                mm = te.matmul(
                                    projb[g % 2][:, 0:512],
                                    wslice(m + 1, c),
                                    xT_sb[:, c * S + qt * 512: c * S + qt * 512 + 512],
                                    start=(c == 0), stop=(c == NC_ - 1),
                                )
                            mm.then_inc(sPM, 1)
                            g += 1

                    for qt in range(QT):
                        grp = h * QT + qt

                        def s_issue(qt_, c):
                            s = (h * QT + qt_) * NC_ + c
                            if qt_ == 0 and c < 2:
                                te.wait_ge(sPD, h * 33 + 32 + c)
                            if s >= 2:
                                te.wait_ge(sEXP, s - 1)
                            for j, (kk, qq) in enumerate(((k1_sb, q1_sb), (k2_sb, q2_sb))):
                                te.matmul(
                                    sbank[c % 2][:, j * 512:(j + 1) * 512],
                                    kk[:, c * 128:(c + 1) * 128],
                                    qq[:, qt_ * 512:(qt_ + 1) * 512],
                                    start=True, stop=True,
                                ).then_inc(sS, 1)

                        def pv_issue(cc):
                            te.wait_ge(sEXP, grp * NC_ + cc + 1)
                            if cc == 0 and grp > 0:
                                te.wait_ge(sOC, grp)
                                te.wait_ge(sOC2, grp)
                            for j, vv in enumerate((v_sb, v2_sb)):
                                pslc = pb_sb[:, (cc % 2) * 1024 + j * 512:(cc % 2) * 1024 + (j + 1) * 512]
                                te.matmul(pO[:, j * 512:(j + 1) * 512], vv[:, cc * 128:(cc + 1) * 128],
                                          pslc, start=(cc == 0), stop=(cc == NC_ - 1))
                            for j in range(2):
                                pslc = pb_sb[:, (cc % 2) * 1024 + j * 512:(cc % 2) * 1024 + (j + 1) * 512]
                                mm = te.matmul(pS[:, j * 512:(j + 1) * 512], ones_sb[:], pslc,
                                               start=(cc == 0), stop=(cc == NC_ - 1))
                            mm.then_inc(sPVD, 1)

                        for c in range(NC_):
                            s_issue(qt, c)
                            if c >= 1:
                                pv_issue(c - 1)
                        pv_issue(NC_ - 1)

            @blk.scalar
            def _(sc):
                for grp in range(H * QT):
                    for c in range(NC_):
                        s = grp * NC_ + c
                        sc.wait_ge(sS, sS_after[(grp, c)])
                        if s >= 2:
                            sc.wait_ge(sPVD, s - 1)
                        sc.activation(
                            pb_sb[:, (c % 2) * 1024:(c % 2) * 1024 + 1024],
                            sbank[c % 2][:], AF.Exp,
                            scale=INV_SQRT_HD,
                        ).then_inc(sEXP, 1)
                    sc.wait_ge(sPVD, NC_ * (grp + 1))
                    if grp >= 3:
                        sc.wait_ge(sCB, grp - 2)
                    sc.copy(oc_sb[:, (grp % 3) * 1024:(grp % 3) * 1024 + 1024],
                            pO[:]).then_inc(sOC2, 1)
                for grp in range(61, H * QT):
                    sc.wait_ge(sCB, grp + 1)
                    h, qt = grp // QT, grp % QT
                    for j in range(4):
                        sc.wait_ge(sODMA, 64 * grp + 16 * j)
                        sc.dma_start(
                            out=osc_d[qt * 4 + j, h],
                            in_=oout_sb[:, (grp % 2) * 512 + j * 128:(grp % 2) * 512 + (j + 1) * 128],
                        ).then_inc(sODMA, 16)


            vchain = [None, 0]

            @blk.vector
            def _(ve):
                ve = ChainedEng(ve, sVC, vchain[0])
                ve._n = vchain[1]

                def divide(g2):
                    b2 = (g2 % 3) * 1024
                    ve.wait_ge(sOC2, g2 + 1)
                    if g2 >= 2:
                        ve.wait_ge(sODMA, 64 * (g2 - 1))
                    ve.reciprocal(r1_sb[:], sc_sb[:, b2:b2 + 512])
                    ve.reciprocal(r2_sb[:], sc_sb[:, b2 + 512:b2 + 1024])
                    ve.tensor_mul(t1_sb[:], oc_sb[:, b2:b2 + 512], r1_sb[:])
                    ve.tensor_mul(t2_sb[:], oc_sb[:, b2 + 512:b2 + 1024], r2_sb[:])
                    ve.tensor_add(oout_sb[:, (g2 % 2) * 512:(g2 % 2) * 512 + 512], t1_sb[:], t2_sb[:], inc=(sCB, 1, g2 + 1))

                ve.wait_ge(sAUX, 16)
                ve.tensor_scalar(neglam_sb[:], lam_sb[:], -1.0, None, OP.mult)
                for h in range(H):
                    g = 0
                    pd = 0
                    for tt in range(NT):
                        ve.wait_ge(sPM, h * 32 + g + 1)
                        pd += 1
                        ve.tensor_copy(
                            v_sb[:, tt * 128:(tt + 1) * 128], projb[g % 2][:, 0:128],
                            inc=(sPD, 1, h * 33 + pd),
                        )
                        g += 1
                    pd += 1
                    ve.tensor_scalar(
                        v2_sb[:], v_sb[:], neglam_sb[:, h:h + 1], None, OP.mult,
                        inc=(sPD, 1, h * 33 + pd),
                    )
                    for m, dst in enumerate((q1_sb, k1_sb, q2_sb, k2_sb)):
                        for qt in range(QT):
                            ve.wait_ge(sPM, h * 32 + g + 1)
                            pd += 1
                            ve.tensor_copy(
                                dst[:, qt * 512:(qt + 1) * 512], projb[g % 2][:, 0:512],
                                inc=(sPD, 1, h * 33 + pd),
                            )
                            g += 1
                    for qt in range(QT):
                        grp = h * QT + qt
                        if grp >= 2:
                            divide(grp - 2)
                        ve.wait_ge(sPVD, NC_ * (grp + 1))
                        if grp >= 3:
                            ve.wait_ge(sCB, grp - 2)
                        b = (grp % 3) * 1024
                        ve.tensor_copy(sc_sb[:, b:b + 1024], pS[:], inc=(sOC, 1, grp + 1))
                divide(H * QT - 2)
                divide(H * QT - 1)
                vchain[0] = ve.last
                vchain[1] = ve._n

        with nc.sbuf_tensor("qT", [128, NC_ * S], BF16, side="right") as qT_sb:
            with ExitStack() as phc:
                Ec = phc.enter_context
                wo_sb = Ec(nc.sbuf_tensor("woS", [128, NC_ * D], BF16))
                ocol_sb = Ec(nc.sbuf_tensor("ocol", [128, 2 * H * 128], BF16))
                xin_sb = Ec(nc.sbuf_tensor("xin", [128, 2 * D], F32))
                z_sb = Ec(nc.sbuf_tensor("zz", [128, 2 * D], F32))
                y_sb = Ec(nc.sbuf_tensor("yy", [128, D], F32))
                t_sb = Ec(nc.sbuf_tensor("tt_", [128, D], F32))
                dump_sb = Ec(nc.sbuf_tensor("dump", [128, D], F32))
                q8_sb = Ec(nc.sbuf_tensor("qi8", [128, D], I8))
                qb_sb = Ec(nc.sbuf_tensor("qbf", [128, D], BF16))
                yf16_sb = Ec(nc.sbuf_tensor("yf16", [128, 2 * D], F16))
                st1_sb = Ec(nc.sbuf_tensor("st1", [128, 2], F32))
                st2_sb = Ec(nc.sbuf_tensor("st2", [128, 2], F32))
                st3_sb = Ec(nc.sbuf_tensor("st3", [128, 1], F32))
                st4_sb = Ec(nc.sbuf_tensor("st4", [128, 1], F32))
                st5_sb = Ec(nc.sbuf_tensor("st5", [128, 1], F32))
                st6_sb = Ec(nc.sbuf_tensor("st6", [128, 1], F32))
                pattnL = Ec(nc.psum_tensor("pattnL", [128, 1024], F32))
                pattnH = Ec(nc.psum_tensor("pattnH", [128, 1024], F32))
                ptrA = Ec(nc.psum_tensor("ptrA", [128, 1024], BF16))
                ptrB = Ec(nc.psum_tensor("ptrB", [128, 1024], BF16))
                blk2 = blkM
                path = [pattnL, pattnH]
                trb = [ptrA, ptrB]
                wosem = [sWO, sWO2, sWO3, sWO4]

                @blk2.sync
                def _(sy):
                    sy.wait_ge(sCS, 32 * NC_)
                    sy.wait_ge(sODMA, 64 * H * QT)
                    sy.wait_ge(sPVD, 16 * H * QT)
                    sy.wait_ge(sEXP, 16 * H * QT)
                    sy.wait_ge(sCB, H * QT)

                    def ywrite(tt):
                        sy.wait_ge(sYD, tt + 1)
                        sy.wait_ge(sYDMA, 16 * tt)
                        sy.dma_start(out=ysc_d[tt], in_=yf16_sb[:, (tt % 2) * D:(tt % 2) * D + D]).then_inc(sYDMA, 16)

                    for tt in range(NT):
                        if tt >= 2:
                            sy.wait_ge(sAR, 2 * (tt - 1))
                        for hq in range(4):
                            sy.wait_ge(sOI, 64 * tt + 16 * hq)
                            sy.dma_start(
                                out=ocol_sb[:, (tt % 2) * 2048 + hq * 512:(tt % 2) * 2048 + (hq + 1) * 512].rearrange("p (h t) -> p h t", h=4),
                                in_=osc_d[tt, hq * 4:(hq + 1) * 4].rearrange("h p t -> p h t"),
                            ).then_inc(sOI, 16)
                        if tt >= 2:
                            sy.wait_ge(sZD, 2 * (tt - 1))
                        sy.wait_ge(sXI, 16 * tt)
                        sy.dma_start(
                            out=xin_sb[:, (tt % 2) * D:(tt % 2) * D + D], in_=x_d[tt * 128:(tt + 1) * 128, :]
                        ).then_inc(sXI, 16)
                        if tt >= 2:
                            ywrite(tt - 2)
                    ywrite(NT - 2)
                    ywrite(NT - 1)

                @blk2.gpsimd
                def _(g):
                    g.wait_ge(sPM, 32 * H)
                    g.wait_ge(sCS, 16 * NC_)
                    for c in range(NC_):
                        sem = wosem[c % 4]
                        g.wait_ge(sem, 16 * (c // 4))
                        g.dma_start(
                            out=wo_sb[:, c * D:(c + 1) * D],
                            in_=wob_d[c],
                        ).then_inc(sem, 16)

                @blk2.tensor
                def _(te):
                    te.wait_ge(sCS, 32 * NC_)
                    te.wait_ge(sEXP, 16 * H * QT)
                    te.wait_ge(sCB, H * QT)
                    te.wait_ge(sOC, H * QT)
                    te.wait_ge(sOC2, H * QT)

                    def transposes(tt):
                        te.wait_ge(sQ, tt + 1)
                        for grp8 in range(2):
                            gg = 2 * tt + grp8
                            if gg >= 2:
                                te.wait_ge(sTRE, gg - 1)
                            for j8 in range(8):
                                j = grp8 * 8 + j8
                                mm = te.transpose(
                                    trb[gg % 2][:, j8 * 128:(j8 + 1) * 128],
                                    qb_sb[:, j * 128:(j + 1) * 128],
                                    id_sb[:],
                                )
                            mm.then_inc(sTR, 1)

                    for tt in range(NT):
                        for half in range(2):
                            if tt >= 1:
                                te.wait_ge(sZD, 2 * tt + half - 1)
                            for c in range(NC_):
                                if half == 0 and c % 4 == 0:
                                    te.wait_ge(sOI, 64 * tt + 16 * (c // 4 + 1))
                                if tt == 0 and half == 0:
                                    te.wait_ge(wosem[c % 4], 16 * (c // 4 + 1))
                                for nt in (half * 2, half * 2 + 1):
                                    mm = te.matmul(
                                        path[half][:, (nt - half * 2) * 512:(nt - half * 2) * 512 + 512],
                                        ocol_sb[:, (tt % 2) * 2048 + c * 128:(tt % 2) * 2048 + (c + 1) * 128],
                                        wo_sb[:, c * D + nt * 512: c * D + nt * 512 + 512],
                                        start=(c == 0), stop=(c == NC_ - 1),
                                    )
                            mm.then_inc(sAR, 1)
                        if tt >= 2:
                            transposes(tt - 2)
                    transposes(NT - 2)
                    transposes(NT - 1)

                @blk2.scalar
                def _(sc):
                    sc.wait_ge(sCS, 32 * NC_)
                    sc.wait_ge(sCB, H * QT)
                    sc.wait_ge(sODMA, 64 * H * QT)

                    def evacs(tt):
                        for grp8 in range(2):
                            gg = 2 * tt + grp8
                            sc.wait_ge(sTR, gg + 1)
                            out_ap = qT_sb[:].rearrange("p (j r) -> p j r", j=NC_)[
                                :, grp8 * 8:(grp8 + 1) * 8, tt * 128:(tt + 1) * 128]
                            sc.copy(
                                out_ap,
                                trb[gg % 2][:].rearrange("p (j r) -> p j r", j=8),
                            ).then_inc(sTRE, 1)

                    for tt in range(NT):
                        if tt >= 1:
                            sc.wait_ge(sVST, tt)
                            sc.activation(st5_sb[:], st6_sb[:], AF.Sqrt).then_inc(sSQT, 1)
                        sc.wait_ge(sZD, 2 * (tt + 1))
                        if tt >= 2:
                            sc.wait_ge(sVST, tt - 1)
                        if tt >= 1:
                            sc.wait_ge(sST, 2 * tt)
                        sc.activation(dump_sb[:], z_sb[:, (tt % 2) * D:(tt % 2) * D + D],
                                      AF.Copy, accum_out=st1_sb[:, tt % 2:tt % 2 + 1]).then_inc(sST, 1)
                        sc.wait_ge(sST, 2 * tt + 1)
                        sc.activation(dump_sb[:], z_sb[:, (tt % 2) * D:(tt % 2) * D + D],
                                      AF.Square, accum_out=st2_sb[:, tt % 2:tt % 2 + 1]).then_inc(sST, 1)
                        if tt >= 2:
                            evacs(tt - 2)
                    sc.wait_ge(sVST, NT)
                    sc.activation(st5_sb[:], st6_sb[:], AF.Sqrt).then_inc(sSQT, 1)
                    evacs(NT - 2)
                    evacs(NT - 1)

                @blk2.vector
                def _(ve):
                    ve = ChainedEng(ve, sVC, vchain[0])
                    ve._n = vchain[1]
                    ve.wait_ge(sCS, 32 * NC_)

                    def post(k):
                        ve.wait_ge(sST, 2 * (k + 1))
                        ve.tensor_scalar(st1_sb[:, k % 2:k % 2 + 1], st1_sb[:, k % 2:k % 2 + 1], INV_D, None, OP.mult)
                        ve.tensor_scalar(st2_sb[:, k % 2:k % 2 + 1], st2_sb[:, k % 2:k % 2 + 1], INV_D, None, OP.mult)
                        ve.tensor_mul(st6_sb[:], st1_sb[:, k % 2:k % 2 + 1], st1_sb[:, k % 2:k % 2 + 1])
                        ve.tensor_sub(st6_sb[:], st2_sb[:, k % 2:k % 2 + 1], st6_sb[:])
                        ve.tensor_scalar(st6_sb[:], st6_sb[:], LN_EPS, None, OP.add, inc=(sVST, 1, k + 1))
                        ve.wait_ge(sSQT, k + 1)
                        ve.reciprocal(st3_sb[:], st5_sb[:])
                        ve.tensor_mul(st2_sb[:, k % 2:k % 2 + 1], st6_sb[:], st3_sb[:])
                        ve.tensor_mul(st2_sb[:, k % 2:k % 2 + 1], st2_sb[:, k % 2:k % 2 + 1], st3_sb[:])
                        ve.tensor_scalar(st2_sb[:, k % 2:k % 2 + 1], st2_sb[:, k % 2:k % 2 + 1], -0.5, 1.5, OP.mult, OP.add)
                        ve.tensor_mul(st3_sb[:], st3_sb[:], st2_sb[:, k % 2:k % 2 + 1])
                        ve.tensor_mul(st4_sb[:], st1_sb[:, k % 2:k % 2 + 1], st3_sb[:])
                        ve.tensor_scalar(st4_sb[:], st4_sb[:], -1.0, None, OP.mult)
                        ve.tensor_scalar(y_sb[:], z_sb[:, (k % 2) * D:(k % 2) * D + D], st3_sb[:], st4_sb[:],
                                         OP.mult, OP.add)
                        ve.tensor_reduce(st5_sb[:], y_sb[:], AX, OP.max,
                                         apply_absolute_value=True)
                        ve.tensor_scalar(scale_all[:, k:k + 1], st5_sb[:], Q_EPS,
                                         INV127, OP.max, OP.mult)
                        ve.reciprocal(st6_sb[:], scale_all[:, k:k + 1])
                        ve.tensor_scalar(t_sb[:], y_sb[:], st6_sb[:], MAGIC, OP.mult, OP.add)
                        ve.tensor_scalar(q8_sb[:], t_sb[:], MAGIC, None, OP.subtract)
                        if k >= 1:
                            ve.wait_ge(sTR, 2 * k)
                        ve.tensor_copy(qb_sb[:], q8_sb[:], inc=(sQ, 1, k + 1))
                        if k >= 2:
                            ve.wait_ge(sYDMA, 16 * (k - 1))
                        ve.tensor_copy(yf16_sb[:, (k % 2) * D:(k % 2) * D + D], y_sb[:], inc=(sYD, 1, k + 1))

                    for tt in range(NT):
                        ve.wait_ge(sXI, 16 * (tt + 1))
                        if tt >= 2:
                            ve.wait_ge(sST, 2 * (tt - 1))
                        for half in range(2):
                            ve.wait_ge(sAR, 2 * tt + half + 1)
                            ve.tensor_add(
                                z_sb[:, (tt % 2) * D + half * 1024:(tt % 2) * D + half * 1024 + 1024],
                                path[half][:],
                                xin_sb[:, (tt % 2) * D + half * 1024:(tt % 2) * D + half * 1024 + 1024],
                                inc=(sZD, 1, 2 * tt + half + 1))
                        if tt >= 1:
                            post(tt - 1)
                    post(NT - 1)
                    vchain[0] = ve.last
                    vchain[1] = ve._n

            with ExitStack() as phd:
                Ed = phd.enter_context
                wsb_sb = Ed(nc.sbuf_tensor("wsb_sb", [128, D], F32))
                bb_sb = Ed(nc.sbuf_tensor("bb_sb", [128, D], F32))
                wf_sb = Ed(nc.sbuf_tensor("wfsb", [128, NC_ * D], BF16))
                yin_sb = Ed(nc.sbuf_tensor("yin", [128, 2 * D], F16))
                f_sb = Ed(nc.sbuf_tensor("ff", [128, D], F32))
                f16_sb = Ed(nc.sbuf_tensor("f16", [128, D], F16))
                z2_sb = Ed(nc.sbuf_tensor("z2", [128, 2 * D], F32))
                o_sb = Ed(nc.sbuf_tensor("oo", [128, 2 * D], F32))
                su1_sb = Ed(nc.sbuf_tensor("su1", [128, 2], F32))
                su2_sb = Ed(nc.sbuf_tensor("su2", [128, 2], F32))
                su3_sb = Ed(nc.sbuf_tensor("su3", [128, 1], F32))
                su4_sb = Ed(nc.sbuf_tensor("su4", [128, 1], F32))
                su5_sb = Ed(nc.sbuf_tensor("su5", [128, 1], F32))
                su6_sb = Ed(nc.sbuf_tensor("su6", [128, 1], F32))
                pffnA = Ed(nc.psum_tensor("pffnA", [128, 2048], F32))
                pffnB = Ed(nc.psum_tensor("pffnB", [128, 2048], F32))
                blk3 = blkM
                pffn = [pffnA, pffnB]

                @blk3.sync
                def _(sy):
                    sy.wait_ge(sAR, 2 * NT)
                    sy.wait_ge(sZD, 2 * NT)
                    sy.wait_ge(sTRE, 2 * NT)
                    sy.wait_ge(sCS, 32 * NC_)
                    wfsem = [sWF, sWC, sYR, sAFF]
                    for c in range(NC_):
                        sy.wait_ge(wfsem[c % 4], 16 * (c // 4))
                        sy.dma_start(
                            out=wf_sb[:, c * D:(c + 1) * D], in_=wfb_d[c]
                        ).then_inc(wfsem[c % 4], 16)
                    sy.wait_ge(sYDMA, 16 * NT)

                    def owrite(tt):
                        sy.wait_ge(sOUTV, tt + 1)
                        sy.wait_ge(sOUT, 16 * tt)
                        sy.dma_start(
                            out=out_d[tt * 128:(tt + 1) * 128, :], in_=o_sb[:, (tt % 2) * D:(tt % 2) * D + D]
                        ).then_inc(sOUT, 16)

                    for tt in range(NT):
                        if tt >= 2:
                            sy.wait_ge(sZ2, tt - 1)
                        sy.wait_ge(sYI, 16 * tt)
                        sy.dma_start(out=yin_sb[:, (tt % 2) * D:(tt % 2) * D + D], in_=ysc_d[tt]).then_inc(sYI, 16)
                        if tt >= 2:
                            owrite(tt - 2)
                    owrite(NT - 2)
                    owrite(NT - 1)
                    sy.wait_ge(sOUT, 16 * NT)

                @blk3.scalar
                def _(sc):
                    sc.wait_ge(sAR, 2 * NT)
                    sc.wait_ge(sAUX, 16)
                    sc.dma_start(out=wsb_sb[:], in_=wsb_d).then_inc(sAUX, 16)
                    sc.wait_ge(sAUX, 32)
                    sc.dma_start(out=bb_sb[:], in_=bb_d).then_inc(sAUX, 16)
                    for tt in range(NT):
                        if tt >= 1:
                            sc.wait_ge(sVS2, tt)
                            sc.activation(su5_sb[:], su6_sb[:], AF.Sqrt).then_inc(sSQ2, 1)
                        sc.wait_ge(sZ2, tt + 1)
                        if tt >= 2:
                            sc.wait_ge(sVS2, tt - 1)
                        if tt >= 1:
                            sc.wait_ge(sS2T, 2 * tt)
                        sc.activation(f_sb[:], z2_sb[:, (tt % 2) * D:(tt % 2) * D + D],
                                      AF.Copy, accum_out=su1_sb[:, tt % 2:tt % 2 + 1]).then_inc(sS2T, 1)
                        sc.wait_ge(sS2T, 2 * tt + 1)
                        sc.activation(f_sb[:], z2_sb[:, (tt % 2) * D:(tt % 2) * D + D],
                                      AF.Square, accum_out=su2_sb[:, tt % 2:tt % 2 + 1]).then_inc(sS2T, 1)
                    sc.wait_ge(sVS2, NT)
                    sc.activation(su5_sb[:], su6_sb[:], AF.Sqrt).then_inc(sSQ2, 1)

                @blk3.tensor
                def _(te):
                    te.wait_ge(sTRE, 2 * NT)
                    te.wait_ge(sZD, 2 * NT)
                    for tt in range(NT):
                        if tt >= 2:
                            te.wait_ge(sPF, tt - 1)
                        for c in range(NC_):
                            if tt == 0:
                                te.wait_ge((sWF, sWC, sYR, sAFF)[c % 4], 16 * (c // 4 + 1))
                            for nt in range(4):
                                mm = te.matmul(
                                    pffn[tt % 2][:, nt * 512:(nt + 1) * 512],
                                    qT_sb[:, c * S + tt * 128: c * S + tt * 128 + 128],
                                    wf_sb[:, c * D + nt * 512: c * D + nt * 512 + 512],
                                    start=(c == 0), stop=(c == NC_ - 1),
                                )
                        mm.then_inc(sAR, 1)

                @blk3.vector
                def _(ve):
                    ve = ChainedEng(ve, sVC, vchain[0])
                    ve._n = vchain[1]
                    ve.wait_ge(sAUX, 48)

                    def post2(k):
                        ve.wait_ge(sS2T, 2 * (k + 1))
                        ve.tensor_scalar(su1_sb[:, k % 2:k % 2 + 1], su1_sb[:, k % 2:k % 2 + 1], INV_D, None, OP.mult)
                        ve.tensor_scalar(su2_sb[:, k % 2:k % 2 + 1], su2_sb[:, k % 2:k % 2 + 1], INV_D, None, OP.mult)
                        ve.tensor_mul(su6_sb[:], su1_sb[:, k % 2:k % 2 + 1], su1_sb[:, k % 2:k % 2 + 1])
                        ve.tensor_sub(su6_sb[:], su2_sb[:, k % 2:k % 2 + 1], su6_sb[:])
                        ve.tensor_scalar(su6_sb[:], su6_sb[:], LN_EPS, None, OP.add, inc=(sVS2, 1, k + 1))
                        ve.wait_ge(sSQ2, k + 1)
                        ve.reciprocal(su3_sb[:], su5_sb[:])
                        ve.tensor_mul(su2_sb[:, k % 2:k % 2 + 1], su6_sb[:], su3_sb[:])
                        ve.tensor_mul(su2_sb[:, k % 2:k % 2 + 1], su2_sb[:, k % 2:k % 2 + 1], su3_sb[:])
                        ve.tensor_scalar(su2_sb[:, k % 2:k % 2 + 1], su2_sb[:, k % 2:k % 2 + 1], -0.5, 1.5, OP.mult, OP.add)
                        ve.tensor_mul(su3_sb[:], su3_sb[:], su2_sb[:, k % 2:k % 2 + 1])
                        ve.tensor_mul(su4_sb[:], su1_sb[:, k % 2:k % 2 + 1], su3_sb[:])
                        ve.tensor_scalar(su4_sb[:], su4_sb[:], -1.0, None, OP.mult)
                        if k >= 2:
                            ve.wait_ge(sOUT, 16 * (k - 1))
                        ve.tensor_scalar(o_sb[:, (k % 2) * D:(k % 2) * D + D], z2_sb[:, (k % 2) * D:(k % 2) * D + D],
                                         su3_sb[:], su4_sb[:], OP.mult, OP.add,
                                         inc=(sOUTV, 1, k + 1))

                    for tt in range(NT):
                        ve.wait_ge(sAR, 2 * NT + tt + 1)
                        ve.wait_ge(sYI, 16 * (tt + 1))
                        if tt >= 1:
                            ve.wait_ge(sS2T, 2 * tt)
                        if tt >= 2:
                            ve.wait_ge(sOUTV, tt - 1)
                        ve.scalar_tensor_tensor(
                            f_sb[:], pffn[tt % 2][:], scale_all[:, tt:tt + 1], wsb_sb[:],
                            OP.mult, OP.mult, inc=(sPF, 1, tt + 1),
                        )
                        ve.tensor_add(f16_sb[:], f_sb[:], bb_sb[:])
                        ve.tensor_add(z2_sb[:, (tt % 2) * D:(tt % 2) * D + D], f16_sb[:], yin_sb[:, (tt % 2) * D:(tt % 2) * D + D], inc=(sZ2, 1, tt + 1))
                        if tt >= 1:
                            post2(tt - 1)
                    post2(NT - 1)

    return nc
    return nc


_NC_CACHE = None


def kernel(L0_attn_x, L0_attn_q1_w, L0_attn_k1_w, L0_attn_q2_w, L0_attn_k2_w,
           L0_attn_v_w, L0_attn_o_w, L0_attn_lambda_param,
           L0_ffn_weight, L0_ffn_w_scale, L0_ffn_bias):
    global _NC_CACHE
    B = L0_attn_x.shape[0]

    def head_major(w):
        wt = np.ascontiguousarray(np.asarray(w).T)
        return wt.reshape(16, 128, 16, 128).transpose(2, 1, 0, 3)

    parts = [head_major(w) for w in (L0_attn_v_w, L0_attn_q1_w, L0_attn_k1_w,
                                     L0_attn_q2_w, L0_attn_k2_w)]
    wqkv = np.ascontiguousarray(np.stack(parts, axis=2))
    wqkv = wqkv.reshape(16, 128, 5 * 16 * 128)

    wo = np.ascontiguousarray(np.asarray(L0_attn_o_w).T)
    wffn = np.ascontiguousarray(np.asarray(L0_ffn_weight).T)
    wsb = np.ascontiguousarray(np.broadcast_to(
        np.asarray(L0_ffn_w_scale)[:, 0][None, :], (128, 2048)))
    bb = np.ascontiguousarray(np.broadcast_to(
        np.asarray(L0_ffn_bias)[None, :], (128, 2048)))
    lam = np.ascontiguousarray(np.broadcast_to(
        np.asarray(L0_attn_lambda_param)[None, :], (128, 16)))

    if _NC_CACHE is None:
        _NC_CACHE = build_program()
    nc = _NC_CACHE

    in_maps = []
    for b in range(B):
        xb = np.ascontiguousarray(np.asarray(L0_attn_x[b]))
        in_maps.append({
            "xT": np.ascontiguousarray(xb.T),
            "x": xb,
            "wqkv": wqkv, "wo": wo, "wffn": wffn,
            "wsb": wsb, "bb": bb, "lam": lam,
        })
    res = run_bass_kernel_spmd(nc, in_maps, core_ids=list(range(8)))
    out = np.stack([np.asarray(r["out"]) for r in res.results], axis=0)
    return out.astype(np.float32)


# version 25
# speedup vs baseline: 1.2054x; 1.2054x over previous
from contextlib import ExitStack

import numpy as np

import concourse.bass as bass
import concourse.mybir as mybir
from concourse.bass_utils import run_bass_kernel_spmd

F32 = mybir.dt.float32
BF16 = mybir.dt.bfloat16
F16 = mybir.dt.float16
I8 = mybir.dt.int8
AX = mybir.AxisListType.X
OP = mybir.AluOpType
AF = mybir.ActivationFunctionType

D = 2048
S = 2048
H = 16
NC_ = 16
NT = 16
QT = 4
INV_SQRT_HD = 1.0 / np.sqrt(128.0)
MAGIC = 12582912.0
LN_EPS = 1e-5
Q_EPS = 1e-7
INV_D = 1.0 / 2048.0
INV127 = 1.0 / 127.0


class ChainedEng:

    def __init__(self, eng, sem, last=None):
        self._e, self._s = eng, sem
        self.last = last

    def __getattr__(self, name):
        f = getattr(self._e, name)

        def g(*a, inc=None, **k):
            if self.last is not None:
                self._e.wait_ge(self.last[0], self.last[1])
            inst = f(*a, **k)
            if inc is None:
                if self.last is not None and self.last[0] is self._s:
                    nxt = self.last[1] + 1
                else:
                    nxt = self._n + 1
                inst.then_inc(self._s, 1)
                self._n = nxt
                self.last = (self._s, nxt)
            else:
                sem, kk, after = inc
                inst.then_inc(sem, kk)
                self.last = (sem, after)
            return inst

        return g

    @property
    def _n(self):
        return self.__dict__.setdefault("_nv", 0)

    @_n.setter
    def _n(self, v):
        self.__dict__["_nv"] = v

    def wait_ge(self, sem, v):
        return self._e.wait_ge(sem, v)


def build_program():
    nc = bass.Bass("TRN2", target_bir_lowering=False)

    xT_d = nc.dram_tensor("xT", [D, S], F32, kind="ExternalInput").ap()
    x_d = nc.dram_tensor("x", [S, D], F32, kind="ExternalInput").ap()
    wqkv_d = nc.dram_tensor("wqkv", [H, 128, 5 * 16 * 128], F32, kind="ExternalInput").ap()
    wo_d = nc.dram_tensor("wo", [D, D], F32, kind="ExternalInput").ap()
    wffn_d = nc.dram_tensor("wffn", [D, D], I8, kind="ExternalInput").ap()
    wsb_d = nc.dram_tensor("wsb", [128, D], F32, kind="ExternalInput").ap()
    bb_d = nc.dram_tensor("bb", [128, D], F32, kind="ExternalInput").ap()
    lam_d = nc.dram_tensor("lam", [128, H], F32, kind="ExternalInput").ap()
    out_d = nc.dram_tensor("out", [S, D], F32, kind="ExternalOutput").ap()
    osc_d = nc.dram_tensor("osc", [NT, H, 128, 128], BF16).ap()
    wob_d = nc.dram_tensor("wob", [NC_, 128, D], BF16).ap()
    wfb_d = nc.dram_tensor("wfb", [NC_, 128, D], BF16).ap()
    ysc_d = nc.dram_tensor("ysc", [NT, 128, D], F16).ap()

    with ExitStack() as top:
        E = top.enter_context
        ones_sb = E(nc.sbuf_tensor("ones", [128, 128], BF16))
        id_sb = E(nc.sbuf_tensor("ident", [128, 128], BF16))
        lam_sb = E(nc.sbuf_tensor("lam_sb", [128, H], F32))
        neglam_sb = E(nc.sbuf_tensor("neglam", [128, H], F32))
        scale_all = E(nc.sbuf_tensor("scale_all", [128, NT], F32))
        (sWQ, sXT, sPM, sPD, sS, sEXP, sPVD, sCB, sODMA, sAUX,
         sOI, sXI, sWO, sAR, sZD, sQ, sTR, sTRE, sYD, sYDMA,
         sVST, sSQT, sWF, sWC, sYI, sZ2, sPF, sOUTV, sOUT, sVS2, sSQ2, sID, sVC, sOC,
         sST, sYR, sAFF, sS2T, sAF2, sWO2, sWO3, sWO4, sTQ2, sTQ3, sOC2, sCV, sCS, sXT2, sXT3, sXT4, sWFC, sODM2) = [
            E(nc.semaphore(n)) for n in (
                "sWQ", "sXT", "sPM", "sPD", "sS", "sEXP", "sPVD", "sCB",
                "sODMA", "sAUX", "sOI", "sXI", "sWO", "sAR", "sZD", "sQ",
                "sTR", "sTRE", "sYD", "sYDMA", "sVST", "sSQT", "sWF", "sWC",
                "sYI", "sZ2", "sPF", "sOUTV", "sOUT", "sVS2", "sSQ2", "sID", "sVC", "sOC",
                "sST", "sYR", "sAFF", "sS2T", "sAF2", "sWO2", "sWO3", "sWO4", "sTQ2", "sTQ3", "sOC2", "sCV", "sCS", "sXT2", "sXT3", "sXT4", "sWFC", "sODM2")]
        with nc.Block() as blk0:
            @blk0.gpsimd
            def _(g):
                g.memset(ones_sb[:], 1.0)
                g.memset(id_sb[:], 0.0).then_inc(sID, 1)
                g.wait_ge(sID, 1)
                g.affine_select(
                    out=id_sb[:], in_=id_sb[:],
                    compare_op=OP.not_equal, fill=1.0,
                    base=0, pattern=[[-1, 128]], channel_multiplier=1,
                )

        blkM = E(nc.Block())
        with ExitStack() as phb:
            Eb = phb.enter_context
            xT_sb = Eb(nc.sbuf_tensor("xTb", [128, NC_ * S], BF16))
            wq_sb = Eb(nc.sbuf_tensor("wq", [128, 2 * 5 * 16 * 128], BF16))
            q1_sb = Eb(nc.sbuf_tensor("q1", [128, S], BF16))
            k1_sb = Eb(nc.sbuf_tensor("k1", [128, S], BF16))
            q2_sb = Eb(nc.sbuf_tensor("q2", [128, S], BF16))
            k2_sb = Eb(nc.sbuf_tensor("k2", [128, S], BF16))
            v_sb = Eb(nc.sbuf_tensor("vv", [128, S], BF16))
            v2_sb = Eb(nc.sbuf_tensor("v2", [128, S], BF16))
            pb_sb = Eb(nc.sbuf_tensor("pb", [128, 2048], BF16))
            oc_sb = Eb(nc.sbuf_tensor("oc", [128, 3072], F32))
            sc_sb = Eb(nc.sbuf_tensor("sc_", [128, 3072], F32))
            r1_sb = Eb(nc.sbuf_tensor("r1", [128, 512], F32))
            r2_sb = Eb(nc.sbuf_tensor("r2", [128, 512], F32))
            t1_sb = Eb(nc.sbuf_tensor("t1", [128, 512], F32))
            t2_sb = Eb(nc.sbuf_tensor("t2", [128, 512], F32))
            oout_sb = Eb(nc.sbuf_tensor("oout", [128, 1024], BF16))
            stg_sb = Eb(nc.sbuf_tensor("stg", [128, 2 * D], BF16))
            psA = Eb(nc.psum_tensor("psA", [128, 1024], F32))
            psB = Eb(nc.psum_tensor("psB", [128, 1024], F32))
            pO = Eb(nc.psum_tensor("pO", [128, 1024], F32))
            pS = Eb(nc.psum_tensor("pS", [128, 1024], F32))
            blk = blkM
            sbank = [psA, psB]
            projb = [psA, psB]

            sS_after = {}
            _cnt = 0
            for _grp in range(H * QT):
                for _c in range(NC_):
                    _cnt += 2
                    sS_after[(_grp, _c)] = _cnt

            @blk.gpsimd
            def _(g):
                xtsem = [sXT, sXT2, sXT3, sXT4]
                xtv = xT_sb[:].rearrange("p (c t) -> p c t", c=NC_)
                xdv = xT_d.rearrange("(c p) t -> p c t", p=128)
                g.dma_start(out=wq_sb[:, 0:4096], in_=wqkv_d[0, :, 0:4096]).then_inc(sWQ, 16)
                g.dma_start(out=xtv[:, 0:4, :], in_=xdv[:, 0:4, :]).then_inc(sXT, 16)
                g.dma_start(out=xtv[:, 4:8, :], in_=xdv[:, 4:8, :]).then_inc(sXT2, 16)
                g.wait_ge(sWQ, 16)
                g.dma_start(out=wq_sb[:, 4096:10240], in_=wqkv_d[0, :, 4096:10240]).then_inc(sWQ, 16)
                g.dma_start(out=xtv[:, 8:12, :], in_=xdv[:, 8:12, :]).then_inc(sXT3, 16)
                g.dma_start(out=xtv[:, 12:16, :], in_=xdv[:, 12:16, :]).then_inc(sXT4, 16)
                for h in range(1, H):
                    if h >= 2:
                        g.wait_ge(sPM, (h - 1) * 32)
                    g.wait_ge(sWQ, 16 * (h + 1))
                    g.dma_start(out=wq_sb[:, (h % 2) * 10240:(h % 2) * 10240 + 10240],
                                in_=wqkv_d[h]).then_inc(sWQ, 16)
                    if h == 1:
                        for i in range(2 * NC_):
                            src_ap = (wo_d[(i % NC_) * 128:(i % NC_) * 128 + 128, :]
                                      if i < NC_ else
                                      wffn_d[(i % NC_) * 128:(i % NC_) * 128 + 128, :])
                            g.wait_ge(sCV, 16 * i)
                            if i >= 2:
                                g.wait_ge(sCS, 16 * (i - 1))
                            g.dma_start(
                                out=stg_sb[:, (i % 2) * D:(i % 2) * D + D], in_=src_ap
                            ).then_inc(sCV, 16)

            @blk.sync
            def _(sy):
                sy.dma_start(out=lam_sb[:], in_=lam_d).then_inc(sAUX, 16)
                for i in range(2 * NC_):
                    dst = wob_d[i % NC_] if i < NC_ else wfb_d[i % NC_]
                    sy.wait_ge(sCV, 16 * (i + 1))
                    sy.wait_ge(sCS, 16 * i)
                    sy.dma_start(out=dst, in_=stg_sb[:, (i % 2) * D:(i % 2) * D + D]).then_inc(sCS, 16)
                for grp in range(61):
                    sy.wait_ge(sCB, grp + 1)
                    h, qt = grp // QT, grp % QT
                    sy.wait_ge(sODMA, 16 * grp)
                    sy.dma_start(
                        out=osc_d[qt * 4:qt * 4 + 4, h].rearrange("j p t -> p j t"),
                        in_=oout_sb[:, (grp % 2) * 512:(grp % 2) * 512 + 512].rearrange(
                            "p (j t) -> p j t", j=4),
                    ).then_inc(sODMA, 16)

            @blk.tensor
            def _(te):
                for _s in (sXT, sXT2, sXT3, sXT4):
                    te.wait_ge(_s, 16)
                for h in range(H):
                    te.wait_ge(sWQ, 16 * (h + 2))
                    if h > 0:
                        te.wait_ge(sEXP, 64 * h)

                    def wslice(m, c):
                        base = (h % 2) * 10240 + (m * 16 + c) * 128
                        return wq_sb[:, base:base + 128]

                    def pdval(g):
                        return g + 1 if g < 16 else g + 2

                    g = 0
                    for tt in range(NT):
                        if h > 0 or g >= 2:
                            te.wait_ge(sPD, h * 33 + pdval(g - 2))
                        for c in range(NC_):
                            mm = te.matmul(
                                projb[g % 2][:, 0:128],
                                xT_sb[:, c * S + tt * 128: c * S + tt * 128 + 128],
                                wslice(0, c),
                                start=(c == 0), stop=(c == NC_ - 1),
                            )
                        mm.then_inc(sPM, 1)
                        g += 1
                    for m, _dst in enumerate((q1_sb, k1_sb, q2_sb, k2_sb)):
                        for qt in range(QT):
                            te.wait_ge(sPD, h * 33 + pdval(g - 2))
                            for c in range(NC_):
                                mm = te.matmul(
                                    projb[g % 2][:, 0:512],
                                    wslice(m + 1, c),
                                    xT_sb[:, c * S + qt * 512: c * S + qt * 512 + 512],
                                    start=(c == 0), stop=(c == NC_ - 1),
                                )
                            mm.then_inc(sPM, 1)
                            g += 1

                    for qt in range(QT):
                        grp = h * QT + qt

                        def s_issue(qt_, c):
                            s = (h * QT + qt_) * NC_ + c
                            if qt_ == 0 and c < 2:
                                te.wait_ge(sPD, h * 33 + 32 + c)
                            if s >= 2:
                                te.wait_ge(sEXP, s - 1)
                            for j, (kk, qq) in enumerate(((k1_sb, q1_sb), (k2_sb, q2_sb))):
                                te.matmul(
                                    sbank[c % 2][:, j * 512:(j + 1) * 512],
                                    kk[:, c * 128:(c + 1) * 128],
                                    qq[:, qt_ * 512:(qt_ + 1) * 512],
                                    start=True, stop=True,
                                ).then_inc(sS, 1)

                        def pv_issue(cc):
                            te.wait_ge(sEXP, grp * NC_ + cc + 1)
                            if cc == 0 and grp > 0:
                                te.wait_ge(sOC, grp)
                                te.wait_ge(sOC2, grp)
                            for j, vv in enumerate((v_sb, v2_sb)):
                                pslc = pb_sb[:, (cc % 2) * 1024 + j * 512:(cc % 2) * 1024 + (j + 1) * 512]
                                te.matmul(pO[:, j * 512:(j + 1) * 512], vv[:, cc * 128:(cc + 1) * 128],
                                          pslc, start=(cc == 0), stop=(cc == NC_ - 1))
                            for j in range(2):
                                pslc = pb_sb[:, (cc % 2) * 1024 + j * 512:(cc % 2) * 1024 + (j + 1) * 512]
                                mm = te.matmul(pS[:, j * 512:(j + 1) * 512], ones_sb[:], pslc,
                                               start=(cc == 0), stop=(cc == NC_ - 1))
                            mm.then_inc(sPVD, 1)

                        for c in range(NC_):
                            s_issue(qt, c)
                            if c >= 1:
                                pv_issue(c - 1)
                        pv_issue(NC_ - 1)

            @blk.scalar
            def _(sc):
                for grp in range(H * QT):
                    for c in range(NC_):
                        s = grp * NC_ + c
                        sc.wait_ge(sS, sS_after[(grp, c)])
                        if s >= 2:
                            sc.wait_ge(sPVD, s - 1)
                        sc.activation(
                            pb_sb[:, (c % 2) * 1024:(c % 2) * 1024 + 1024],
                            sbank[c % 2][:], AF.Exp,
                            scale=INV_SQRT_HD,
                        ).then_inc(sEXP, 1)
                    sc.wait_ge(sPVD, NC_ * (grp + 1))
                    if grp >= 3:
                        sc.wait_ge(sCB, grp - 2)
                    sc.copy(oc_sb[:, (grp % 3) * 1024:(grp % 3) * 1024 + 1024],
                            pO[:]).then_inc(sOC2, 1)


            vchain = [None, 0]

            @blk.vector
            def _(ve):
                ve = ChainedEng(ve, sVC, vchain[0])
                ve._n = vchain[1]

                def divide(g2):
                    b2 = (g2 % 3) * 1024
                    ve.wait_ge(sOC2, g2 + 1)
                    if g2 >= 2:
                        ve.wait_ge(sODMA, 16 * min(g2 - 1, 61))
                    if g2 >= 63:
                        ve.wait_ge(sODM2, 16 * (g2 - 62))
                    ve.reciprocal(r1_sb[:], sc_sb[:, b2:b2 + 512])
                    ve.reciprocal(r2_sb[:], sc_sb[:, b2 + 512:b2 + 1024])
                    ve.tensor_mul(t1_sb[:], oc_sb[:, b2:b2 + 512], r1_sb[:])
                    ve.tensor_mul(t2_sb[:], oc_sb[:, b2 + 512:b2 + 1024], r2_sb[:])
                    ve.tensor_add(oout_sb[:, (g2 % 2) * 512:(g2 % 2) * 512 + 512], t1_sb[:], t2_sb[:], inc=(sCB, 1, g2 + 1))

                ve.wait_ge(sAUX, 16)
                ve.tensor_scalar(neglam_sb[:], lam_sb[:], -1.0, None, OP.mult)
                for h in range(H):
                    g = 0
                    pd = 0
                    for tt in range(NT):
                        ve.wait_ge(sPM, h * 32 + g + 1)
                        pd += 1
                        ve.tensor_copy(
                            v_sb[:, tt * 128:(tt + 1) * 128], projb[g % 2][:, 0:128],
                            inc=(sPD, 1, h * 33 + pd),
                        )
                        g += 1
                    pd += 1
                    ve.tensor_scalar(
                        v2_sb[:], v_sb[:], neglam_sb[:, h:h + 1], None, OP.mult,
                        inc=(sPD, 1, h * 33 + pd),
                    )
                    for m, dst in enumerate((q1_sb, k1_sb, q2_sb, k2_sb)):
                        for qt in range(QT):
                            ve.wait_ge(sPM, h * 32 + g + 1)
                            pd += 1
                            ve.tensor_copy(
                                dst[:, qt * 512:(qt + 1) * 512], projb[g % 2][:, 0:512],
                                inc=(sPD, 1, h * 33 + pd),
                            )
                            g += 1
                    for qt in range(QT):
                        grp = h * QT + qt
                        if grp >= 2:
                            divide(grp - 2)
                        ve.wait_ge(sPVD, NC_ * (grp + 1))
                        if grp >= 3:
                            ve.wait_ge(sCB, grp - 2)
                        b = (grp % 3) * 1024
                        ve.tensor_copy(sc_sb[:, b:b + 1024], pS[:], inc=(sOC, 1, grp + 1))
                divide(H * QT - 2)
                divide(H * QT - 1)
                vchain[0] = ve.last
                vchain[1] = ve._n

        with nc.sbuf_tensor("qT", [128, NC_ * S], BF16, side="right") as qT_sb:
            with ExitStack() as phc:
                Ec = phc.enter_context
                wo_sb = Ec(nc.sbuf_tensor("woS", [128, NC_ * D], BF16))
                ocol_sb = Ec(nc.sbuf_tensor("ocol", [128, 2 * H * 128], BF16))
                xin_sb = Ec(nc.sbuf_tensor("xin", [128, 2 * D], F32))
                z_sb = Ec(nc.sbuf_tensor("zz", [128, 2 * D], F32))
                y_sb = Ec(nc.sbuf_tensor("yy", [128, D], F32))
                t_sb = Ec(nc.sbuf_tensor("tt_", [128, D], F32))
                dump_sb = Ec(nc.sbuf_tensor("dump", [128, D], F32))
                q8_sb = Ec(nc.sbuf_tensor("qi8", [128, D], I8))
                qb_sb = Ec(nc.sbuf_tensor("qbf", [128, D], BF16))
                yf16_sb = Ec(nc.sbuf_tensor("yf16", [128, 2 * D], F16))
                st1_sb = Ec(nc.sbuf_tensor("st1", [128, 2], F32))
                st2_sb = Ec(nc.sbuf_tensor("st2", [128, 2], F32))
                st3_sb = Ec(nc.sbuf_tensor("st3", [128, 1], F32))
                st4_sb = Ec(nc.sbuf_tensor("st4", [128, 1], F32))
                st5_sb = Ec(nc.sbuf_tensor("st5", [128, 1], F32))
                st6_sb = Ec(nc.sbuf_tensor("st6", [128, 1], F32))
                pattnL = Ec(nc.psum_tensor("pattnL", [128, 1024], F32))
                pattnH = Ec(nc.psum_tensor("pattnH", [128, 1024], F32))
                ptrA = Ec(nc.psum_tensor("ptrA", [128, 1024], BF16))
                ptrB = Ec(nc.psum_tensor("ptrB", [128, 1024], BF16))
                blk2 = blkM
                path = [pattnL, pattnH]
                trb = [ptrA, ptrB]
                wosem = [sWO, sWO2, sWO3, sWO4]

                @blk2.sync
                def _(sy):
                    sy.wait_ge(sCS, 32 * NC_)
                    sy.wait_ge(sWQ, 16 * H)
                    sy.wait_ge(sPM, 32 * H)

                    def ywrite(tt):
                        sy.wait_ge(sYD, tt + 1)
                        sy.wait_ge(sYDMA, 16 * tt)
                        sy.dma_start(out=ysc_d[tt], in_=yf16_sb[:, (tt % 2) * D:(tt % 2) * D + D]).then_inc(sYDMA, 16)

                    for tt in range(NT):
                        if tt >= 2:
                            sy.wait_ge(sAR, 2 * (tt - 1))
                        sy.wait_ge(sODMA, 16 * 61)
                        if tt >= 4:
                            sy.wait_ge(sODM2, 16 * (tt // 4))
                        sy.wait_ge(sOI, 16 * tt)
                        sy.dma_start(
                            out=ocol_sb[:, (tt % 2) * 2048:(tt % 2) * 2048 + 2048].rearrange("p (h t) -> p h t", h=H),
                            in_=osc_d[tt].rearrange("h p t -> p h t"),
                        ).then_inc(sOI, 16)
                        if tt >= 2:
                            sy.wait_ge(sZD, 2 * (tt - 1))
                        sy.wait_ge(sXI, 16 * tt)
                        sy.dma_start(
                            out=xin_sb[:, (tt % 2) * D:(tt % 2) * D + D], in_=x_d[tt * 128:(tt + 1) * 128, :]
                        ).then_inc(sXI, 16)
                        if tt >= 2:
                            ywrite(tt - 2)
                    ywrite(NT - 2)
                    ywrite(NT - 1)

                @blk2.gpsimd
                def _(g):
                    g.wait_ge(sPM, 32 * H)
                    g.wait_ge(sCS, 16 * NC_)
                    for c in range(NC_):
                        sem = wosem[c % 4]
                        g.wait_ge(sem, 16 * (c // 4))
                        g.dma_start(
                            out=wo_sb[:, c * D:(c + 1) * D],
                            in_=wob_d[c],
                        ).then_inc(sem, 16)
                    for grp in range(61, H * QT):
                        g.wait_ge(sCB, grp + 1)
                        h_, qt_ = grp // QT, grp % QT
                        if grp > 61:
                            g.wait_ge(sODM2, 16 * (grp - 61))
                        g.dma_start(
                            out=osc_d[qt_ * 4:qt_ * 4 + 4, h_].rearrange("j p t -> p j t"),
                            in_=oout_sb[:, (grp % 2) * 512:(grp % 2) * 512 + 512].rearrange(
                                "p (j t) -> p j t", j=4),
                        ).then_inc(sODM2, 16)

                @blk2.tensor
                def _(te):
                    te.wait_ge(sEXP, 16 * H * QT - 1)

                    def transposes(tt):
                        if tt == 0:
                            te.wait_ge(sOC2, H * QT)
                        te.wait_ge(sQ, tt + 1)
                        for grp8 in range(2):
                            gg = 2 * tt + grp8
                            if gg >= 2:
                                te.wait_ge(sTRE, gg - 1)
                            for j8 in range(8):
                                j = grp8 * 8 + j8
                                mm = te.transpose(
                                    trb[gg % 2][:, j8 * 128:(j8 + 1) * 128],
                                    qb_sb[:, j * 128:(j + 1) * 128],
                                    id_sb[:],
                                )
                            mm.then_inc(sTR, 1)

                    for tt in range(NT):
                        for half in range(2):
                            if tt >= 1:
                                te.wait_ge(sZD, 2 * tt + half - 1)
                            if tt == 0 and half == 1:
                                te.wait_ge(sEXP, 16 * H * QT)
                            for c in range(NC_):
                                if half == 0 and c == 0:
                                    te.wait_ge(sOI, 16 * (tt + 1))
                                if tt == 0 and half == 0:
                                    te.wait_ge(wosem[c % 4], 16 * (c // 4 + 1))
                                for nt in (half * 2, half * 2 + 1):
                                    mm = te.matmul(
                                        path[half][:, (nt - half * 2) * 512:(nt - half * 2) * 512 + 512],
                                        ocol_sb[:, (tt % 2) * 2048 + c * 128:(tt % 2) * 2048 + (c + 1) * 128],
                                        wo_sb[:, c * D + nt * 512: c * D + nt * 512 + 512],
                                        start=(c == 0), stop=(c == NC_ - 1),
                                    )
                                if tt == NT - 1 and half == 1 and c < NC_ - 1:
                                    mm.then_inc(sWFC, 1)
                            mm.then_inc(sAR, 1)
                        if tt >= 2:
                            transposes(tt - 2)
                    transposes(NT - 2)
                    transposes(NT - 1)

                @blk2.scalar
                def _(sc):
                    sc.wait_ge(sCS, 32 * NC_)
                    sc.wait_ge(sODMA, 16 * 61)
                    sc.wait_ge(sODM2, 48)

                    def evacs(tt):
                        for grp8 in range(2):
                            gg = 2 * tt + grp8
                            sc.wait_ge(sTR, gg + 1)
                            out_ap = qT_sb[:].rearrange("p (j r) -> p j r", j=NC_)[
                                :, grp8 * 8:(grp8 + 1) * 8, tt * 128:(tt + 1) * 128]
                            sc.copy(
                                out_ap,
                                trb[gg % 2][:].rearrange("p (j r) -> p j r", j=8),
                            ).then_inc(sTRE, 1)

                    for tt in range(NT):
                        if tt >= 1:
                            sc.wait_ge(sVST, tt)
                            sc.activation(st5_sb[:], st6_sb[:], AF.Sqrt).then_inc(sSQT, 1)
                        sc.wait_ge(sZD, 2 * (tt + 1))
                        if tt >= 2:
                            sc.wait_ge(sVST, tt - 1)
                        if tt >= 1:
                            sc.wait_ge(sST, 2 * tt)
                        sc.activation(dump_sb[:], z_sb[:, (tt % 2) * D:(tt % 2) * D + D],
                                      AF.Copy, accum_out=st1_sb[:, tt % 2:tt % 2 + 1]).then_inc(sST, 1)
                        sc.wait_ge(sST, 2 * tt + 1)
                        sc.activation(dump_sb[:], z_sb[:, (tt % 2) * D:(tt % 2) * D + D],
                                      AF.Square, accum_out=st2_sb[:, tt % 2:tt % 2 + 1]).then_inc(sST, 1)
                        if tt >= 2:
                            evacs(tt - 2)
                    sc.wait_ge(sVST, NT)
                    sc.activation(st5_sb[:], st6_sb[:], AF.Sqrt).then_inc(sSQT, 1)
                    evacs(NT - 2)
                    evacs(NT - 1)

                @blk2.vector
                def _(ve):
                    ve = ChainedEng(ve, sVC, vchain[0])
                    ve._n = vchain[1]
                    ve.wait_ge(sCS, 32 * NC_)

                    def post(k):
                        ve.wait_ge(sST, 2 * (k + 1))
                        ve.tensor_scalar(st1_sb[:, k % 2:k % 2 + 1], st1_sb[:, k % 2:k % 2 + 1], INV_D, None, OP.mult)
                        ve.tensor_scalar(st2_sb[:, k % 2:k % 2 + 1], st2_sb[:, k % 2:k % 2 + 1], INV_D, None, OP.mult)
                        ve.tensor_mul(st6_sb[:], st1_sb[:, k % 2:k % 2 + 1], st1_sb[:, k % 2:k % 2 + 1])
                        ve.tensor_sub(st6_sb[:], st2_sb[:, k % 2:k % 2 + 1], st6_sb[:])
                        ve.tensor_scalar(st6_sb[:], st6_sb[:], LN_EPS, None, OP.add, inc=(sVST, 1, k + 1))
                        ve.wait_ge(sSQT, k + 1)
                        ve.reciprocal(st3_sb[:], st5_sb[:])
                        ve.tensor_mul(st2_sb[:, k % 2:k % 2 + 1], st6_sb[:], st3_sb[:])
                        ve.tensor_mul(st2_sb[:, k % 2:k % 2 + 1], st2_sb[:, k % 2:k % 2 + 1], st3_sb[:])
                        ve.tensor_scalar(st2_sb[:, k % 2:k % 2 + 1], st2_sb[:, k % 2:k % 2 + 1], -0.5, 1.5, OP.mult, OP.add)
                        ve.tensor_mul(st3_sb[:], st3_sb[:], st2_sb[:, k % 2:k % 2 + 1])
                        ve.tensor_mul(st4_sb[:], st1_sb[:, k % 2:k % 2 + 1], st3_sb[:])
                        ve.tensor_scalar(st4_sb[:], st4_sb[:], -1.0, None, OP.mult)
                        ve.tensor_scalar(y_sb[:], z_sb[:, (k % 2) * D:(k % 2) * D + D], st3_sb[:], st4_sb[:],
                                         OP.mult, OP.add)
                        ve.tensor_reduce(st5_sb[:], y_sb[:], AX, OP.max,
                                         apply_absolute_value=True)
                        ve.tensor_scalar(scale_all[:, k:k + 1], st5_sb[:], Q_EPS,
                                         INV127, OP.max, OP.mult)
                        ve.reciprocal(st6_sb[:], scale_all[:, k:k + 1])
                        ve.tensor_scalar(t_sb[:], y_sb[:], st6_sb[:], MAGIC, OP.mult, OP.add)
                        ve.tensor_scalar(q8_sb[:], t_sb[:], MAGIC, None, OP.subtract)
                        if k >= 1:
                            ve.wait_ge(sTR, 2 * k)
                        ve.tensor_copy(qb_sb[:], q8_sb[:], inc=(sQ, 1, k + 1))
                        if k >= 2:
                            ve.wait_ge(sYDMA, 16 * (k - 1))
                        ve.tensor_copy(yf16_sb[:, (k % 2) * D:(k % 2) * D + D], y_sb[:], inc=(sYD, 1, k + 1))

                    for tt in range(NT):
                        ve.wait_ge(sXI, 16 * (tt + 1))
                        if tt >= 2:
                            ve.wait_ge(sST, 2 * (tt - 1))
                        for half in range(2):
                            ve.wait_ge(sAR, 2 * tt + half + 1)
                            ve.tensor_add(
                                z_sb[:, (tt % 2) * D + half * 1024:(tt % 2) * D + half * 1024 + 1024],
                                path[half][:],
                                xin_sb[:, (tt % 2) * D + half * 1024:(tt % 2) * D + half * 1024 + 1024],
                                inc=(sZD, 1, 2 * tt + half + 1))
                        if tt >= 1:
                            post(tt - 1)
                    post(NT - 1)
                    vchain[0] = ve.last
                    vchain[1] = ve._n

            with ExitStack() as phd:
                Ed = phd.enter_context
                wsb_sb = Ed(nc.sbuf_tensor("wsb_sb", [128, D], F32))
                bb_sb = Ed(nc.sbuf_tensor("bb_sb", [128, D], F32))
                wf_sb = Ed(nc.sbuf_tensor("wfsb", [128, NC_ * D], BF16))
                yin_sb = Ed(nc.sbuf_tensor("yin", [128, 2 * D], F16))
                f_sb = Ed(nc.sbuf_tensor("ff", [128, D], F32))
                f16_sb = Ed(nc.sbuf_tensor("f16", [128, D], F16))
                z2_sb = Ed(nc.sbuf_tensor("z2", [128, 2 * D], F32))
                o_sb = Ed(nc.sbuf_tensor("oo", [128, 2 * D], F32))
                su1_sb = Ed(nc.sbuf_tensor("su1", [128, 2], F32))
                su2_sb = Ed(nc.sbuf_tensor("su2", [128, 2], F32))
                su3_sb = Ed(nc.sbuf_tensor("su3", [128, 1], F32))
                su4_sb = Ed(nc.sbuf_tensor("su4", [128, 1], F32))
                su5_sb = Ed(nc.sbuf_tensor("su5", [128, 1], F32))
                su6_sb = Ed(nc.sbuf_tensor("su6", [128, 1], F32))
                pffnA = Ed(nc.psum_tensor("pffnA", [128, 2048], F32))
                pffnB = Ed(nc.psum_tensor("pffnB", [128, 2048], F32))
                blk3 = blkM
                pffn = [pffnA, pffnB]

                @blk3.gpsimd
                def _(g):
                    g.wait_ge(sWFC, 2)
                    g.dma_start(out=wsb_sb[:], in_=wsb_d).then_inc(sTQ2, 16)
                    g.wait_ge(sWFC, 4)
                    g.wait_ge(sTQ2, 16)
                    g.dma_start(out=bb_sb[:], in_=bb_d).then_inc(sTQ2, 16)
                    g.wait_ge(sCS, 32 * NC_)
                    wfsemd = [sWF, sWC, sYR, sAFF]
                    for j in range(4):
                        if j == 0:
                            g.wait_ge(sWFC, 8)
                        elif j == 1:
                            g.wait_ge(sWFC, 12)
                        else:
                            g.wait_ge(sAR, 2 * NT)
                            if j == 3:
                                g.wait_ge(sZD, 2 * NT - 2)
                        g.dma_start(
                            out=wf_sb[:, j * 4 * D:(j + 1) * 4 * D].rearrange(
                                "p (c d) -> p c d", c=4),
                            in_=wfb_d[j * 4:(j + 1) * 4].rearrange("c p d -> p c d"),
                        ).then_inc(wfsemd[j], 16)

                @blk3.sync
                def _(sy):
                    sy.wait_ge(sZD, 2 * NT)
                    sy.wait_ge(sYDMA, 16 * NT)

                    def owrite(tt):
                        sy.wait_ge(sOUTV, tt + 1)
                        sy.wait_ge(sOUT, 16 * tt)
                        sy.dma_start(
                            out=out_d[tt * 128:(tt + 1) * 128, :], in_=o_sb[:, (tt % 2) * D:(tt % 2) * D + D]
                        ).then_inc(sOUT, 16)

                    for tt in range(NT):
                        if tt >= 2:
                            sy.wait_ge(sZ2, tt - 1)
                        sy.wait_ge(sYI, 16 * tt)
                        sy.dma_start(out=yin_sb[:, (tt % 2) * D:(tt % 2) * D + D], in_=ysc_d[tt]).then_inc(sYI, 16)
                        if tt >= 2:
                            owrite(tt - 2)
                    owrite(NT - 2)
                    owrite(NT - 1)
                    sy.wait_ge(sOUT, 16 * NT)

                @blk3.scalar
                def _(sc):
                    for tt in range(NT):
                        if tt >= 1:
                            sc.wait_ge(sVS2, tt)
                            sc.activation(su5_sb[:], su6_sb[:], AF.Sqrt).then_inc(sSQ2, 1)
                        sc.wait_ge(sZ2, tt + 1)
                        if tt >= 2:
                            sc.wait_ge(sVS2, tt - 1)
                        if tt >= 1:
                            sc.wait_ge(sS2T, 2 * tt)
                        sc.activation(f_sb[:], z2_sb[:, (tt % 2) * D:(tt % 2) * D + D],
                                      AF.Copy, accum_out=su1_sb[:, tt % 2:tt % 2 + 1]).then_inc(sS2T, 1)
                        sc.wait_ge(sS2T, 2 * tt + 1)
                        sc.activation(f_sb[:], z2_sb[:, (tt % 2) * D:(tt % 2) * D + D],
                                      AF.Square, accum_out=su2_sb[:, tt % 2:tt % 2 + 1]).then_inc(sS2T, 1)
                    sc.wait_ge(sVS2, NT)
                    sc.activation(su5_sb[:], su6_sb[:], AF.Sqrt).then_inc(sSQ2, 1)

                @blk3.tensor
                def _(te):
                    te.wait_ge(sZD, 2 * NT)
                    for tt in range(NT):
                        if tt == 1:
                            te.wait_ge(sTRE, 2 * NT)
                        te.wait_ge(sTRE, 2 * tt + 2)
                        if tt >= 2:
                            te.wait_ge(sPF, tt - 1)
                        for c in range(NC_):
                            if tt == 0 and c % 4 == 0:
                                te.wait_ge((sWF, sWC, sYR, sAFF)[c // 4], 16)
                            for nt in range(4):
                                mm = te.matmul(
                                    pffn[tt % 2][:, nt * 512:(nt + 1) * 512],
                                    qT_sb[:, c * S + tt * 128: c * S + tt * 128 + 128],
                                    wf_sb[:, c * D + nt * 512: c * D + nt * 512 + 512],
                                    start=(c == 0), stop=(c == NC_ - 1),
                                )
                        mm.then_inc(sAR, 1)

                @blk3.vector
                def _(ve):
                    ve = ChainedEng(ve, sVC, vchain[0])
                    ve._n = vchain[1]
                    ve.wait_ge(sTQ2, 32)

                    def post2(k):
                        ve.wait_ge(sS2T, 2 * (k + 1))
                        ve.tensor_scalar(su1_sb[:, k % 2:k % 2 + 1], su1_sb[:, k % 2:k % 2 + 1], INV_D, None, OP.mult)
                        ve.tensor_scalar(su2_sb[:, k % 2:k % 2 + 1], su2_sb[:, k % 2:k % 2 + 1], INV_D, None, OP.mult)
                        ve.tensor_mul(su6_sb[:], su1_sb[:, k % 2:k % 2 + 1], su1_sb[:, k % 2:k % 2 + 1])
                        ve.tensor_sub(su6_sb[:], su2_sb[:, k % 2:k % 2 + 1], su6_sb[:])
                        ve.tensor_scalar(su6_sb[:], su6_sb[:], LN_EPS, None, OP.add, inc=(sVS2, 1, k + 1))
                        ve.wait_ge(sSQ2, k + 1)
                        ve.reciprocal(su3_sb[:], su5_sb[:])
                        ve.tensor_mul(su2_sb[:, k % 2:k % 2 + 1], su6_sb[:], su3_sb[:])
                        ve.tensor_mul(su2_sb[:, k % 2:k % 2 + 1], su2_sb[:, k % 2:k % 2 + 1], su3_sb[:])
                        ve.tensor_scalar(su2_sb[:, k % 2:k % 2 + 1], su2_sb[:, k % 2:k % 2 + 1], -0.5, 1.5, OP.mult, OP.add)
                        ve.tensor_mul(su3_sb[:], su3_sb[:], su2_sb[:, k % 2:k % 2 + 1])
                        ve.tensor_mul(su4_sb[:], su1_sb[:, k % 2:k % 2 + 1], su3_sb[:])
                        ve.tensor_scalar(su4_sb[:], su4_sb[:], -1.0, None, OP.mult)
                        if k >= 2:
                            ve.wait_ge(sOUT, 16 * (k - 1))
                        ve.tensor_scalar(o_sb[:, (k % 2) * D:(k % 2) * D + D], z2_sb[:, (k % 2) * D:(k % 2) * D + D],
                                         su3_sb[:], su4_sb[:], OP.mult, OP.add,
                                         inc=(sOUTV, 1, k + 1))

                    for tt in range(NT):
                        ve.wait_ge(sAR, 2 * NT + tt + 1)
                        ve.wait_ge(sYI, 16 * (tt + 1))
                        if tt >= 1:
                            ve.wait_ge(sS2T, 2 * tt)
                        if tt >= 2:
                            ve.wait_ge(sOUTV, tt - 1)
                        ve.scalar_tensor_tensor(
                            f_sb[:], pffn[tt % 2][:], scale_all[:, tt:tt + 1], wsb_sb[:],
                            OP.mult, OP.mult, inc=(sPF, 1, tt + 1),
                        )
                        ve.tensor_add(f16_sb[:], f_sb[:], bb_sb[:])
                        ve.tensor_add(z2_sb[:, (tt % 2) * D:(tt % 2) * D + D], f16_sb[:], yin_sb[:, (tt % 2) * D:(tt % 2) * D + D], inc=(sZ2, 1, tt + 1))
                        if tt >= 1:
                            post2(tt - 1)
                    post2(NT - 1)

    return nc
    return nc


_NC_CACHE = None


def kernel(L0_attn_x, L0_attn_q1_w, L0_attn_k1_w, L0_attn_q2_w, L0_attn_k2_w,
           L0_attn_v_w, L0_attn_o_w, L0_attn_lambda_param,
           L0_ffn_weight, L0_ffn_w_scale, L0_ffn_bias):
    global _NC_CACHE
    B = L0_attn_x.shape[0]

    def head_major(w):
        wt = np.ascontiguousarray(np.asarray(w).T)
        return wt.reshape(16, 128, 16, 128).transpose(2, 1, 0, 3)

    parts = [head_major(w) for w in (L0_attn_v_w, L0_attn_q1_w, L0_attn_k1_w,
                                     L0_attn_q2_w, L0_attn_k2_w)]
    wqkv = np.ascontiguousarray(np.stack(parts, axis=2))
    wqkv = wqkv.reshape(16, 128, 5 * 16 * 128)

    wo = np.ascontiguousarray(np.asarray(L0_attn_o_w).T)
    wffn = np.ascontiguousarray(np.asarray(L0_ffn_weight).T)
    wsb = np.ascontiguousarray(np.broadcast_to(
        np.asarray(L0_ffn_w_scale)[:, 0][None, :], (128, 2048)))
    bb = np.ascontiguousarray(np.broadcast_to(
        np.asarray(L0_ffn_bias)[None, :], (128, 2048)))
    lam = np.ascontiguousarray(np.broadcast_to(
        np.asarray(L0_attn_lambda_param)[None, :], (128, 16)))

    if _NC_CACHE is None:
        _NC_CACHE = build_program()
    nc = _NC_CACHE

    in_maps = []
    for b in range(B):
        xb = np.ascontiguousarray(np.asarray(L0_attn_x[b]))
        in_maps.append({
            "xT": np.ascontiguousarray(xb.T),
            "x": xb,
            "wqkv": wqkv, "wo": wo, "wffn": wffn,
            "wsb": wsb, "bb": bb, "lam": lam,
        })
    res = run_bass_kernel_spmd(nc, in_maps, core_ids=list(range(8)))
    out = np.stack([np.asarray(r["out"]) for r in res.results], axis=0)
    return out.astype(np.float32)


# version 27
# speedup vs baseline: 1.2266x; 1.0176x over previous
from contextlib import ExitStack

import numpy as np

import concourse.bass as bass
import concourse.mybir as mybir
from concourse.bass_utils import run_bass_kernel_spmd

F32 = mybir.dt.float32
BF16 = mybir.dt.bfloat16
F16 = mybir.dt.float16
I8 = mybir.dt.int8
AX = mybir.AxisListType.X
OP = mybir.AluOpType
AF = mybir.ActivationFunctionType

D = 2048
S = 2048
H = 16
NC_ = 16
NT = 16
QT = 4
INV_SQRT_HD = 1.0 / np.sqrt(128.0)
MAGIC = 12582912.0
LN_EPS = 1e-5
Q_EPS = 1e-7
INV_D = 1.0 / 2048.0
INV127 = 1.0 / 127.0


class ChainedEng:

    def __init__(self, eng, sem, last=None):
        self._e, self._s = eng, sem
        self.last = last

    def __getattr__(self, name):
        f = getattr(self._e, name)

        def g(*a, inc=None, **k):
            if self.last is not None:
                self._e.wait_ge(self.last[0], self.last[1])
            inst = f(*a, **k)
            if inc is None:
                if self.last is not None and self.last[0] is self._s:
                    nxt = self.last[1] + 1
                else:
                    nxt = self._n + 1
                inst.then_inc(self._s, 1)
                self._n = nxt
                self.last = (self._s, nxt)
            else:
                sem, kk, after = inc
                inst.then_inc(sem, kk)
                self.last = (sem, after)
            return inst

        return g

    @property
    def _n(self):
        return self.__dict__.setdefault("_nv", 0)

    @_n.setter
    def _n(self, v):
        self.__dict__["_nv"] = v

    def wait_ge(self, sem, v):
        return self._e.wait_ge(sem, v)


def build_program():
    nc = bass.Bass("TRN2", target_bir_lowering=False)

    xT_d = nc.dram_tensor("xT", [D, S], F32, kind="ExternalInput").ap()
    x_d = nc.dram_tensor("x", [S, D], F32, kind="ExternalInput").ap()
    wqkv_d = nc.dram_tensor("wqkv", [H, 128, 5 * 16 * 128], F32, kind="ExternalInput").ap()
    wo_d = nc.dram_tensor("wo", [D, D], F32, kind="ExternalInput").ap()
    wffn_d = nc.dram_tensor("wffn", [D, D], I8, kind="ExternalInput").ap()
    wsb_d = nc.dram_tensor("wsb", [128, D], F32, kind="ExternalInput").ap()
    bb_d = nc.dram_tensor("bb", [128, D], F32, kind="ExternalInput").ap()
    lam_d = nc.dram_tensor("lam", [128, H], F32, kind="ExternalInput").ap()
    out_d = nc.dram_tensor("out", [S, D], F32, kind="ExternalOutput").ap()
    osc_d = nc.dram_tensor("osc", [NT, H, 128, 128], BF16).ap()
    wob_d = nc.dram_tensor("wob", [NC_, 128, D], BF16).ap()
    wfb_d = nc.dram_tensor("wfb", [NC_, 128, D], BF16).ap()
    ysc_d = nc.dram_tensor("ysc", [NT, 128, D], F16).ap()

    with ExitStack() as top:
        E = top.enter_context
        ones_sb = E(nc.sbuf_tensor("ones", [128, 128], BF16))
        id_sb = E(nc.sbuf_tensor("ident", [128, 128], BF16))
        lam_sb = E(nc.sbuf_tensor("lam_sb", [128, H], F32))
        neglam_sb = E(nc.sbuf_tensor("neglam", [128, H], F32))
        scale_all = E(nc.sbuf_tensor("scale_all", [128, NT], F32))
        (sWQ, sXT, sPM, sPD, sS, sEXP, sPVD, sCB, sODMA, sAUX,
         sOI, sXI, sWO, sAR, sZD, sQ, sTR, sTRE, sYD, sYDMA,
         sVST, sSQT, sWF, sWC, sYI, sZ2, sPF, sOUTV, sOUT, sVS2, sSQ2, sID, sVC, sOC,
         sST, sYR, sAFF, sS2T, sAF2, sWO2, sWO3, sWO4, sTQ2, sTQ3, sOC2, sCV, sCS, sXT2, sXT3, sXT4, sWFC, sODM2, sPO0, sPO1, sPS0) = [
            E(nc.semaphore(n)) for n in (
                "sWQ", "sXT", "sPM", "sPD", "sS", "sEXP", "sPVD", "sCB",
                "sODMA", "sAUX", "sOI", "sXI", "sWO", "sAR", "sZD", "sQ",
                "sTR", "sTRE", "sYD", "sYDMA", "sVST", "sSQT", "sWF", "sWC",
                "sYI", "sZ2", "sPF", "sOUTV", "sOUT", "sVS2", "sSQ2", "sID", "sVC", "sOC",
                "sST", "sYR", "sAFF", "sS2T", "sAF2", "sWO2", "sWO3", "sWO4", "sTQ2", "sTQ3", "sOC2", "sCV", "sCS", "sXT2", "sXT3", "sXT4", "sWFC", "sODM2", "sPO0", "sPO1", "sPS0")]
        with nc.Block() as blk0:
            @blk0.gpsimd
            def _(g):
                g.memset(ones_sb[:], 1.0)
                g.memset(id_sb[:], 0.0).then_inc(sID, 1)
                g.wait_ge(sID, 1)
                g.affine_select(
                    out=id_sb[:], in_=id_sb[:],
                    compare_op=OP.not_equal, fill=1.0,
                    base=0, pattern=[[-1, 128]], channel_multiplier=1,
                )

        blkM = E(nc.Block())
        with ExitStack() as phb:
            Eb = phb.enter_context
            xT_sb = Eb(nc.sbuf_tensor("xTb", [128, NC_ * S], BF16))
            wq_sb = Eb(nc.sbuf_tensor("wq", [128, 2 * 5 * 16 * 128], BF16))
            q1_sb = Eb(nc.sbuf_tensor("q1", [128, S], BF16))
            k1_sb = Eb(nc.sbuf_tensor("k1", [128, S], BF16))
            q2_sb = Eb(nc.sbuf_tensor("q2", [128, S], BF16))
            k2_sb = Eb(nc.sbuf_tensor("k2", [128, S], BF16))
            v_sb = Eb(nc.sbuf_tensor("vv", [128, S], BF16))
            v2_sb = Eb(nc.sbuf_tensor("v2", [128, S], BF16))
            pb_sb = Eb(nc.sbuf_tensor("pb", [128, 3072], BF16))
            oc_sb = Eb(nc.sbuf_tensor("oc", [128, 3072], F32))
            sc_sb = Eb(nc.sbuf_tensor("sc_", [128, 3072], F32))
            r1_sb = Eb(nc.sbuf_tensor("r1", [128, 512], F32))
            r2_sb = Eb(nc.sbuf_tensor("r2", [128, 512], F32))
            t1_sb = Eb(nc.sbuf_tensor("t1", [128, 512], F32))
            t2_sb = Eb(nc.sbuf_tensor("t2", [128, 512], F32))
            oout_sb = Eb(nc.sbuf_tensor("oout", [128, 1024], BF16))
            stg_sb = Eb(nc.sbuf_tensor("stg", [128, 2 * D], BF16))
            psA = Eb(nc.psum_tensor("psA", [128, 1024], F32))
            psB = Eb(nc.psum_tensor("psB", [128, 1024], F32))
            pO0 = Eb(nc.psum_tensor("pO0", [128, 512], F32))
            pO1 = Eb(nc.psum_tensor("pO1", [128, 512], F32))
            pS0 = Eb(nc.psum_tensor("pS0", [128, 512], F32))
            pS1 = Eb(nc.psum_tensor("pS1", [128, 512], F32))
            blk = blkM
            sbank = [psA, psB]
            projb = [psA, psB]

            sS_after = {}
            _cnt = 0
            for _grp in range(H * QT):
                for _c in range(NC_):
                    _cnt += 2
                    sS_after[(_grp, _c)] = _cnt

            @blk.gpsimd
            def _(g):
                xtsem = [sXT, sXT2, sXT3, sXT4]
                xtv = xT_sb[:].rearrange("p (c t) -> p c t", c=NC_)
                xdv = xT_d.rearrange("(c p) t -> p c t", p=128)
                g.dma_start(out=wq_sb[:, 0:4096], in_=wqkv_d[0, :, 0:4096]).then_inc(sWQ, 16)
                g.dma_start(out=xtv[:, 0:4, :], in_=xdv[:, 0:4, :]).then_inc(sXT, 16)
                g.dma_start(out=xtv[:, 4:8, :], in_=xdv[:, 4:8, :]).then_inc(sXT2, 16)
                g.wait_ge(sWQ, 16)
                g.dma_start(out=wq_sb[:, 4096:10240], in_=wqkv_d[0, :, 4096:10240]).then_inc(sWQ, 16)
                g.dma_start(out=xtv[:, 8:12, :], in_=xdv[:, 8:12, :]).then_inc(sXT3, 16)
                g.dma_start(out=xtv[:, 12:16, :], in_=xdv[:, 12:16, :]).then_inc(sXT4, 16)
                for h in range(1, H):
                    if h >= 2:
                        g.wait_ge(sPM, (h - 1) * 32)
                    g.wait_ge(sWQ, 16 * (h + 1))
                    g.dma_start(out=wq_sb[:, (h % 2) * 10240:(h % 2) * 10240 + 10240],
                                in_=wqkv_d[h]).then_inc(sWQ, 16)
                    if h == 1:
                        for i in range(2 * NC_):
                            src_ap = (wo_d[(i % NC_) * 128:(i % NC_) * 128 + 128, :]
                                      if i < NC_ else
                                      wffn_d[(i % NC_) * 128:(i % NC_) * 128 + 128, :])
                            g.wait_ge(sCV, 16 * i)
                            if i >= 2:
                                g.wait_ge(sCS, 16 * (i - 1))
                            g.dma_start(
                                out=stg_sb[:, (i % 2) * D:(i % 2) * D + D], in_=src_ap
                            ).then_inc(sCV, 16)

            @blk.sync
            def _(sy):
                sy.dma_start(out=lam_sb[:], in_=lam_d).then_inc(sAUX, 16)
                for i in range(2 * NC_):
                    dst = wob_d[i % NC_] if i < NC_ else wfb_d[i % NC_]
                    sy.wait_ge(sCV, 16 * (i + 1))
                    sy.wait_ge(sCS, 16 * i)
                    sy.dma_start(out=dst, in_=stg_sb[:, (i % 2) * D:(i % 2) * D + D]).then_inc(sCS, 16)
                for grp in range(61):
                    sy.wait_ge(sCB, grp + 1)
                    h, qt = grp // QT, grp % QT
                    sy.wait_ge(sODMA, 16 * grp)
                    sy.dma_start(
                        out=osc_d[qt * 4:qt * 4 + 4, h].rearrange("j p t -> p j t"),
                        in_=oout_sb[:, (grp % 2) * 512:(grp % 2) * 512 + 512].rearrange(
                            "p (j t) -> p j t", j=4),
                    ).then_inc(sODMA, 16)

            @blk.tensor
            def _(te):
                for _s in (sXT, sXT2, sXT3, sXT4):
                    te.wait_ge(_s, 16)
                for h in range(H):
                    te.wait_ge(sWQ, 16 * (h + 2))
                    if h > 0:
                        te.wait_ge(sEXP, 64 * h)

                    def wslice(m, c):
                        base = (h % 2) * 10240 + (m * 16 + c) * 128
                        return wq_sb[:, base:base + 128]

                    def pdval(g):
                        return g + 1 if g < 16 else g + 2

                    g = 0
                    for tt in range(NT):
                        if h > 0 or g >= 2:
                            te.wait_ge(sPD, h * 33 + pdval(g - 2))
                        for c in range(NC_):
                            mm = te.matmul(
                                projb[g % 2][:, 0:128],
                                xT_sb[:, c * S + tt * 128: c * S + tt * 128 + 128],
                                wslice(0, c),
                                start=(c == 0), stop=(c == NC_ - 1),
                            )
                        mm.then_inc(sPM, 1)
                        g += 1
                    for m, _dst in enumerate((q1_sb, k1_sb, q2_sb, k2_sb)):
                        for qt in range(QT):
                            te.wait_ge(sPD, h * 33 + pdval(g - 2))
                            for c in range(NC_):
                                mm = te.matmul(
                                    projb[g % 2][:, 0:512],
                                    wslice(m + 1, c),
                                    xT_sb[:, c * S + qt * 512: c * S + qt * 512 + 512],
                                    start=(c == 0), stop=(c == NC_ - 1),
                                )
                            mm.then_inc(sPM, 1)
                            g += 1

                    def s_issue(k):
                        qt_, c = k // NC_, k % NC_
                        s = (h * QT + qt_) * NC_ + c
                        if k < 2:
                            te.wait_ge(sPD, h * 33 + 32 + k)
                        if s >= 2:
                            te.wait_ge(sEXP, s - 1)
                        for j, (kk, qq) in enumerate(((k1_sb, q1_sb), (k2_sb, q2_sb))):
                            te.matmul(
                                sbank[c % 2][:, j * 512:(j + 1) * 512],
                                kk[:, c * 128:(c + 1) * 128],
                                qq[:, qt_ * 512:(qt_ + 1) * 512],
                                start=True, stop=True,
                            ).then_inc(sS, 1)

                    def pv_issue(k):
                        qt_, cc = k // NC_, k % NC_
                        grp = h * QT + qt_
                        s = grp * NC_ + cc
                        te.wait_ge(sEXP, s + 1)
                        if cc == 0 and grp > 0:
                            te.wait_ge(sOC, grp)
                            te.wait_ge(sOC2, grp)
                        b = (s % 3) * 1024
                        p0 = pb_sb[:, b:b + 512]
                        p1 = pb_sb[:, b + 512:b + 1024]
                        if cc < NC_ - 1:
                            te.matmul(pO0[:], v_sb[:, cc * 128:(cc + 1) * 128], p0, start=(cc == 0), stop=False)
                            te.matmul(pO1[:], v2_sb[:, cc * 128:(cc + 1) * 128], p1, start=(cc == 0), stop=False)
                            te.matmul(pS0[:], ones_sb[:], p0, start=(cc == 0), stop=False)
                            te.matmul(pS1[:], ones_sb[:], p1, start=(cc == 0),
                                      stop=False).then_inc(sPVD, 1)
                        else:
                            te.matmul(pO0[:], v_sb[:, cc * 128:(cc + 1) * 128], p0,
                                      start=False, stop=True).then_inc(sPO0, 1)
                            te.matmul(pS0[:], ones_sb[:], p0,
                                      start=False, stop=True).then_inc(sPS0, 1)
                            te.matmul(pO1[:], v2_sb[:, cc * 128:(cc + 1) * 128], p1,
                                      start=False, stop=True).then_inc(sPO1, 1)
                            te.matmul(pS1[:], ones_sb[:], p1,
                                      start=False, stop=True).then_inc(sPVD, 1)

                    for k in range(QT * NC_):
                        s_issue(k)
                        if k >= 2:
                            pv_issue(k - 2)
                    pv_issue(QT * NC_ - 2)
                    pv_issue(QT * NC_ - 1)

            @blk.scalar
            def _(sc):
                for grp in range(H * QT):
                    for c in range(NC_):
                        s = grp * NC_ + c
                        sc.wait_ge(sS, sS_after[(grp, c)])
                        if s >= 3:
                            sc.wait_ge(sPVD, s - 2)
                        sc.activation(
                            pb_sb[:, (s % 3) * 1024:(s % 3) * 1024 + 1024],
                            sbank[c % 2][:], AF.Exp,
                            scale=INV_SQRT_HD,
                        ).then_inc(sEXP, 1)
                    sc.wait_ge(sPO0, grp + 1)
                    if grp >= 3:
                        sc.wait_ge(sCB, grp - 2)
                    sc.copy(oc_sb[:, (grp % 3) * 1024:(grp % 3) * 1024 + 512], pO0[:])
                    sc.wait_ge(sPO1, grp + 1)
                    sc.copy(oc_sb[:, (grp % 3) * 1024 + 512:(grp % 3) * 1024 + 1024],
                            pO1[:]).then_inc(sOC2, 1)


            vchain = [None, 0]

            @blk.vector
            def _(ve):
                ve = ChainedEng(ve, sVC, vchain[0])
                ve._n = vchain[1]

                def divide(g2):
                    b2 = (g2 % 3) * 1024
                    ve.wait_ge(sOC2, g2 + 1)
                    if g2 >= 2:
                        ve.wait_ge(sODMA, 16 * min(g2 - 1, 61))
                    if g2 >= 63:
                        ve.wait_ge(sODM2, 16 * (g2 - 62))
                    ve.reciprocal(r1_sb[:], sc_sb[:, b2:b2 + 512])
                    ve.reciprocal(r2_sb[:], sc_sb[:, b2 + 512:b2 + 1024])
                    ve.tensor_mul(t1_sb[:], oc_sb[:, b2:b2 + 512], r1_sb[:])
                    ve.tensor_mul(t2_sb[:], oc_sb[:, b2 + 512:b2 + 1024], r2_sb[:])
                    ve.tensor_add(oout_sb[:, (g2 % 2) * 512:(g2 % 2) * 512 + 512], t1_sb[:], t2_sb[:], inc=(sCB, 1, g2 + 1))

                ve.wait_ge(sAUX, 16)
                ve.tensor_scalar(neglam_sb[:], lam_sb[:], -1.0, None, OP.mult)
                for h in range(H):
                    g = 0
                    pd = 0
                    for tt in range(NT):
                        ve.wait_ge(sPM, h * 32 + g + 1)
                        pd += 1
                        ve.tensor_copy(
                            v_sb[:, tt * 128:(tt + 1) * 128], projb[g % 2][:, 0:128],
                            inc=(sPD, 1, h * 33 + pd),
                        )
                        g += 1
                    pd += 1
                    ve.tensor_scalar(
                        v2_sb[:], v_sb[:], neglam_sb[:, h:h + 1], None, OP.mult,
                        inc=(sPD, 1, h * 33 + pd),
                    )
                    for m, dst in enumerate((q1_sb, k1_sb, q2_sb, k2_sb)):
                        for qt in range(QT):
                            ve.wait_ge(sPM, h * 32 + g + 1)
                            pd += 1
                            ve.tensor_copy(
                                dst[:, qt * 512:(qt + 1) * 512], projb[g % 2][:, 0:512],
                                inc=(sPD, 1, h * 33 + pd),
                            )
                            g += 1
                    for qt in range(QT):
                        grp = h * QT + qt
                        if grp >= 2:
                            divide(grp - 2)
                        ve.wait_ge(sPS0, grp + 1)
                        if grp >= 3:
                            ve.wait_ge(sCB, grp - 2)
                        b = (grp % 3) * 1024
                        ve.tensor_copy(sc_sb[:, b:b + 512], pS0[:])
                        ve.wait_ge(sPVD, NC_ * (grp + 1))
                        ve.tensor_copy(sc_sb[:, b + 512:b + 1024], pS1[:], inc=(sOC, 1, grp + 1))
                divide(H * QT - 2)
                divide(H * QT - 1)
                vchain[0] = ve.last
                vchain[1] = ve._n

        with nc.sbuf_tensor("qT", [128, NC_ * S], BF16, side="right") as qT_sb:
            with ExitStack() as phc:
                Ec = phc.enter_context
                wo_sb = Ec(nc.sbuf_tensor("woS", [128, NC_ * D], BF16))
                ocol_sb = Ec(nc.sbuf_tensor("ocol", [128, 2 * H * 128], BF16))
                xin_sb = Ec(nc.sbuf_tensor("xin", [128, 2 * D], F32))
                z_sb = Ec(nc.sbuf_tensor("zz", [128, 2 * D], F32))
                y_sb = Ec(nc.sbuf_tensor("yy", [128, D], F32))
                t_sb = Ec(nc.sbuf_tensor("tt_", [128, D], F32))
                dump_sb = Ec(nc.sbuf_tensor("dump", [128, D], F32))
                q8_sb = Ec(nc.sbuf_tensor("qi8", [128, D], I8))
                qb_sb = Ec(nc.sbuf_tensor("qbf", [128, D], BF16))
                yf16_sb = Ec(nc.sbuf_tensor("yf16", [128, 2 * D], F16))
                st1_sb = Ec(nc.sbuf_tensor("st1", [128, 2], F32))
                st2_sb = Ec(nc.sbuf_tensor("st2", [128, 2], F32))
                st3_sb = Ec(nc.sbuf_tensor("st3", [128, 1], F32))
                st4_sb = Ec(nc.sbuf_tensor("st4", [128, 1], F32))
                st5_sb = Ec(nc.sbuf_tensor("st5", [128, 1], F32))
                st6_sb = Ec(nc.sbuf_tensor("st6", [128, 1], F32))
                pattnL = Ec(nc.psum_tensor("pattnL", [128, 1024], F32))
                pattnH = Ec(nc.psum_tensor("pattnH", [128, 1024], F32))
                ptrA = Ec(nc.psum_tensor("ptrA", [128, 1024], BF16))
                ptrB = Ec(nc.psum_tensor("ptrB", [128, 1024], BF16))
                pattnC = Ec(nc.psum_tensor("pattnC", [128, 1024], F32))
                blk2 = blkM
                path = [pattnL, pattnH, pattnC]
                trb = [ptrA, ptrB]
                wosem = [sWO, sWO2, sWO3, sWO4]

                @blk2.sync
                def _(sy):
                    sy.wait_ge(sCS, 32 * NC_)
                    sy.wait_ge(sWQ, 16 * H)
                    sy.wait_ge(sPM, 32 * H)

                    def ywrite(tt):
                        sy.wait_ge(sYD, tt + 1)
                        sy.wait_ge(sYDMA, 16 * tt)
                        sy.dma_start(out=ysc_d[tt], in_=yf16_sb[:, (tt % 2) * D:(tt % 2) * D + D]).then_inc(sYDMA, 16)

                    for tt in range(NT):
                        if tt >= 2:
                            sy.wait_ge(sAR, 2 * (tt - 1))
                        sy.wait_ge(sODMA, 16 * 61)
                        if tt >= 4:
                            sy.wait_ge(sODM2, 16 * (tt // 4))
                        sy.wait_ge(sOI, 16 * tt)
                        sy.dma_start(
                            out=ocol_sb[:, (tt % 2) * 2048:(tt % 2) * 2048 + 2048].rearrange("p (h t) -> p h t", h=H),
                            in_=osc_d[tt].rearrange("h p t -> p h t"),
                        ).then_inc(sOI, 16)
                        if tt >= 2:
                            sy.wait_ge(sZD, 2 * (tt - 1))
                        sy.wait_ge(sXI, 16 * tt)
                        sy.dma_start(
                            out=xin_sb[:, (tt % 2) * D:(tt % 2) * D + D], in_=x_d[tt * 128:(tt + 1) * 128, :]
                        ).then_inc(sXI, 16)
                        if tt >= 2:
                            ywrite(tt - 2)
                    ywrite(NT - 2)
                    ywrite(NT - 1)

                @blk2.gpsimd
                def _(g):
                    g.wait_ge(sPM, 32 * H)
                    g.wait_ge(sCS, 16 * NC_)
                    for c in range(NC_):
                        sem = wosem[c % 4]
                        g.wait_ge(sem, 16 * (c // 4))
                        g.dma_start(
                            out=wo_sb[:, c * D:(c + 1) * D],
                            in_=wob_d[c],
                        ).then_inc(sem, 16)
                    for grp in range(61, H * QT):
                        g.wait_ge(sCB, grp + 1)
                        h_, qt_ = grp // QT, grp % QT
                        if grp > 61:
                            g.wait_ge(sODM2, 16 * (grp - 61))
                        g.dma_start(
                            out=osc_d[qt_ * 4:qt_ * 4 + 4, h_].rearrange("j p t -> p j t"),
                            in_=oout_sb[:, (grp % 2) * 512:(grp % 2) * 512 + 512].rearrange(
                                "p (j t) -> p j t", j=4),
                        ).then_inc(sODM2, 16)

                @blk2.tensor
                def _(te):
                    te.wait_ge(sEXP, 16 * H * QT - 1)

                    def transposes(tt):
                        if tt == 0:
                            te.wait_ge(sOC2, H * QT)
                        te.wait_ge(sQ, tt + 1)
                        for grp8 in range(2):
                            gg = 2 * tt + grp8
                            if gg >= 2:
                                te.wait_ge(sTRE, gg - 1)
                            for j8 in range(8):
                                j = grp8 * 8 + j8
                                mm = te.transpose(
                                    trb[gg % 2][:, j8 * 128:(j8 + 1) * 128],
                                    qb_sb[:, j * 128:(j + 1) * 128],
                                    id_sb[:],
                                )
                            mm.then_inc(sTR, 1)

                    for tt in range(NT):
                        for half in range(2):
                            s_ = 2 * tt + half
                            if s_ >= 3:
                                te.wait_ge(sZD, s_ - 2)
                            if tt == 0 and half == 1:
                                te.wait_ge(sEXP, 16 * H * QT)
                            if s_ == 2:
                                te.wait_ge(sOC, H * QT)
                            for c in range(NC_):
                                if half == 0 and c == 0:
                                    te.wait_ge(sOI, 16 * (tt + 1))
                                if tt == 0 and half == 0:
                                    te.wait_ge(wosem[c % 4], 16 * (c // 4 + 1))
                                for nt in (half * 2, half * 2 + 1):
                                    mm = te.matmul(
                                        path[s_ % 3][:, (nt - half * 2) * 512:(nt - half * 2) * 512 + 512],
                                        ocol_sb[:, (tt % 2) * 2048 + c * 128:(tt % 2) * 2048 + (c + 1) * 128],
                                        wo_sb[:, c * D + nt * 512: c * D + nt * 512 + 512],
                                        start=(c == 0), stop=(c == NC_ - 1),
                                    )
                                if tt == NT - 1 and half == 1 and c < NC_ - 1:
                                    mm.then_inc(sWFC, 1)
                            mm.then_inc(sAR, 1)
                        if tt >= 2:
                            transposes(tt - 2)
                    transposes(NT - 2)
                    transposes(NT - 1)

                @blk2.scalar
                def _(sc):
                    sc.wait_ge(sCS, 32 * NC_)
                    sc.wait_ge(sODMA, 16 * 61)
                    sc.wait_ge(sODM2, 48)

                    def evacs(tt):
                        for grp8 in range(2):
                            gg = 2 * tt + grp8
                            sc.wait_ge(sTR, gg + 1)
                            out_ap = qT_sb[:].rearrange("p (j r) -> p j r", j=NC_)[
                                :, grp8 * 8:(grp8 + 1) * 8, tt * 128:(tt + 1) * 128]
                            sc.copy(
                                out_ap,
                                trb[gg % 2][:].rearrange("p (j r) -> p j r", j=8),
                            ).then_inc(sTRE, 1)

                    for tt in range(NT):
                        if tt >= 1:
                            sc.wait_ge(sVST, tt)
                            sc.activation(st5_sb[:], st6_sb[:], AF.Sqrt).then_inc(sSQT, 1)
                        sc.wait_ge(sZD, 2 * (tt + 1))
                        if tt >= 2:
                            sc.wait_ge(sVST, tt - 1)
                        if tt >= 1:
                            sc.wait_ge(sST, 2 * tt)
                        sc.activation(dump_sb[:], z_sb[:, (tt % 2) * D:(tt % 2) * D + D],
                                      AF.Copy, accum_out=st1_sb[:, tt % 2:tt % 2 + 1]).then_inc(sST, 1)
                        sc.wait_ge(sST, 2 * tt + 1)
                        sc.activation(dump_sb[:], z_sb[:, (tt % 2) * D:(tt % 2) * D + D],
                                      AF.Square, accum_out=st2_sb[:, tt % 2:tt % 2 + 1]).then_inc(sST, 1)
                        if tt >= 2:
                            evacs(tt - 2)
                    sc.wait_ge(sVST, NT)
                    sc.activation(st5_sb[:], st6_sb[:], AF.Sqrt).then_inc(sSQT, 1)
                    evacs(NT - 2)
                    evacs(NT - 1)

                @blk2.vector
                def _(ve):
                    ve = ChainedEng(ve, sVC, vchain[0])
                    ve._n = vchain[1]
                    ve.wait_ge(sCS, 32 * NC_)

                    def post(k):
                        ve.wait_ge(sST, 2 * (k + 1))
                        ve.tensor_scalar(st1_sb[:, k % 2:k % 2 + 1], st1_sb[:, k % 2:k % 2 + 1], INV_D, None, OP.mult)
                        ve.tensor_scalar(st2_sb[:, k % 2:k % 2 + 1], st2_sb[:, k % 2:k % 2 + 1], INV_D, None, OP.mult)
                        ve.tensor_mul(st6_sb[:], st1_sb[:, k % 2:k % 2 + 1], st1_sb[:, k % 2:k % 2 + 1])
                        ve.tensor_sub(st6_sb[:], st2_sb[:, k % 2:k % 2 + 1], st6_sb[:])
                        ve.tensor_scalar(st6_sb[:], st6_sb[:], LN_EPS, None, OP.add, inc=(sVST, 1, k + 1))
                        ve.wait_ge(sSQT, k + 1)
                        ve.reciprocal(st3_sb[:], st5_sb[:])
                        ve.tensor_mul(st2_sb[:, k % 2:k % 2 + 1], st6_sb[:], st3_sb[:])
                        ve.tensor_mul(st2_sb[:, k % 2:k % 2 + 1], st2_sb[:, k % 2:k % 2 + 1], st3_sb[:])
                        ve.tensor_scalar(st2_sb[:, k % 2:k % 2 + 1], st2_sb[:, k % 2:k % 2 + 1], -0.5, 1.5, OP.mult, OP.add)
                        ve.tensor_mul(st3_sb[:], st3_sb[:], st2_sb[:, k % 2:k % 2 + 1])
                        ve.tensor_mul(st4_sb[:], st1_sb[:, k % 2:k % 2 + 1], st3_sb[:])
                        ve.tensor_scalar(st4_sb[:], st4_sb[:], -1.0, None, OP.mult)
                        ve.tensor_scalar(y_sb[:], z_sb[:, (k % 2) * D:(k % 2) * D + D], st3_sb[:], st4_sb[:],
                                         OP.mult, OP.add)
                        ve.tensor_reduce(st5_sb[:], y_sb[:], AX, OP.max,
                                         apply_absolute_value=True)
                        ve.tensor_scalar(scale_all[:, k:k + 1], st5_sb[:], Q_EPS,
                                         INV127, OP.max, OP.mult)
                        ve.reciprocal(st6_sb[:], scale_all[:, k:k + 1])
                        ve.tensor_scalar(t_sb[:], y_sb[:], st6_sb[:], MAGIC, OP.mult, OP.add)
                        ve.tensor_scalar(q8_sb[:], t_sb[:], MAGIC, None, OP.subtract)
                        if k >= 1:
                            ve.wait_ge(sTR, 2 * k)
                        ve.tensor_copy(qb_sb[:], q8_sb[:], inc=(sQ, 1, k + 1))
                        if k >= 2:
                            ve.wait_ge(sYDMA, 16 * (k - 1))
                        ve.tensor_copy(yf16_sb[:, (k % 2) * D:(k % 2) * D + D], y_sb[:], inc=(sYD, 1, k + 1))

                    for tt in range(NT):
                        ve.wait_ge(sXI, 16 * (tt + 1))
                        if tt >= 2:
                            ve.wait_ge(sST, 2 * (tt - 1))
                        for half in range(2):
                            ve.wait_ge(sAR, 2 * tt + half + 1)
                            ve.tensor_add(
                                z_sb[:, (tt % 2) * D + half * 1024:(tt % 2) * D + half * 1024 + 1024],
                                path[(2 * tt + half) % 3][:],
                                xin_sb[:, (tt % 2) * D + half * 1024:(tt % 2) * D + half * 1024 + 1024],
                                inc=(sZD, 1, 2 * tt + half + 1))
                        if tt >= 1:
                            post(tt - 1)
                    post(NT - 1)
                    vchain[0] = ve.last
                    vchain[1] = ve._n

            with ExitStack() as phd:
                Ed = phd.enter_context
                wsb_sb = Ed(nc.sbuf_tensor("wsb_sb", [128, D], F32))
                bb_sb = Ed(nc.sbuf_tensor("bb_sb", [128, D], F32))
                wf_sb = Ed(nc.sbuf_tensor("wfsb", [128, NC_ * D], BF16))
                yin_sb = Ed(nc.sbuf_tensor("yin", [128, 2 * D], F16))
                f_sb = Ed(nc.sbuf_tensor("ff", [128, D], F32))
                f16_sb = Ed(nc.sbuf_tensor("f16", [128, D], F16))
                z2_sb = Ed(nc.sbuf_tensor("z2", [128, 2 * D], F32))
                o_sb = Ed(nc.sbuf_tensor("oo", [128, 2 * D], F32))
                su1_sb = Ed(nc.sbuf_tensor("su1", [128, 2], F32))
                su2_sb = Ed(nc.sbuf_tensor("su2", [128, 2], F32))
                su3_sb = Ed(nc.sbuf_tensor("su3", [128, 1], F32))
                su4_sb = Ed(nc.sbuf_tensor("su4", [128, 1], F32))
                su5_sb = Ed(nc.sbuf_tensor("su5", [128, 1], F32))
                su6_sb = Ed(nc.sbuf_tensor("su6", [128, 1], F32))
                pffnA = Ed(nc.psum_tensor("pffnA", [128, 2048], F32))
                pffnB = Ed(nc.psum_tensor("pffnB", [128, 2048], F32))
                blk3 = blkM
                pffn = [pffnA, pffnB]

                @blk3.gpsimd
                def _(g):
                    g.wait_ge(sWFC, 2)
                    g.dma_start(out=wsb_sb[:], in_=wsb_d).then_inc(sTQ2, 16)
                    g.wait_ge(sWFC, 4)
                    g.wait_ge(sTQ2, 16)
                    g.dma_start(out=bb_sb[:], in_=bb_d).then_inc(sTQ2, 16)
                    g.wait_ge(sCS, 32 * NC_)
                    wfsemd = [sWF, sWC, sYR, sAFF]
                    for j in range(4):
                        if j == 0:
                            g.wait_ge(sWFC, 8)
                        elif j == 1:
                            g.wait_ge(sWFC, 12)
                        else:
                            g.wait_ge(sAR, 2 * NT)
                            if j == 3:
                                g.wait_ge(sZD, 2 * NT - 2)
                        g.dma_start(
                            out=wf_sb[:, j * 4 * D:(j + 1) * 4 * D].rearrange(
                                "p (c d) -> p c d", c=4),
                            in_=wfb_d[j * 4:(j + 1) * 4].rearrange("c p d -> p c d"),
                        ).then_inc(wfsemd[j], 16)

                @blk3.sync
                def _(sy):
                    sy.wait_ge(sZD, 2 * NT)
                    sy.wait_ge(sYDMA, 16 * NT)

                    def owrite(tt):
                        sy.wait_ge(sOUTV, tt + 1)
                        sy.wait_ge(sOUT, 16 * tt)
                        sy.dma_start(
                            out=out_d[tt * 128:(tt + 1) * 128, :], in_=o_sb[:, (tt % 2) * D:(tt % 2) * D + D]
                        ).then_inc(sOUT, 16)

                    for tt in range(NT):
                        if tt >= 2:
                            sy.wait_ge(sZ2, tt - 1)
                        sy.wait_ge(sYI, 16 * tt)
                        sy.dma_start(out=yin_sb[:, (tt % 2) * D:(tt % 2) * D + D], in_=ysc_d[tt]).then_inc(sYI, 16)
                        if tt >= 2:
                            owrite(tt - 2)
                    owrite(NT - 2)
                    owrite(NT - 1)
                    sy.wait_ge(sOUT, 16 * NT)

                @blk3.scalar
                def _(sc):
                    for tt in range(NT):
                        if tt >= 1:
                            sc.wait_ge(sVS2, tt)
                            sc.activation(su5_sb[:], su6_sb[:], AF.Sqrt).then_inc(sSQ2, 1)
                        sc.wait_ge(sZ2, tt + 1)
                        if tt >= 2:
                            sc.wait_ge(sVS2, tt - 1)
                        if tt >= 1:
                            sc.wait_ge(sS2T, 2 * tt)
                        sc.activation(f_sb[:], z2_sb[:, (tt % 2) * D:(tt % 2) * D + D],
                                      AF.Copy, accum_out=su1_sb[:, tt % 2:tt % 2 + 1]).then_inc(sS2T, 1)
                        sc.wait_ge(sS2T, 2 * tt + 1)
                        sc.activation(f_sb[:], z2_sb[:, (tt % 2) * D:(tt % 2) * D + D],
                                      AF.Square, accum_out=su2_sb[:, tt % 2:tt % 2 + 1]).then_inc(sS2T, 1)
                    sc.wait_ge(sVS2, NT)
                    sc.activation(su5_sb[:], su6_sb[:], AF.Sqrt).then_inc(sSQ2, 1)

                @blk3.tensor
                def _(te):
                    te.wait_ge(sZD, 2 * NT)
                    for tt in range(NT):
                        if tt == 1:
                            te.wait_ge(sTRE, 2 * NT)
                        te.wait_ge(sTRE, 2 * tt + 2)
                        if tt >= 2:
                            te.wait_ge(sPF, tt - 1)
                        for c in range(NC_):
                            if tt == 0 and c % 4 == 0:
                                te.wait_ge((sWF, sWC, sYR, sAFF)[c // 4], 16)
                            for nt in range(4):
                                mm = te.matmul(
                                    pffn[tt % 2][:, nt * 512:(nt + 1) * 512],
                                    qT_sb[:, c * S + tt * 128: c * S + tt * 128 + 128],
                                    wf_sb[:, c * D + nt * 512: c * D + nt * 512 + 512],
                                    start=(c == 0), stop=(c == NC_ - 1),
                                )
                        mm.then_inc(sAR, 1)

                @blk3.vector
                def _(ve):
                    ve = ChainedEng(ve, sVC, vchain[0])
                    ve._n = vchain[1]
                    ve.wait_ge(sTQ2, 32)

                    def post2(k):
                        ve.wait_ge(sS2T, 2 * (k + 1))
                        ve.tensor_scalar(su1_sb[:, k % 2:k % 2 + 1], su1_sb[:, k % 2:k % 2 + 1], INV_D, None, OP.mult)
                        ve.tensor_scalar(su2_sb[:, k % 2:k % 2 + 1], su2_sb[:, k % 2:k % 2 + 1], INV_D, None, OP.mult)
                        ve.tensor_mul(su6_sb[:], su1_sb[:, k % 2:k % 2 + 1], su1_sb[:, k % 2:k % 2 + 1])
                        ve.tensor_sub(su6_sb[:], su2_sb[:, k % 2:k % 2 + 1], su6_sb[:])
                        ve.tensor_scalar(su6_sb[:], su6_sb[:], LN_EPS, None, OP.add, inc=(sVS2, 1, k + 1))
                        ve.wait_ge(sSQ2, k + 1)
                        ve.reciprocal(su3_sb[:], su5_sb[:])
                        ve.tensor_mul(su2_sb[:, k % 2:k % 2 + 1], su6_sb[:], su3_sb[:])
                        ve.tensor_mul(su2_sb[:, k % 2:k % 2 + 1], su2_sb[:, k % 2:k % 2 + 1], su3_sb[:])
                        ve.tensor_scalar(su2_sb[:, k % 2:k % 2 + 1], su2_sb[:, k % 2:k % 2 + 1], -0.5, 1.5, OP.mult, OP.add)
                        ve.tensor_mul(su3_sb[:], su3_sb[:], su2_sb[:, k % 2:k % 2 + 1])
                        ve.tensor_mul(su4_sb[:], su1_sb[:, k % 2:k % 2 + 1], su3_sb[:])
                        ve.tensor_scalar(su4_sb[:], su4_sb[:], -1.0, None, OP.mult)
                        if k >= 2:
                            ve.wait_ge(sOUT, 16 * (k - 1))
                        ve.tensor_scalar(o_sb[:, (k % 2) * D:(k % 2) * D + D], z2_sb[:, (k % 2) * D:(k % 2) * D + D],
                                         su3_sb[:], su4_sb[:], OP.mult, OP.add,
                                         inc=(sOUTV, 1, k + 1))

                    for tt in range(NT):
                        ve.wait_ge(sAR, 2 * NT + tt + 1)
                        ve.wait_ge(sYI, 16 * (tt + 1))
                        if tt >= 1:
                            ve.wait_ge(sS2T, 2 * tt)
                        if tt >= 2:
                            ve.wait_ge(sOUTV, tt - 1)
                        ve.scalar_tensor_tensor(
                            f_sb[:], pffn[tt % 2][:], scale_all[:, tt:tt + 1], wsb_sb[:],
                            OP.mult, OP.mult, inc=(sPF, 1, tt + 1),
                        )
                        ve.tensor_add(f16_sb[:], f_sb[:], bb_sb[:])
                        ve.tensor_add(z2_sb[:, (tt % 2) * D:(tt % 2) * D + D], f16_sb[:], yin_sb[:, (tt % 2) * D:(tt % 2) * D + D], inc=(sZ2, 1, tt + 1))
                        if tt >= 1:
                            post2(tt - 1)
                    post2(NT - 1)

    return nc
    return nc


_NC_CACHE = None


def kernel(L0_attn_x, L0_attn_q1_w, L0_attn_k1_w, L0_attn_q2_w, L0_attn_k2_w,
           L0_attn_v_w, L0_attn_o_w, L0_attn_lambda_param,
           L0_ffn_weight, L0_ffn_w_scale, L0_ffn_bias):
    global _NC_CACHE
    B = L0_attn_x.shape[0]

    def head_major(w):
        wt = np.ascontiguousarray(np.asarray(w).T)
        return wt.reshape(16, 128, 16, 128).transpose(2, 1, 0, 3)

    parts = [head_major(w) for w in (L0_attn_v_w, L0_attn_q1_w, L0_attn_k1_w,
                                     L0_attn_q2_w, L0_attn_k2_w)]
    wqkv = np.ascontiguousarray(np.stack(parts, axis=2))
    wqkv = wqkv.reshape(16, 128, 5 * 16 * 128)

    wo = np.ascontiguousarray(np.asarray(L0_attn_o_w).T)
    wffn = np.ascontiguousarray(np.asarray(L0_ffn_weight).T)
    wsb = np.ascontiguousarray(np.broadcast_to(
        np.asarray(L0_ffn_w_scale)[:, 0][None, :], (128, 2048)))
    bb = np.ascontiguousarray(np.broadcast_to(
        np.asarray(L0_ffn_bias)[None, :], (128, 2048)))
    lam = np.ascontiguousarray(np.broadcast_to(
        np.asarray(L0_attn_lambda_param)[None, :], (128, 16)))

    if _NC_CACHE is None:
        _NC_CACHE = build_program()
    nc = _NC_CACHE

    in_maps = []
    for b in range(B):
        xb = np.ascontiguousarray(np.asarray(L0_attn_x[b]))
        in_maps.append({
            "xT": np.ascontiguousarray(xb.T),
            "x": xb,
            "wqkv": wqkv, "wo": wo, "wffn": wffn,
            "wsb": wsb, "bb": bb, "lam": lam,
        })
    res = run_bass_kernel_spmd(nc, in_maps, core_ids=list(range(8)))
    out = np.stack([np.asarray(r["out"]) for r in res.results], axis=0)
    return out.astype(np.float32)


# version 28
# speedup vs baseline: 1.2559x; 1.0239x over previous
from contextlib import ExitStack

import numpy as np

import concourse.bass as bass
import concourse.mybir as mybir
from concourse.bass_utils import run_bass_kernel_spmd

F32 = mybir.dt.float32
BF16 = mybir.dt.bfloat16
F16 = mybir.dt.float16
I8 = mybir.dt.int8
AX = mybir.AxisListType.X
OP = mybir.AluOpType
AF = mybir.ActivationFunctionType

D = 2048
S = 2048
H = 16
NC_ = 16
NT = 16
QT = 4
INV_SQRT_HD = 1.0 / np.sqrt(128.0)
MAGIC = 12582912.0
LN_EPS = 1e-5
Q_EPS = 1e-7
INV_D = 1.0 / 2048.0
INV127 = 1.0 / 127.0


class ChainedEng:

    def __init__(self, eng, sem, last=None):
        self._e, self._s = eng, sem
        self.last = last

    def __getattr__(self, name):
        f = getattr(self._e, name)

        def g(*a, inc=None, **k):
            if self.last is not None:
                self._e.wait_ge(self.last[0], self.last[1])
            inst = f(*a, **k)
            if inc is None:
                if self.last is not None and self.last[0] is self._s:
                    nxt = self.last[1] + 1
                else:
                    nxt = self._n + 1
                inst.then_inc(self._s, 1)
                self._n = nxt
                self.last = (self._s, nxt)
            else:
                sem, kk, after = inc
                inst.then_inc(sem, kk)
                self.last = (sem, after)
            return inst

        return g

    @property
    def _n(self):
        return self.__dict__.setdefault("_nv", 0)

    @_n.setter
    def _n(self, v):
        self.__dict__["_nv"] = v

    def wait_ge(self, sem, v):
        return self._e.wait_ge(sem, v)


def build_program():
    nc = bass.Bass("TRN2", target_bir_lowering=False)

    xT_d = nc.dram_tensor("xT", [D, S], F32, kind="ExternalInput").ap()
    x_d = nc.dram_tensor("x", [S, D], F32, kind="ExternalInput").ap()
    wqkv_d = nc.dram_tensor("wqkv", [H, 128, 5 * 16 * 128], F32, kind="ExternalInput").ap()
    wo_d = nc.dram_tensor("wo", [D, D], F32, kind="ExternalInput").ap()
    wffn_d = nc.dram_tensor("wffn", [D, D], I8, kind="ExternalInput").ap()
    wsb_d = nc.dram_tensor("wsb", [128, D], F32, kind="ExternalInput").ap()
    bb_d = nc.dram_tensor("bb", [128, D], F32, kind="ExternalInput").ap()
    lam_d = nc.dram_tensor("lam", [128, H], F32, kind="ExternalInput").ap()
    out_d = nc.dram_tensor("out", [S, D], F32, kind="ExternalOutput").ap()
    osc_d = nc.dram_tensor("osc", [NT, H, 128, 128], BF16).ap()
    wob_d = nc.dram_tensor("wob", [NC_, 128, D], BF16).ap()
    wfb_d = nc.dram_tensor("wfb", [NC_, 128, D], BF16).ap()
    ysc_d = nc.dram_tensor("ysc", [NT, 128, D], F16).ap()

    with ExitStack() as top:
        E = top.enter_context
        ones_sb = E(nc.sbuf_tensor("ones", [128, 128], BF16))
        id_sb = E(nc.sbuf_tensor("ident", [128, 128], BF16))
        lam_sb = E(nc.sbuf_tensor("lam_sb", [128, H], F32))
        neglam_sb = E(nc.sbuf_tensor("neglam", [128, H], F32))
        scale_all = E(nc.sbuf_tensor("scale_all", [128, NT], F32))
        (sWQ, sXT, sPM, sPD, sS, sEXP, sPVD, sCB, sODMA, sAUX,
         sOI, sXI, sWO, sAR, sZD, sQ, sTR, sTRE, sYD, sYDMA,
         sVST, sSQT, sWF, sWC, sYI, sZ2, sPF, sOUTV, sOUT, sVS2, sSQ2, sID, sVC, sOC,
         sST, sYR, sAFF, sS2T, sAF2, sWO2, sWO3, sWO4, sTQ2, sTQ3, sOC2, sCV, sCS, sXT2, sXT3, sXT4, sWFC, sODM2, sPO0, sPO1, sPS0) = [
            E(nc.semaphore(n)) for n in (
                "sWQ", "sXT", "sPM", "sPD", "sS", "sEXP", "sPVD", "sCB",
                "sODMA", "sAUX", "sOI", "sXI", "sWO", "sAR", "sZD", "sQ",
                "sTR", "sTRE", "sYD", "sYDMA", "sVST", "sSQT", "sWF", "sWC",
                "sYI", "sZ2", "sPF", "sOUTV", "sOUT", "sVS2", "sSQ2", "sID", "sVC", "sOC",
                "sST", "sYR", "sAFF", "sS2T", "sAF2", "sWO2", "sWO3", "sWO4", "sTQ2", "sTQ3", "sOC2", "sCV", "sCS", "sXT2", "sXT3", "sXT4", "sWFC", "sODM2", "sPO0", "sPO1", "sPS0")]
        with nc.Block() as blk0:
            @blk0.gpsimd
            def _(g):
                g.memset(ones_sb[:], 1.0)
                g.memset(id_sb[:], 0.0).then_inc(sID, 1)
                g.wait_ge(sID, 1)
                g.affine_select(
                    out=id_sb[:], in_=id_sb[:],
                    compare_op=OP.not_equal, fill=1.0,
                    base=0, pattern=[[-1, 128]], channel_multiplier=1,
                )

        blkM = E(nc.Block())
        with ExitStack() as phb:
            Eb = phb.enter_context
            xT_sb = Eb(nc.sbuf_tensor("xTb", [128, NC_ * S], BF16))
            wq_sb = Eb(nc.sbuf_tensor("wq", [128, 2 * 5 * 16 * 128], BF16))
            q1_sb = Eb(nc.sbuf_tensor("q1", [128, S], BF16))
            k1_sb = Eb(nc.sbuf_tensor("k1", [128, S], BF16))
            q2_sb = Eb(nc.sbuf_tensor("q2", [128, S], BF16))
            k2_sb = Eb(nc.sbuf_tensor("k2", [128, S], BF16))
            v_sb = Eb(nc.sbuf_tensor("vv", [128, S], BF16))
            v2_sb = Eb(nc.sbuf_tensor("v2", [128, S], BF16))
            pb_sb = Eb(nc.sbuf_tensor("pb", [128, 3072], BF16))
            oc_sb = Eb(nc.sbuf_tensor("oc", [128, 3072], F32))
            sc_sb = Eb(nc.sbuf_tensor("sc_", [128, 3072], F32))
            r1_sb = Eb(nc.sbuf_tensor("r1", [128, 512], F32))
            r2_sb = Eb(nc.sbuf_tensor("r2", [128, 512], F32))
            t1_sb = Eb(nc.sbuf_tensor("t1", [128, 512], F32))
            t2_sb = Eb(nc.sbuf_tensor("t2", [128, 512], F32))
            oout_sb = Eb(nc.sbuf_tensor("oout", [128, 1024], BF16))
            stg_sb = Eb(nc.sbuf_tensor("stg", [128, 2 * D], BF16))
            psA = Eb(nc.psum_tensor("psA", [128, 1024], F32))
            psB = Eb(nc.psum_tensor("psB", [128, 1024], F32))
            pO0 = Eb(nc.psum_tensor("pO0", [128, 512], F32))
            pO1 = Eb(nc.psum_tensor("pO1", [128, 512], F32))
            pS0 = Eb(nc.psum_tensor("pS0", [128, 512], F32))
            pS1 = Eb(nc.psum_tensor("pS1", [128, 512], F32))
            blk = blkM
            sbank = [psA, psB]
            projb = [psA, psB]

            sS_after = {}
            _cnt = 0
            for _grp in range(H * QT):
                for _c in range(NC_):
                    _cnt += 2
                    sS_after[(_grp, _c)] = _cnt

            @blk.gpsimd
            def _(g):
                xtsem = [sXT, sXT2, sXT3, sXT4]
                xtv = xT_sb[:].rearrange("p (c t) -> p c t", c=NC_)
                xdv = xT_d.rearrange("(c p) t -> p c t", p=128)
                g.dma_start(out=wq_sb[:, 0:4096], in_=wqkv_d[0, :, 0:4096]).then_inc(sWQ, 16)
                g.dma_start(out=xtv[:, 0:4, :], in_=xdv[:, 0:4, :]).then_inc(sXT, 16)
                g.wait_ge(sWQ, 16)
                g.dma_start(out=wq_sb[:, 4096:10240], in_=wqkv_d[0, :, 4096:10240]).then_inc(sWQ, 16)
                g.dma_start(out=xtv[:, 4:8, :], in_=xdv[:, 4:8, :]).then_inc(sXT2, 16)
                g.dma_start(out=xtv[:, 8:12, :], in_=xdv[:, 8:12, :]).then_inc(sXT3, 16)
                g.dma_start(out=xtv[:, 12:16, :], in_=xdv[:, 12:16, :]).then_inc(sXT4, 16)
                for h in range(1, H):
                    if h >= 2:
                        g.wait_ge(sPM, (h - 1) * 32)
                    g.wait_ge(sWQ, 16 * (h + 1))
                    g.dma_start(out=wq_sb[:, (h % 2) * 10240:(h % 2) * 10240 + 10240],
                                in_=wqkv_d[h]).then_inc(sWQ, 16)
                    if h == 1:
                        for i in range(2 * NC_):
                            src_ap = (wo_d[(i % NC_) * 128:(i % NC_) * 128 + 128, :]
                                      if i < NC_ else
                                      wffn_d[(i % NC_) * 128:(i % NC_) * 128 + 128, :])
                            g.wait_ge(sCV, 16 * i)
                            if i >= 2:
                                g.wait_ge(sCS, 16 * (i - 1))
                            g.dma_start(
                                out=stg_sb[:, (i % 2) * D:(i % 2) * D + D], in_=src_ap
                            ).then_inc(sCV, 16)

            @blk.sync
            def _(sy):
                sy.dma_start(out=lam_sb[:], in_=lam_d).then_inc(sAUX, 16)
                for i in range(2 * NC_):
                    dst = wob_d[i % NC_] if i < NC_ else wfb_d[i % NC_]
                    sy.wait_ge(sCV, 16 * (i + 1))
                    sy.wait_ge(sCS, 16 * i)
                    sy.dma_start(out=dst, in_=stg_sb[:, (i % 2) * D:(i % 2) * D + D]).then_inc(sCS, 16)
                for grp in range(61):
                    sy.wait_ge(sCB, grp + 1)
                    h, qt = grp // QT, grp % QT
                    sy.wait_ge(sODMA, 16 * grp)
                    sy.dma_start(
                        out=osc_d[qt * 4:qt * 4 + 4, h].rearrange("j p t -> p j t"),
                        in_=oout_sb[:, (grp % 2) * 512:(grp % 2) * 512 + 512].rearrange(
                            "p (j t) -> p j t", j=4),
                    ).then_inc(sODMA, 16)

            @blk.tensor
            def _(te):
                for h in range(H):
                    te.wait_ge(sWQ, 16 if h == 0 else 16 * (h + 2))
                    if h > 0:
                        te.wait_ge(sEXP, 64 * h)

                    def wslice(m, c):
                        base = (h % 2) * 10240 + (m * 16 + c) * 128
                        return wq_sb[:, base:base + 128]

                    def pdval(g):
                        return g + 1 if g < 16 else g + 2

                    g = 0
                    for tt in range(NT):
                        if h > 0 or g >= 2:
                            te.wait_ge(sPD, h * 33 + pdval(g - 2))
                        for c in range(NC_):
                            if h == 0:
                                te.wait_ge((sXT, sXT2, sXT3, sXT4)[c // 4], 16)
                            mm = te.matmul(
                                projb[g % 2][:, 0:128],
                                xT_sb[:, c * S + tt * 128: c * S + tt * 128 + 128],
                                wslice(0, c),
                                start=(c == 0), stop=(c == NC_ - 1),
                            )
                        mm.then_inc(sPM, 1)
                        g += 1
                    if h == 0:
                        te.wait_ge(sWQ, 32)
                        for _s in (sXT2, sXT3, sXT4):
                            te.wait_ge(_s, 16)
                    for m, _dst in enumerate((q1_sb, k1_sb, q2_sb, k2_sb)):
                        for qt in range(QT):
                            te.wait_ge(sPD, h * 33 + pdval(g - 2))
                            for c in range(NC_):
                                mm = te.matmul(
                                    projb[g % 2][:, 0:512],
                                    wslice(m + 1, c),
                                    xT_sb[:, c * S + qt * 512: c * S + qt * 512 + 512],
                                    start=(c == 0), stop=(c == NC_ - 1),
                                )
                            mm.then_inc(sPM, 1)
                            g += 1

                    def s_issue(k):
                        qt_, c = k // NC_, k % NC_
                        s = (h * QT + qt_) * NC_ + c
                        if k < 2:
                            te.wait_ge(sPD, h * 33 + 32 + k)
                        if s >= 2:
                            te.wait_ge(sEXP, s - 1)
                        for j, (kk, qq) in enumerate(((k1_sb, q1_sb), (k2_sb, q2_sb))):
                            te.matmul(
                                sbank[c % 2][:, j * 512:(j + 1) * 512],
                                kk[:, c * 128:(c + 1) * 128],
                                qq[:, qt_ * 512:(qt_ + 1) * 512],
                                start=True, stop=True,
                            ).then_inc(sS, 1)

                    def pv_issue(k):
                        qt_, cc = k // NC_, k % NC_
                        grp = h * QT + qt_
                        s = grp * NC_ + cc
                        te.wait_ge(sEXP, s + 1)
                        if cc == 0 and grp > 0:
                            te.wait_ge(sOC, grp)
                            te.wait_ge(sOC2, grp)
                        b = (s % 3) * 1024
                        p0 = pb_sb[:, b:b + 512]
                        p1 = pb_sb[:, b + 512:b + 1024]
                        if cc < NC_ - 1:
                            te.matmul(pO0[:], v_sb[:, cc * 128:(cc + 1) * 128], p0, start=(cc == 0), stop=False)
                            te.matmul(pO1[:], v2_sb[:, cc * 128:(cc + 1) * 128], p1, start=(cc == 0), stop=False)
                            te.matmul(pS0[:], ones_sb[:], p0, start=(cc == 0), stop=False)
                            te.matmul(pS1[:], ones_sb[:], p1, start=(cc == 0),
                                      stop=False).then_inc(sPVD, 1)
                        else:
                            te.matmul(pO0[:], v_sb[:, cc * 128:(cc + 1) * 128], p0,
                                      start=False, stop=True).then_inc(sPO0, 1)
                            te.matmul(pS0[:], ones_sb[:], p0,
                                      start=False, stop=True).then_inc(sPS0, 1)
                            te.matmul(pO1[:], v2_sb[:, cc * 128:(cc + 1) * 128], p1,
                                      start=False, stop=True).then_inc(sPO1, 1)
                            te.matmul(pS1[:], ones_sb[:], p1,
                                      start=False, stop=True).then_inc(sPVD, 1)

                    for k in range(QT * NC_):
                        s_issue(k)
                        if k >= 2:
                            pv_issue(k - 2)
                    pv_issue(QT * NC_ - 2)
                    pv_issue(QT * NC_ - 1)

            @blk.scalar
            def _(sc):
                for grp in range(H * QT):
                    for c in range(NC_):
                        s = grp * NC_ + c
                        sc.wait_ge(sS, sS_after[(grp, c)])
                        if s >= 3:
                            sc.wait_ge(sPVD, s - 2)
                        sc.activation(
                            pb_sb[:, (s % 3) * 1024:(s % 3) * 1024 + 1024],
                            sbank[c % 2][:], AF.Exp,
                            scale=INV_SQRT_HD,
                        ).then_inc(sEXP, 1)
                    sc.wait_ge(sPO0, grp + 1)
                    if grp >= 3:
                        sc.wait_ge(sCB, grp - 2)
                    sc.copy(oc_sb[:, (grp % 3) * 1024:(grp % 3) * 1024 + 512], pO0[:])
                    sc.wait_ge(sPO1, grp + 1)
                    sc.copy(oc_sb[:, (grp % 3) * 1024 + 512:(grp % 3) * 1024 + 1024],
                            pO1[:]).then_inc(sOC2, 1)


            vchain = [None, 0]

            @blk.vector
            def _(ve):
                ve = ChainedEng(ve, sVC, vchain[0])
                ve._n = vchain[1]

                def divide(g2):
                    b2 = (g2 % 3) * 1024
                    ve.wait_ge(sOC2, g2 + 1)
                    if g2 >= 2:
                        ve.wait_ge(sODMA, 16 * min(g2 - 1, 61))
                    if g2 >= 63:
                        ve.wait_ge(sODM2, 16 * (g2 - 62))
                    ve.reciprocal(r1_sb[:], sc_sb[:, b2:b2 + 512])
                    ve.reciprocal(r2_sb[:], sc_sb[:, b2 + 512:b2 + 1024])
                    ve.tensor_mul(t1_sb[:], oc_sb[:, b2:b2 + 512], r1_sb[:])
                    ve.tensor_mul(t2_sb[:], oc_sb[:, b2 + 512:b2 + 1024], r2_sb[:])
                    ve.tensor_add(oout_sb[:, (g2 % 2) * 512:(g2 % 2) * 512 + 512], t1_sb[:], t2_sb[:], inc=(sCB, 1, g2 + 1))

                ve.wait_ge(sAUX, 16)
                ve.tensor_scalar(neglam_sb[:], lam_sb[:], -1.0, None, OP.mult)
                for h in range(H):
                    g = 0
                    pd = 0
                    for tt in range(NT):
                        ve.wait_ge(sPM, h * 32 + g + 1)
                        pd += 1
                        ve.tensor_copy(
                            v_sb[:, tt * 128:(tt + 1) * 128], projb[g % 2][:, 0:128],
                            inc=(sPD, 1, h * 33 + pd),
                        )
                        g += 1
                    pd += 1
                    ve.tensor_scalar(
                        v2_sb[:], v_sb[:], neglam_sb[:, h:h + 1], None, OP.mult,
                        inc=(sPD, 1, h * 33 + pd),
                    )
                    for m, dst in enumerate((q1_sb, k1_sb, q2_sb, k2_sb)):
                        for qt in range(QT):
                            ve.wait_ge(sPM, h * 32 + g + 1)
                            pd += 1
                            ve.tensor_copy(
                                dst[:, qt * 512:(qt + 1) * 512], projb[g % 2][:, 0:512],
                                inc=(sPD, 1, h * 33 + pd),
                            )
                            g += 1
                    for qt in range(QT):
                        grp = h * QT + qt
                        if grp >= 2:
                            divide(grp - 2)
                        ve.wait_ge(sPS0, grp + 1)
                        if grp >= 3:
                            ve.wait_ge(sCB, grp - 2)
                        b = (grp % 3) * 1024
                        ve.tensor_copy(sc_sb[:, b:b + 512], pS0[:])
                        ve.wait_ge(sPVD, NC_ * (grp + 1))
                        ve.tensor_copy(sc_sb[:, b + 512:b + 1024], pS1[:], inc=(sOC, 1, grp + 1))
                divide(H * QT - 2)
                divide(H * QT - 1)
                vchain[0] = ve.last
                vchain[1] = ve._n

        with nc.sbuf_tensor("qT", [128, NC_ * S], BF16, side="right") as qT_sb:
            with ExitStack() as phc:
                Ec = phc.enter_context
                wo_sb = Ec(nc.sbuf_tensor("woS", [128, NC_ * D], BF16))
                ocol_sb = Ec(nc.sbuf_tensor("ocol", [128, 2 * H * 128], BF16))
                xin_sb = Ec(nc.sbuf_tensor("xin", [128, 2 * D], F32))
                z_sb = Ec(nc.sbuf_tensor("zz", [128, 2 * D], F32))
                y_sb = Ec(nc.sbuf_tensor("yy", [128, D], F32))
                t_sb = Ec(nc.sbuf_tensor("tt_", [128, D], F32))
                dump_sb = Ec(nc.sbuf_tensor("dump", [128, D], F32))
                q8_sb = Ec(nc.sbuf_tensor("qi8", [128, D], I8))
                qb_sb = Ec(nc.sbuf_tensor("qbf", [128, D], BF16))
                yf16_sb = Ec(nc.sbuf_tensor("yf16", [128, 2 * D], F16))
                st1_sb = Ec(nc.sbuf_tensor("st1", [128, 2], F32))
                st2_sb = Ec(nc.sbuf_tensor("st2", [128, 2], F32))
                st3_sb = Ec(nc.sbuf_tensor("st3", [128, 1], F32))
                st4_sb = Ec(nc.sbuf_tensor("st4", [128, 1], F32))
                st5_sb = Ec(nc.sbuf_tensor("st5", [128, 1], F32))
                st6_sb = Ec(nc.sbuf_tensor("st6", [128, 1], F32))
                pattnL = Ec(nc.psum_tensor("pattnL", [128, 1024], F32))
                pattnH = Ec(nc.psum_tensor("pattnH", [128, 1024], F32))
                ptrA = Ec(nc.psum_tensor("ptrA", [128, 1024], BF16))
                ptrB = Ec(nc.psum_tensor("ptrB", [128, 1024], BF16))
                pattnC = Ec(nc.psum_tensor("pattnC", [128, 1024], F32))
                blk2 = blkM
                path = [pattnL, pattnH, pattnC]
                trb = [ptrA, ptrB]
                wosem = [sWO, sWO2, sWO3, sWO4]

                @blk2.sync
                def _(sy):
                    sy.wait_ge(sCS, 32 * NC_)
                    sy.wait_ge(sWQ, 16 * H)
                    sy.wait_ge(sPM, 32 * H)

                    def ywrite(tt):
                        sy.wait_ge(sYD, tt + 1)
                        sy.wait_ge(sYDMA, 16 * tt)
                        sy.dma_start(out=ysc_d[tt], in_=yf16_sb[:, (tt % 2) * D:(tt % 2) * D + D]).then_inc(sYDMA, 16)

                    for tt in range(NT):
                        if tt >= 2:
                            sy.wait_ge(sAR, 2 * (tt - 1))
                        sy.wait_ge(sODMA, 16 * 61)
                        if tt >= 4:
                            sy.wait_ge(sODM2, 16 * (tt // 4))
                        sy.wait_ge(sOI, 16 * tt)
                        sy.dma_start(
                            out=ocol_sb[:, (tt % 2) * 2048:(tt % 2) * 2048 + 2048].rearrange("p (h t) -> p h t", h=H),
                            in_=osc_d[tt].rearrange("h p t -> p h t"),
                        ).then_inc(sOI, 16)
                        if tt >= 2:
                            sy.wait_ge(sZD, 2 * (tt - 1))
                        sy.wait_ge(sXI, 16 * tt)
                        sy.dma_start(
                            out=xin_sb[:, (tt % 2) * D:(tt % 2) * D + D], in_=x_d[tt * 128:(tt + 1) * 128, :]
                        ).then_inc(sXI, 16)
                        if tt >= 2:
                            ywrite(tt - 2)
                    ywrite(NT - 2)
                    ywrite(NT - 1)

                @blk2.gpsimd
                def _(g):
                    g.wait_ge(sPM, 32 * H)
                    g.wait_ge(sCS, 16 * NC_)
                    for c in range(NC_):
                        sem = wosem[c % 4]
                        g.wait_ge(sem, 16 * (c // 4))
                        g.dma_start(
                            out=wo_sb[:, c * D:(c + 1) * D],
                            in_=wob_d[c],
                        ).then_inc(sem, 16)
                    for grp in range(61, H * QT):
                        g.wait_ge(sCB, grp + 1)
                        h_, qt_ = grp // QT, grp % QT
                        if grp > 61:
                            g.wait_ge(sODM2, 16 * (grp - 61))
                        g.dma_start(
                            out=osc_d[qt_ * 4:qt_ * 4 + 4, h_].rearrange("j p t -> p j t"),
                            in_=oout_sb[:, (grp % 2) * 512:(grp % 2) * 512 + 512].rearrange(
                                "p (j t) -> p j t", j=4),
                        ).then_inc(sODM2, 16)

                @blk2.tensor
                def _(te):
                    te.wait_ge(sEXP, 16 * H * QT - 1)

                    def transposes(tt):
                        if tt == 0:
                            te.wait_ge(sOC2, H * QT)
                        te.wait_ge(sQ, tt + 1)
                        for grp8 in range(2):
                            gg = 2 * tt + grp8
                            if gg >= 2:
                                te.wait_ge(sTRE, gg - 1)
                            for j8 in range(8):
                                j = grp8 * 8 + j8
                                mm = te.transpose(
                                    trb[gg % 2][:, j8 * 128:(j8 + 1) * 128],
                                    qb_sb[:, j * 128:(j + 1) * 128],
                                    id_sb[:],
                                )
                            mm.then_inc(sTR, 1)

                    for tt in range(NT):
                        for half in range(2):
                            s_ = 2 * tt + half
                            if s_ >= 3:
                                te.wait_ge(sZD, s_ - 2)
                            if tt == 0 and half == 1:
                                te.wait_ge(sEXP, 16 * H * QT)
                            if s_ == 2:
                                te.wait_ge(sOC, H * QT)
                            for c in range(NC_):
                                if half == 0 and c == 0:
                                    te.wait_ge(sOI, 16 * (tt + 1))
                                if tt == 0 and half == 0:
                                    te.wait_ge(wosem[c % 4], 16 * (c // 4 + 1))
                                for nt in (half * 2, half * 2 + 1):
                                    mm = te.matmul(
                                        path[s_ % 3][:, (nt - half * 2) * 512:(nt - half * 2) * 512 + 512],
                                        ocol_sb[:, (tt % 2) * 2048 + c * 128:(tt % 2) * 2048 + (c + 1) * 128],
                                        wo_sb[:, c * D + nt * 512: c * D + nt * 512 + 512],
                                        start=(c == 0), stop=(c == NC_ - 1),
                                    )
                                if tt == NT - 1 and half == 1 and c < NC_ - 1:
                                    mm.then_inc(sWFC, 1)
                            mm.then_inc(sAR, 1)
                        if tt >= 2:
                            transposes(tt - 2)
                    transposes(NT - 2)
                    transposes(NT - 1)

                @blk2.scalar
                def _(sc):
                    sc.wait_ge(sCS, 32 * NC_)
                    sc.wait_ge(sODMA, 16 * 61)
                    sc.wait_ge(sODM2, 48)

                    def evacs(tt):
                        for grp8 in range(2):
                            gg = 2 * tt + grp8
                            sc.wait_ge(sTR, gg + 1)
                            out_ap = qT_sb[:].rearrange("p (j r) -> p j r", j=NC_)[
                                :, grp8 * 8:(grp8 + 1) * 8, tt * 128:(tt + 1) * 128]
                            sc.copy(
                                out_ap,
                                trb[gg % 2][:].rearrange("p (j r) -> p j r", j=8),
                            ).then_inc(sTRE, 1)

                    for tt in range(NT):
                        if tt >= 1:
                            sc.wait_ge(sVST, tt)
                            sc.activation(st5_sb[:], st6_sb[:], AF.Sqrt).then_inc(sSQT, 1)
                        sc.wait_ge(sZD, 2 * (tt + 1))
                        if tt >= 2:
                            sc.wait_ge(sVST, tt - 1)
                        if tt >= 1:
                            sc.wait_ge(sST, 2 * tt)
                        sc.activation(dump_sb[:], z_sb[:, (tt % 2) * D:(tt % 2) * D + D],
                                      AF.Copy, accum_out=st1_sb[:, tt % 2:tt % 2 + 1]).then_inc(sST, 1)
                        sc.wait_ge(sST, 2 * tt + 1)
                        sc.activation(dump_sb[:], z_sb[:, (tt % 2) * D:(tt % 2) * D + D],
                                      AF.Square, accum_out=st2_sb[:, tt % 2:tt % 2 + 1]).then_inc(sST, 1)
                        if tt >= 2:
                            evacs(tt - 2)
                    sc.wait_ge(sVST, NT)
                    sc.activation(st5_sb[:], st6_sb[:], AF.Sqrt).then_inc(sSQT, 1)
                    evacs(NT - 2)
                    evacs(NT - 1)

                @blk2.vector
                def _(ve):
                    ve = ChainedEng(ve, sVC, vchain[0])
                    ve._n = vchain[1]
                    ve.wait_ge(sCS, 32 * NC_)

                    def post(k):
                        ve.wait_ge(sST, 2 * (k + 1))
                        ve.tensor_scalar(st1_sb[:, k % 2:k % 2 + 1], st1_sb[:, k % 2:k % 2 + 1], INV_D, None, OP.mult)
                        ve.tensor_scalar(st2_sb[:, k % 2:k % 2 + 1], st2_sb[:, k % 2:k % 2 + 1], INV_D, None, OP.mult)
                        ve.tensor_mul(st6_sb[:], st1_sb[:, k % 2:k % 2 + 1], st1_sb[:, k % 2:k % 2 + 1])
                        ve.tensor_sub(st6_sb[:], st2_sb[:, k % 2:k % 2 + 1], st6_sb[:])
                        ve.tensor_scalar(st6_sb[:], st6_sb[:], LN_EPS, None, OP.add, inc=(sVST, 1, k + 1))
                        ve.wait_ge(sSQT, k + 1)
                        ve.reciprocal(st3_sb[:], st5_sb[:])
                        ve.tensor_mul(st2_sb[:, k % 2:k % 2 + 1], st6_sb[:], st3_sb[:])
                        ve.tensor_mul(st2_sb[:, k % 2:k % 2 + 1], st2_sb[:, k % 2:k % 2 + 1], st3_sb[:])
                        ve.tensor_scalar(st2_sb[:, k % 2:k % 2 + 1], st2_sb[:, k % 2:k % 2 + 1], -0.5, 1.5, OP.mult, OP.add)
                        ve.tensor_mul(st3_sb[:], st3_sb[:], st2_sb[:, k % 2:k % 2 + 1])
                        ve.tensor_mul(st4_sb[:], st1_sb[:, k % 2:k % 2 + 1], st3_sb[:])
                        ve.tensor_scalar(st4_sb[:], st4_sb[:], -1.0, None, OP.mult)
                        ve.tensor_scalar(y_sb[:], z_sb[:, (k % 2) * D:(k % 2) * D + D], st3_sb[:], st4_sb[:],
                                         OP.mult, OP.add)
                        ve.tensor_reduce(st5_sb[:], y_sb[:], AX, OP.max,
                                         apply_absolute_value=True)
                        ve.tensor_scalar(scale_all[:, k:k + 1], st5_sb[:], Q_EPS,
                                         INV127, OP.max, OP.mult)
                        ve.reciprocal(st6_sb[:], scale_all[:, k:k + 1])
                        ve.tensor_scalar(t_sb[:], y_sb[:], st6_sb[:], MAGIC, OP.mult, OP.add)
                        ve.tensor_scalar(q8_sb[:], t_sb[:], MAGIC, None, OP.subtract)
                        if k >= 1:
                            ve.wait_ge(sTR, 2 * k)
                        ve.tensor_copy(qb_sb[:], q8_sb[:], inc=(sQ, 1, k + 1))
                        if k >= 2:
                            ve.wait_ge(sYDMA, 16 * (k - 1))
                        ve.tensor_copy(yf16_sb[:, (k % 2) * D:(k % 2) * D + D], y_sb[:], inc=(sYD, 1, k + 1))

                    for tt in range(NT):
                        ve.wait_ge(sXI, 16 * (tt + 1))
                        if tt >= 2:
                            ve.wait_ge(sST, 2 * (tt - 1))
                        for half in range(2):
                            ve.wait_ge(sAR, 2 * tt + half + 1)
                            ve.tensor_add(
                                z_sb[:, (tt % 2) * D + half * 1024:(tt % 2) * D + half * 1024 + 1024],
                                path[(2 * tt + half) % 3][:],
                                xin_sb[:, (tt % 2) * D + half * 1024:(tt % 2) * D + half * 1024 + 1024],
                                inc=(sZD, 1, 2 * tt + half + 1))
                        if tt >= 1:
                            post(tt - 1)
                    post(NT - 1)
                    vchain[0] = ve.last
                    vchain[1] = ve._n

            with ExitStack() as phd:
                Ed = phd.enter_context
                wsb_sb = Ed(nc.sbuf_tensor("wsb_sb", [128, D], F32))
                bb_sb = Ed(nc.sbuf_tensor("bb_sb", [128, D], F32))
                wf_sb = Ed(nc.sbuf_tensor("wfsb", [128, NC_ * D], BF16))
                yin_sb = Ed(nc.sbuf_tensor("yin", [128, 2 * D], F16))
                f_sb = Ed(nc.sbuf_tensor("ff", [128, D], F32))
                f16_sb = Ed(nc.sbuf_tensor("f16", [128, D], F16))
                z2_sb = Ed(nc.sbuf_tensor("z2", [128, 2 * D], F32))
                o_sb = Ed(nc.sbuf_tensor("oo", [128, 2 * D], F32))
                su1_sb = Ed(nc.sbuf_tensor("su1", [128, 2], F32))
                su2_sb = Ed(nc.sbuf_tensor("su2", [128, 2], F32))
                su3_sb = Ed(nc.sbuf_tensor("su3", [128, 1], F32))
                su4_sb = Ed(nc.sbuf_tensor("su4", [128, 1], F32))
                su5_sb = Ed(nc.sbuf_tensor("su5", [128, 1], F32))
                su6_sb = Ed(nc.sbuf_tensor("su6", [128, 1], F32))
                pffnA = Ed(nc.psum_tensor("pffnA", [128, 2048], F32))
                pffnB = Ed(nc.psum_tensor("pffnB", [128, 2048], F32))
                blk3 = blkM
                pffn = [pffnA, pffnB]

                @blk3.gpsimd
                def _(g):
                    g.wait_ge(sWFC, 2)
                    g.dma_start(out=wsb_sb[:], in_=wsb_d).then_inc(sTQ2, 16)
                    g.wait_ge(sWFC, 4)
                    g.wait_ge(sTQ2, 16)
                    g.dma_start(out=bb_sb[:], in_=bb_d).then_inc(sTQ2, 16)
                    g.wait_ge(sCS, 32 * NC_)
                    wfsemd = [sWF, sWC, sYR, sAFF]
                    for j in range(4):
                        if j == 0:
                            g.wait_ge(sWFC, 8)
                        elif j == 1:
                            g.wait_ge(sWFC, 12)
                        else:
                            g.wait_ge(sAR, 2 * NT)
                            if j == 3:
                                g.wait_ge(sZD, 2 * NT - 2)
                        g.dma_start(
                            out=wf_sb[:, j * 4 * D:(j + 1) * 4 * D].rearrange(
                                "p (c d) -> p c d", c=4),
                            in_=wfb_d[j * 4:(j + 1) * 4].rearrange("c p d -> p c d"),
                        ).then_inc(wfsemd[j], 16)

                @blk3.sync
                def _(sy):
                    sy.wait_ge(sZD, 2 * NT)
                    sy.wait_ge(sYDMA, 16 * NT)

                    def owrite(tt):
                        sy.wait_ge(sOUTV, tt + 1)
                        sy.wait_ge(sOUT, 16 * tt)
                        sy.dma_start(
                            out=out_d[tt * 128:(tt + 1) * 128, :], in_=o_sb[:, (tt % 2) * D:(tt % 2) * D + D]
                        ).then_inc(sOUT, 16)

                    for tt in range(NT):
                        if tt >= 2:
                            sy.wait_ge(sZ2, tt - 1)
                        sy.wait_ge(sYI, 16 * tt)
                        sy.dma_start(out=yin_sb[:, (tt % 2) * D:(tt % 2) * D + D], in_=ysc_d[tt]).then_inc(sYI, 16)
                        if tt >= 2:
                            owrite(tt - 2)
                    owrite(NT - 2)
                    owrite(NT - 1)
                    sy.wait_ge(sOUT, 16 * NT)

                @blk3.scalar
                def _(sc):
                    for tt in range(NT):
                        if tt >= 1:
                            sc.wait_ge(sVS2, tt)
                            sc.activation(su5_sb[:], su6_sb[:], AF.Sqrt).then_inc(sSQ2, 1)
                        sc.wait_ge(sZ2, tt + 1)
                        if tt >= 2:
                            sc.wait_ge(sVS2, tt - 1)
                        if tt >= 1:
                            sc.wait_ge(sS2T, 2 * tt)
                        sc.activation(f_sb[:], z2_sb[:, (tt % 2) * D:(tt % 2) * D + D],
                                      AF.Copy, accum_out=su1_sb[:, tt % 2:tt % 2 + 1]).then_inc(sS2T, 1)
                        sc.wait_ge(sS2T, 2 * tt + 1)
                        sc.activation(f_sb[:], z2_sb[:, (tt % 2) * D:(tt % 2) * D + D],
                                      AF.Square, accum_out=su2_sb[:, tt % 2:tt % 2 + 1]).then_inc(sS2T, 1)
                    sc.wait_ge(sVS2, NT)
                    sc.activation(su5_sb[:], su6_sb[:], AF.Sqrt).then_inc(sSQ2, 1)

                @blk3.tensor
                def _(te):
                    te.wait_ge(sZD, 2 * NT)
                    for tt in range(NT):
                        if tt == 1:
                            te.wait_ge(sTRE, 2 * NT)
                        te.wait_ge(sTRE, 2 * tt + 2)
                        if tt >= 2:
                            te.wait_ge(sPF, tt - 1)
                        for c in range(NC_):
                            if tt == 0 and c % 4 == 0:
                                te.wait_ge((sWF, sWC, sYR, sAFF)[c // 4], 16)
                            for nt in range(4):
                                mm = te.matmul(
                                    pffn[tt % 2][:, nt * 512:(nt + 1) * 512],
                                    qT_sb[:, c * S + tt * 128: c * S + tt * 128 + 128],
                                    wf_sb[:, c * D + nt * 512: c * D + nt * 512 + 512],
                                    start=(c == 0), stop=(c == NC_ - 1),
                                )
                        mm.then_inc(sAR, 1)

                @blk3.vector
                def _(ve):
                    ve = ChainedEng(ve, sVC, vchain[0])
                    ve._n = vchain[1]
                    ve.wait_ge(sTQ2, 32)

                    def post2(k):
                        ve.wait_ge(sS2T, 2 * (k + 1))
                        ve.tensor_scalar(su1_sb[:, k % 2:k % 2 + 1], su1_sb[:, k % 2:k % 2 + 1], INV_D, None, OP.mult)
                        ve.tensor_scalar(su2_sb[:, k % 2:k % 2 + 1], su2_sb[:, k % 2:k % 2 + 1], INV_D, None, OP.mult)
                        ve.tensor_mul(su6_sb[:], su1_sb[:, k % 2:k % 2 + 1], su1_sb[:, k % 2:k % 2 + 1])
                        ve.tensor_sub(su6_sb[:], su2_sb[:, k % 2:k % 2 + 1], su6_sb[:])
                        ve.tensor_scalar(su6_sb[:], su6_sb[:], LN_EPS, None, OP.add, inc=(sVS2, 1, k + 1))
                        ve.wait_ge(sSQ2, k + 1)
                        ve.reciprocal(su3_sb[:], su5_sb[:])
                        ve.tensor_mul(su2_sb[:, k % 2:k % 2 + 1], su6_sb[:], su3_sb[:])
                        ve.tensor_mul(su2_sb[:, k % 2:k % 2 + 1], su2_sb[:, k % 2:k % 2 + 1], su3_sb[:])
                        ve.tensor_scalar(su2_sb[:, k % 2:k % 2 + 1], su2_sb[:, k % 2:k % 2 + 1], -0.5, 1.5, OP.mult, OP.add)
                        ve.tensor_mul(su3_sb[:], su3_sb[:], su2_sb[:, k % 2:k % 2 + 1])
                        ve.tensor_mul(su4_sb[:], su1_sb[:, k % 2:k % 2 + 1], su3_sb[:])
                        ve.tensor_scalar(su4_sb[:], su4_sb[:], -1.0, None, OP.mult)
                        if k >= 2:
                            ve.wait_ge(sOUT, 16 * (k - 1))
                        ve.tensor_scalar(o_sb[:, (k % 2) * D:(k % 2) * D + D], z2_sb[:, (k % 2) * D:(k % 2) * D + D],
                                         su3_sb[:], su4_sb[:], OP.mult, OP.add,
                                         inc=(sOUTV, 1, k + 1))

                    for tt in range(NT):
                        ve.wait_ge(sAR, 2 * NT + tt + 1)
                        ve.wait_ge(sYI, 16 * (tt + 1))
                        if tt >= 1:
                            ve.wait_ge(sS2T, 2 * tt)
                        if tt >= 2:
                            ve.wait_ge(sOUTV, tt - 1)
                        ve.scalar_tensor_tensor(
                            f_sb[:], pffn[tt % 2][:], scale_all[:, tt:tt + 1], wsb_sb[:],
                            OP.mult, OP.mult, inc=(sPF, 1, tt + 1),
                        )
                        ve.tensor_add(f16_sb[:], f_sb[:], bb_sb[:])
                        ve.tensor_add(z2_sb[:, (tt % 2) * D:(tt % 2) * D + D], f16_sb[:], yin_sb[:, (tt % 2) * D:(tt % 2) * D + D], inc=(sZ2, 1, tt + 1))
                        if tt >= 1:
                            post2(tt - 1)
                    post2(NT - 1)

    return nc
    return nc


_NC_CACHE = None


def kernel(L0_attn_x, L0_attn_q1_w, L0_attn_k1_w, L0_attn_q2_w, L0_attn_k2_w,
           L0_attn_v_w, L0_attn_o_w, L0_attn_lambda_param,
           L0_ffn_weight, L0_ffn_w_scale, L0_ffn_bias):
    global _NC_CACHE
    B = L0_attn_x.shape[0]

    def head_major(w):
        wt = np.ascontiguousarray(np.asarray(w).T)
        return wt.reshape(16, 128, 16, 128).transpose(2, 1, 0, 3)

    parts = [head_major(w) for w in (L0_attn_v_w, L0_attn_q1_w, L0_attn_k1_w,
                                     L0_attn_q2_w, L0_attn_k2_w)]
    wqkv = np.ascontiguousarray(np.stack(parts, axis=2))
    wqkv = wqkv.reshape(16, 128, 5 * 16 * 128)

    wo = np.ascontiguousarray(np.asarray(L0_attn_o_w).T)
    wffn = np.ascontiguousarray(np.asarray(L0_ffn_weight).T)
    wsb = np.ascontiguousarray(np.broadcast_to(
        np.asarray(L0_ffn_w_scale)[:, 0][None, :], (128, 2048)))
    bb = np.ascontiguousarray(np.broadcast_to(
        np.asarray(L0_ffn_bias)[None, :], (128, 2048)))
    lam = np.ascontiguousarray(np.broadcast_to(
        np.asarray(L0_attn_lambda_param)[None, :], (128, 16)))

    if _NC_CACHE is None:
        _NC_CACHE = build_program()
    nc = _NC_CACHE

    in_maps = []
    for b in range(B):
        xb = np.ascontiguousarray(np.asarray(L0_attn_x[b]))
        in_maps.append({
            "xT": np.ascontiguousarray(xb.T),
            "x": xb,
            "wqkv": wqkv, "wo": wo, "wffn": wffn,
            "wsb": wsb, "bb": bb, "lam": lam,
        })
    res = run_bass_kernel_spmd(nc, in_maps, core_ids=list(range(8)))
    out = np.stack([np.asarray(r["out"]) for r in res.results], axis=0)
    return out.astype(np.float32)


# version 29 (final)
# speedup vs baseline: 1.2705x; 1.0116x over previous
from contextlib import ExitStack

import numpy as np

import concourse.bass as bass
import concourse.mybir as mybir
from concourse.bass_utils import run_bass_kernel_spmd

F32 = mybir.dt.float32
BF16 = mybir.dt.bfloat16
F16 = mybir.dt.float16
I8 = mybir.dt.int8
AX = mybir.AxisListType.X
OP = mybir.AluOpType
AF = mybir.ActivationFunctionType

D = 2048
S = 2048
H = 16
NC_ = 16
NT = 16
QT = 4
INV_SQRT_HD = 1.0 / np.sqrt(128.0)
MAGIC = 12582912.0
LN_EPS = 1e-5
Q_EPS = 1e-7
INV_D = 1.0 / 2048.0
INV127 = 1.0 / 127.0


class ChainedEng:

    def __init__(self, eng, sem, last=None):
        self._e, self._s = eng, sem
        self.last = last

    def __getattr__(self, name):
        f = getattr(self._e, name)

        def g(*a, inc=None, **k):
            if self.last is not None:
                self._e.wait_ge(self.last[0], self.last[1])
            inst = f(*a, **k)
            if inc is None:
                if self.last is not None and self.last[0] is self._s:
                    nxt = self.last[1] + 1
                else:
                    nxt = self._n + 1
                inst.then_inc(self._s, 1)
                self._n = nxt
                self.last = (self._s, nxt)
            else:
                sem, kk, after = inc
                inst.then_inc(sem, kk)
                self.last = (sem, after)
            return inst

        return g

    @property
    def _n(self):
        return self.__dict__.setdefault("_nv", 0)

    @_n.setter
    def _n(self, v):
        self.__dict__["_nv"] = v

    def wait_ge(self, sem, v):
        return self._e.wait_ge(sem, v)


def build_program():
    nc = bass.Bass("TRN2", target_bir_lowering=False)

    xT_d = nc.dram_tensor("xT", [D, S], F32, kind="ExternalInput").ap()
    x_d = nc.dram_tensor("x", [S, D], F32, kind="ExternalInput").ap()
    wqkv_d = nc.dram_tensor("wqkv", [H, 128, 5 * 16 * 128], F32, kind="ExternalInput").ap()
    wo_d = nc.dram_tensor("wo", [D, D], F32, kind="ExternalInput").ap()
    wffn_d = nc.dram_tensor("wffn", [D, D], I8, kind="ExternalInput").ap()
    wsb_d = nc.dram_tensor("wsb", [128, D], F32, kind="ExternalInput").ap()
    bb_d = nc.dram_tensor("bb", [128, D], F32, kind="ExternalInput").ap()
    lam_d = nc.dram_tensor("lam", [128, H], F32, kind="ExternalInput").ap()
    out_d = nc.dram_tensor("out", [S, D], F32, kind="ExternalOutput").ap()
    osc_d = nc.dram_tensor("osc", [NT, H, 128, 128], BF16).ap()
    wob_d = nc.dram_tensor("wob", [NC_, 128, D], BF16).ap()
    wfb_d = nc.dram_tensor("wfb", [NC_, 128, D], BF16).ap()
    ysc_d = nc.dram_tensor("ysc", [NT, 128, D], F16).ap()

    with ExitStack() as top:
        E = top.enter_context
        ones_sb = E(nc.sbuf_tensor("ones", [128, 128], BF16))
        id_sb = E(nc.sbuf_tensor("ident", [128, 128], BF16))
        lam_sb = E(nc.sbuf_tensor("lam_sb", [128, H], F32))
        neglam_sb = E(nc.sbuf_tensor("neglam", [128, H], F32))
        scale_all = E(nc.sbuf_tensor("scale_all", [128, NT], F32))
        (sWQ, sXT, sPM, sPD, sS, sEXP, sPVD, sCB, sODMA, sAUX,
         sOI, sXI, sWO, sAR, sZD, sQ, sTR, sTRE, sYD, sYDMA,
         sVST, sSQT, sWF, sWC, sYI, sZ2, sPF, sOUTV, sOUT, sVS2, sSQ2, sID, sVC, sOC,
         sST, sYR, sAFF, sS2T, sAF2, sWO2, sWO3, sWO4, sTQ2, sTQ3, sOC2, sCV, sCS, sXT2, sXT3, sXT4, sWFC, sODM2, sPO0, sPO1, sPS0) = [
            E(nc.semaphore(n)) for n in (
                "sWQ", "sXT", "sPM", "sPD", "sS", "sEXP", "sPVD", "sCB",
                "sODMA", "sAUX", "sOI", "sXI", "sWO", "sAR", "sZD", "sQ",
                "sTR", "sTRE", "sYD", "sYDMA", "sVST", "sSQT", "sWF", "sWC",
                "sYI", "sZ2", "sPF", "sOUTV", "sOUT", "sVS2", "sSQ2", "sID", "sVC", "sOC",
                "sST", "sYR", "sAFF", "sS2T", "sAF2", "sWO2", "sWO3", "sWO4", "sTQ2", "sTQ3", "sOC2", "sCV", "sCS", "sXT2", "sXT3", "sXT4", "sWFC", "sODM2", "sPO0", "sPO1", "sPS0")]
        with nc.Block() as blk0:
            @blk0.gpsimd
            def _(g):
                g.memset(ones_sb[:], 1.0)
                g.memset(id_sb[:], 0.0).then_inc(sID, 1)
                g.wait_ge(sID, 1)
                g.affine_select(
                    out=id_sb[:], in_=id_sb[:],
                    compare_op=OP.not_equal, fill=1.0,
                    base=0, pattern=[[-1, 128]], channel_multiplier=1,
                )

        blkM = E(nc.Block())
        with ExitStack() as phb:
            Eb = phb.enter_context
            xT_sb = Eb(nc.sbuf_tensor("xTb", [128, NC_ * S], BF16))
            wq_sb = Eb(nc.sbuf_tensor("wq", [128, 2 * 5 * 16 * 128], BF16))
            q1_sb = Eb(nc.sbuf_tensor("q1", [128, S], BF16))
            k1_sb = Eb(nc.sbuf_tensor("k1", [128, S], BF16))
            q2_sb = Eb(nc.sbuf_tensor("q2", [128, S], BF16))
            k2_sb = Eb(nc.sbuf_tensor("k2", [128, S], BF16))
            v_sb = Eb(nc.sbuf_tensor("vv", [128, S], BF16))
            v2_sb = Eb(nc.sbuf_tensor("v2", [128, S], BF16))
            pb_sb = Eb(nc.sbuf_tensor("pb", [128, 3072], BF16))
            oc_sb = Eb(nc.sbuf_tensor("oc", [128, 3072], F32))
            sc_sb = Eb(nc.sbuf_tensor("sc_", [128, 3072], F32))
            r1_sb = Eb(nc.sbuf_tensor("r1", [128, 512], F32))
            r2_sb = Eb(nc.sbuf_tensor("r2", [128, 512], F32))
            t1_sb = Eb(nc.sbuf_tensor("t1", [128, 512], F32))
            t2_sb = Eb(nc.sbuf_tensor("t2", [128, 512], F32))
            oout_sb = Eb(nc.sbuf_tensor("oout", [128, 1024], BF16))
            stg_sb = Eb(nc.sbuf_tensor("stg", [128, 2 * D], BF16))
            psA = Eb(nc.psum_tensor("psA", [128, 1024], F32))
            psB = Eb(nc.psum_tensor("psB", [128, 1024], F32))
            pO0 = Eb(nc.psum_tensor("pO0", [128, 512], F32))
            pO1 = Eb(nc.psum_tensor("pO1", [128, 512], F32))
            pS0 = Eb(nc.psum_tensor("pS0", [128, 512], F32))
            pS1 = Eb(nc.psum_tensor("pS1", [128, 512], F32))
            blk = blkM
            sbank = [psA, psB]
            projb = [psA, psB]

            sS_after = {}
            _cnt = 0
            for _grp in range(H * QT):
                for _c in range(NC_):
                    _cnt += 2
                    sS_after[(_grp, _c)] = _cnt

            @blk.gpsimd
            def _(g):
                xtsem = [sXT, sXT2, sXT3, sXT4]
                xtv = xT_sb[:].rearrange("p (c t) -> p c t", c=NC_)
                xdv = xT_d.rearrange("(c p) t -> p c t", p=128)
                g.dma_start(out=wq_sb[:, 0:4096], in_=wqkv_d[0, :, 0:4096]).then_inc(sWQ, 16)
                g.dma_start(out=xtv[:, 0:4, :], in_=xdv[:, 0:4, :]).then_inc(sXT, 16)
                g.dma_start(out=xtv[:, 4:8, :], in_=xdv[:, 4:8, :]).then_inc(sXT2, 16)
                g.wait_ge(sWQ, 16)
                g.dma_start(out=wq_sb[:, 4096:10240], in_=wqkv_d[0, :, 4096:10240]).then_inc(sWQ, 16)
                g.dma_start(out=xtv[:, 8:12, :], in_=xdv[:, 8:12, :]).then_inc(sXT3, 16)
                g.dma_start(out=xtv[:, 12:16, :], in_=xdv[:, 12:16, :]).then_inc(sXT4, 16)
                for h in range(1, H):
                    if h >= 2:
                        g.wait_ge(sPM, (h - 1) * 32)
                    g.wait_ge(sWQ, 16 * (h + 1))
                    g.dma_start(out=wq_sb[:, (h % 2) * 10240:(h % 2) * 10240 + 10240],
                                in_=wqkv_d[h]).then_inc(sWQ, 16)
                    if h == 1:
                        for i in range(2 * NC_):
                            src_ap = (wo_d[(i % NC_) * 128:(i % NC_) * 128 + 128, :]
                                      if i < NC_ else
                                      wffn_d[(i % NC_) * 128:(i % NC_) * 128 + 128, :])
                            g.wait_ge(sCV, 16 * i)
                            if i >= 2:
                                g.wait_ge(sCS, 16 * (i - 1))
                            g.dma_start(
                                out=stg_sb[:, (i % 2) * D:(i % 2) * D + D], in_=src_ap
                            ).then_inc(sCV, 16)

            @blk.sync
            def _(sy):
                sy.dma_start(out=lam_sb[:], in_=lam_d).then_inc(sAUX, 16)
                for i in range(2 * NC_):
                    dst = wob_d[i % NC_] if i < NC_ else wfb_d[i % NC_]
                    sy.wait_ge(sCV, 16 * (i + 1))
                    sy.wait_ge(sCS, 16 * i)
                    sy.dma_start(out=dst, in_=stg_sb[:, (i % 2) * D:(i % 2) * D + D]).then_inc(sCS, 16)
                for grp in range(61):
                    sy.wait_ge(sCB, grp + 1)
                    h, qt = grp // QT, grp % QT
                    sy.wait_ge(sODMA, 16 * grp)
                    sy.dma_start(
                        out=osc_d[qt * 4:qt * 4 + 4, h].rearrange("j p t -> p j t"),
                        in_=oout_sb[:, (grp % 2) * 512:(grp % 2) * 512 + 512].rearrange(
                            "p (j t) -> p j t", j=4),
                    ).then_inc(sODMA, 16)

            @blk.tensor
            def _(te):
                for _s in (sXT, sXT2, sXT3, sXT4):
                    te.wait_ge(_s, 16)
                for h in range(H):
                    te.wait_ge(sWQ, 16 * (h + 2))
                    if h > 0:
                        te.wait_ge(sEXP, 64 * h)

                    def wslice(m, c):
                        base = (h % 2) * 10240 + (m * 16 + c) * 128
                        return wq_sb[:, base:base + 128]

                    def pdval(g):
                        return g + 1 if g < 16 else g + 2

                    g = 0
                    for tt in range(NT):
                        if h > 0 or g >= 2:
                            te.wait_ge(sPD, h * 33 + pdval(g - 2))
                        for c in range(NC_):
                            mm = te.matmul(
                                projb[g % 2][:, 0:128],
                                xT_sb[:, c * S + tt * 128: c * S + tt * 128 + 128],
                                wslice(0, c),
                                start=(c == 0), stop=(c == NC_ - 1),
                            )
                        mm.then_inc(sPM, 1)
                        g += 1
                    for m, _dst in enumerate((q1_sb, k1_sb, q2_sb, k2_sb)):
                        for qt in range(QT):
                            te.wait_ge(sPD, h * 33 + pdval(g - 2))
                            for c in range(NC_):
                                mm = te.matmul(
                                    projb[g % 2][:, 0:512],
                                    wslice(m + 1, c),
                                    xT_sb[:, c * S + qt * 512: c * S + qt * 512 + 512],
                                    start=(c == 0), stop=(c == NC_ - 1),
                                )
                            mm.then_inc(sPM, 1)
                            g += 1

                    def s_issue(k):
                        qt_, c = k // NC_, k % NC_
                        s = (h * QT + qt_) * NC_ + c
                        if k < 2:
                            te.wait_ge(sPD, h * 33 + 32 + k)
                        if s >= 2:
                            te.wait_ge(sEXP, s - 1)
                        for j, (kk, qq) in enumerate(((k1_sb, q1_sb), (k2_sb, q2_sb))):
                            te.matmul(
                                sbank[c % 2][:, j * 512:(j + 1) * 512],
                                kk[:, c * 128:(c + 1) * 128],
                                qq[:, qt_ * 512:(qt_ + 1) * 512],
                                start=True, stop=True,
                            ).then_inc(sS, 1)

                    def pv_issue(k):
                        qt_, cc = k // NC_, k % NC_
                        grp = h * QT + qt_
                        s = grp * NC_ + cc
                        te.wait_ge(sEXP, s + 1)
                        if cc == 0 and grp > 0:
                            te.wait_ge(sOC, grp)
                            te.wait_ge(sOC2, grp)
                        b = (s % 3) * 1024
                        p0 = pb_sb[:, b:b + 512]
                        p1 = pb_sb[:, b + 512:b + 1024]
                        if cc < NC_ - 1:
                            te.matmul(pO0[:], v_sb[:, cc * 128:(cc + 1) * 128], p0, start=(cc == 0), stop=False)
                            te.matmul(pO1[:], v2_sb[:, cc * 128:(cc + 1) * 128], p1, start=(cc == 0), stop=False)
                            te.matmul(pS0[:], ones_sb[:], p0, start=(cc == 0), stop=False)
                            te.matmul(pS1[:], ones_sb[:], p1, start=(cc == 0),
                                      stop=False).then_inc(sPVD, 1)
                        else:
                            te.matmul(pO0[:], v_sb[:, cc * 128:(cc + 1) * 128], p0,
                                      start=False, stop=True).then_inc(sPO0, 1)
                            te.matmul(pS0[:], ones_sb[:], p0,
                                      start=False, stop=True).then_inc(sPS0, 1)
                            te.matmul(pO1[:], v2_sb[:, cc * 128:(cc + 1) * 128], p1,
                                      start=False, stop=True).then_inc(sPO1, 1)
                            te.matmul(pS1[:], ones_sb[:], p1,
                                      start=False, stop=True).then_inc(sPVD, 1)

                    for k in range(QT * NC_):
                        s_issue(k)
                        if k >= 2:
                            pv_issue(k - 2)
                    pv_issue(QT * NC_ - 2)
                    pv_issue(QT * NC_ - 1)

            @blk.scalar
            def _(sc):
                for grp in range(H * QT):
                    for c in range(NC_):
                        s = grp * NC_ + c
                        sc.wait_ge(sS, sS_after[(grp, c)])
                        if s >= 3:
                            sc.wait_ge(sPVD, s - 2)
                        sc.activation(
                            pb_sb[:, (s % 3) * 1024:(s % 3) * 1024 + 1024],
                            sbank[c % 2][:], AF.Exp,
                            scale=INV_SQRT_HD,
                        ).then_inc(sEXP, 1)
                    sc.wait_ge(sPO0, grp + 1)
                    if grp >= 3:
                        sc.wait_ge(sCB, grp - 2)
                    sc.copy(oc_sb[:, (grp % 3) * 1024:(grp % 3) * 1024 + 512], pO0[:])
                    sc.wait_ge(sPO1, grp + 1)
                    sc.copy(oc_sb[:, (grp % 3) * 1024 + 512:(grp % 3) * 1024 + 1024],
                            pO1[:]).then_inc(sOC2, 1)


            vchain = [None, 0]

            @blk.vector
            def _(ve):
                ve = ChainedEng(ve, sVC, vchain[0])
                ve._n = vchain[1]

                def divide(g2):
                    b2 = (g2 % 3) * 1024
                    ve.wait_ge(sOC2, g2 + 1)
                    if g2 >= 2:
                        ve.wait_ge(sODMA, 16 * min(g2 - 1, 61))
                    if g2 >= 63:
                        ve.wait_ge(sODM2, 16 * (g2 - 62))
                    ve.reciprocal(r1_sb[:], sc_sb[:, b2:b2 + 512])
                    ve.reciprocal(r2_sb[:], sc_sb[:, b2 + 512:b2 + 1024])
                    ve.tensor_mul(t1_sb[:], oc_sb[:, b2:b2 + 512], r1_sb[:])
                    ve.tensor_mul(t2_sb[:], oc_sb[:, b2 + 512:b2 + 1024], r2_sb[:])
                    ve.tensor_add(oout_sb[:, (g2 % 2) * 512:(g2 % 2) * 512 + 512], t1_sb[:], t2_sb[:], inc=(sCB, 1, g2 + 1))

                ve.wait_ge(sAUX, 16)
                ve.tensor_scalar(neglam_sb[:], lam_sb[:], -1.0, None, OP.mult)
                for h in range(H):
                    g = 0
                    pd = 0
                    for tt in range(NT):
                        ve.wait_ge(sPM, h * 32 + g + 1)
                        pd += 1
                        ve.tensor_copy(
                            v_sb[:, tt * 128:(tt + 1) * 128], projb[g % 2][:, 0:128],
                            inc=(sPD, 1, h * 33 + pd),
                        )
                        g += 1
                    pd += 1
                    ve.tensor_scalar(
                        v2_sb[:], v_sb[:], neglam_sb[:, h:h + 1], None, OP.mult,
                        inc=(sPD, 1, h * 33 + pd),
                    )
                    for m, dst in enumerate((q1_sb, k1_sb, q2_sb, k2_sb)):
                        for qt in range(QT):
                            ve.wait_ge(sPM, h * 32 + g + 1)
                            pd += 1
                            ve.tensor_copy(
                                dst[:, qt * 512:(qt + 1) * 512], projb[g % 2][:, 0:512],
                                inc=(sPD, 1, h * 33 + pd),
                            )
                            g += 1
                    for qt in range(QT):
                        grp = h * QT + qt
                        if grp >= 2:
                            divide(grp - 2)
                        ve.wait_ge(sPS0, grp + 1)
                        if grp >= 3:
                            ve.wait_ge(sCB, grp - 2)
                        b = (grp % 3) * 1024
                        ve.tensor_copy(sc_sb[:, b:b + 512], pS0[:])
                        ve.wait_ge(sPVD, NC_ * (grp + 1))
                        ve.tensor_copy(sc_sb[:, b + 512:b + 1024], pS1[:], inc=(sOC, 1, grp + 1))
                divide(H * QT - 2)
                divide(H * QT - 1)
                vchain[0] = ve.last
                vchain[1] = ve._n

        with nc.sbuf_tensor("qT", [128, NC_ * S], BF16, side="right") as qT_sb:
            with ExitStack() as phc:
                Ec = phc.enter_context
                wo_sb = Ec(nc.sbuf_tensor("woS", [128, NC_ * D], BF16))
                ocol_sb = Ec(nc.sbuf_tensor("ocol", [128, 2 * H * 128], BF16))
                xin_sb = Ec(nc.sbuf_tensor("xin", [128, 2 * D], F32))
                z_sb = Ec(nc.sbuf_tensor("zz", [128, 2 * D], F32))
                y_sb = Ec(nc.sbuf_tensor("yy", [128, D], F32))
                t_sb = Ec(nc.sbuf_tensor("tt_", [128, D], F32))
                dump_sb = Ec(nc.sbuf_tensor("dump", [128, D], F32))
                q8_sb = Ec(nc.sbuf_tensor("qi8", [128, D], I8))
                qb_sb = Ec(nc.sbuf_tensor("qbf", [128, D], BF16))
                yf16_sb = Ec(nc.sbuf_tensor("yf16", [128, 2 * D], F16))
                st1_sb = Ec(nc.sbuf_tensor("st1", [128, 2], F32))
                st2_sb = Ec(nc.sbuf_tensor("st2", [128, 2], F32))
                st3_sb = Ec(nc.sbuf_tensor("st3", [128, 1], F32))
                st4_sb = Ec(nc.sbuf_tensor("st4", [128, 1], F32))
                st5_sb = Ec(nc.sbuf_tensor("st5", [128, 1], F32))
                st6_sb = Ec(nc.sbuf_tensor("st6", [128, 1], F32))
                pattnL = Ec(nc.psum_tensor("pattnL", [128, 1024], F32))
                pattnH = Ec(nc.psum_tensor("pattnH", [128, 1024], F32))
                ptrA = Ec(nc.psum_tensor("ptrA", [128, 1024], BF16))
                ptrB = Ec(nc.psum_tensor("ptrB", [128, 1024], BF16))
                pattnC = Ec(nc.psum_tensor("pattnC", [128, 1024], F32))
                blk2 = blkM
                path = [pattnL, pattnH, pattnC]
                trb = [ptrA, ptrB]
                wosem = [sWO, sWO2, sWO3, sWO4]

                @blk2.sync
                def _(sy):
                    sy.wait_ge(sCS, 32 * NC_)
                    sy.wait_ge(sWQ, 16 * H)
                    sy.wait_ge(sPM, 32 * H)

                    def ywrite(tt):
                        sy.wait_ge(sYD, tt + 1)
                        sy.wait_ge(sYDMA, 16 * tt)
                        sy.dma_start(out=ysc_d[tt], in_=yf16_sb[:, (tt % 2) * D:(tt % 2) * D + D]).then_inc(sYDMA, 16)

                    for tt in range(NT):
                        if tt >= 2:
                            sy.wait_ge(sAR, 2 * (tt - 1))
                        sy.wait_ge(sODMA, 16 * 61)
                        if tt >= 4:
                            sy.wait_ge(sODM2, 16 * (tt // 4))
                        sy.wait_ge(sOI, 16 * tt)
                        sy.dma_start(
                            out=ocol_sb[:, (tt % 2) * 2048:(tt % 2) * 2048 + 2048].rearrange("p (h t) -> p h t", h=H),
                            in_=osc_d[tt].rearrange("h p t -> p h t"),
                        ).then_inc(sOI, 16)
                        if tt >= 2:
                            sy.wait_ge(sZD, 2 * (tt - 1))
                        sy.wait_ge(sXI, 16 * tt)
                        sy.dma_start(
                            out=xin_sb[:, (tt % 2) * D:(tt % 2) * D + D], in_=x_d[tt * 128:(tt + 1) * 128, :]
                        ).then_inc(sXI, 16)
                        if tt >= 2:
                            ywrite(tt - 2)
                    ywrite(NT - 2)
                    ywrite(NT - 1)

                @blk2.gpsimd
                def _(g):
                    g.wait_ge(sPM, 32 * H)
                    g.wait_ge(sCS, 16 * NC_)
                    for c in range(NC_):
                        sem = wosem[c % 4]
                        g.wait_ge(sem, 16 * (c // 4))
                        g.dma_start(
                            out=wo_sb[:, c * D:(c + 1) * D],
                            in_=wob_d[c],
                        ).then_inc(sem, 16)
                    for grp in range(61, H * QT):
                        g.wait_ge(sCB, grp + 1)
                        h_, qt_ = grp // QT, grp % QT
                        if grp > 61:
                            g.wait_ge(sODM2, 16 * (grp - 61))
                        g.dma_start(
                            out=osc_d[qt_ * 4:qt_ * 4 + 4, h_].rearrange("j p t -> p j t"),
                            in_=oout_sb[:, (grp % 2) * 512:(grp % 2) * 512 + 512].rearrange(
                                "p (j t) -> p j t", j=4),
                        ).then_inc(sODM2, 16)

                @blk2.tensor
                def _(te):
                    te.wait_ge(sEXP, 16 * H * QT - 1)

                    def transposes(tt):
                        if tt == 0:
                            te.wait_ge(sOC2, H * QT)
                        te.wait_ge(sQ, tt + 1)
                        for grp8 in range(2):
                            gg = 2 * tt + grp8
                            if gg >= 2:
                                te.wait_ge(sTRE, gg - 1)
                            for j8 in range(8):
                                j = grp8 * 8 + j8
                                mm = te.transpose(
                                    trb[gg % 2][:, j8 * 128:(j8 + 1) * 128],
                                    qb_sb[:, j * 128:(j + 1) * 128],
                                    id_sb[:],
                                )
                            mm.then_inc(sTR, 1)

                    for tt in range(NT):
                        for half in range(2):
                            s_ = 2 * tt + half
                            if s_ >= 3:
                                te.wait_ge(sZD, s_ - 2)
                            if tt == 0 and half == 1:
                                te.wait_ge(sEXP, 16 * H * QT)
                            if s_ == 2:
                                te.wait_ge(sOC, H * QT)
                            for c in range(NC_):
                                if half == 0 and c == 0:
                                    te.wait_ge(sOI, 16 * (tt + 1))
                                if tt == 0 and half == 0:
                                    te.wait_ge(wosem[c % 4], 16 * (c // 4 + 1))
                                for nt in (half * 2, half * 2 + 1):
                                    mm = te.matmul(
                                        path[s_ % 3][:, (nt - half * 2) * 512:(nt - half * 2) * 512 + 512],
                                        ocol_sb[:, (tt % 2) * 2048 + c * 128:(tt % 2) * 2048 + (c + 1) * 128],
                                        wo_sb[:, c * D + nt * 512: c * D + nt * 512 + 512],
                                        start=(c == 0), stop=(c == NC_ - 1),
                                    )
                                if tt == NT - 1 and half == 1 and c < NC_ - 1:
                                    mm.then_inc(sWFC, 1)
                            mm.then_inc(sAR, 1)
                        if tt >= 2:
                            transposes(tt - 2)
                    transposes(NT - 2)
                    transposes(NT - 1)

                @blk2.scalar
                def _(sc):
                    sc.wait_ge(sCS, 32 * NC_)
                    sc.wait_ge(sODMA, 16 * 61)
                    sc.wait_ge(sODM2, 48)

                    def evacs(tt):
                        for grp8 in range(2):
                            gg = 2 * tt + grp8
                            sc.wait_ge(sTR, gg + 1)
                            out_ap = qT_sb[:].rearrange("p (j r) -> p j r", j=NC_)[
                                :, grp8 * 8:(grp8 + 1) * 8, tt * 128:(tt + 1) * 128]
                            sc.copy(
                                out_ap,
                                trb[gg % 2][:].rearrange("p (j r) -> p j r", j=8),
                            ).then_inc(sTRE, 1)

                    for tt in range(NT):
                        if tt >= 1:
                            sc.wait_ge(sVST, tt)
                            sc.activation(st5_sb[:], st6_sb[:], AF.Sqrt).then_inc(sSQT, 1)
                        sc.wait_ge(sZD, 2 * (tt + 1))
                        if tt >= 2:
                            sc.wait_ge(sVST, tt - 1)
                        if tt >= 1:
                            sc.wait_ge(sST, 2 * tt)
                        sc.activation(dump_sb[:], z_sb[:, (tt % 2) * D:(tt % 2) * D + D],
                                      AF.Copy, accum_out=st1_sb[:, tt % 2:tt % 2 + 1]).then_inc(sST, 1)
                        sc.wait_ge(sST, 2 * tt + 1)
                        sc.activation(dump_sb[:], z_sb[:, (tt % 2) * D:(tt % 2) * D + D],
                                      AF.Square, accum_out=st2_sb[:, tt % 2:tt % 2 + 1]).then_inc(sST, 1)
                        if tt >= 1:
                            sc.wait_ge(sAF2, tt)
                            if tt >= 2:
                                sc.wait_ge(sTR, 2 * (tt - 1))
                            sc.copy(qb_sb[:], q8_sb[:]).then_inc(sQ, 1)
                        if tt >= 2:
                            evacs(tt - 2)
                    sc.wait_ge(sVST, NT)
                    sc.activation(st5_sb[:], st6_sb[:], AF.Sqrt).then_inc(sSQT, 1)
                    sc.wait_ge(sAF2, NT)
                    sc.wait_ge(sTR, 2 * (NT - 1))
                    sc.copy(qb_sb[:], q8_sb[:]).then_inc(sQ, 1)
                    evacs(NT - 2)
                    evacs(NT - 1)

                @blk2.vector
                def _(ve):
                    ve = ChainedEng(ve, sVC, vchain[0])
                    ve._n = vchain[1]
                    ve.wait_ge(sCS, 32 * NC_)

                    def post(k):
                        ve.wait_ge(sST, 2 * (k + 1))
                        ve.tensor_scalar(st1_sb[:, k % 2:k % 2 + 1], st1_sb[:, k % 2:k % 2 + 1], INV_D, None, OP.mult)
                        ve.tensor_scalar(st2_sb[:, k % 2:k % 2 + 1], st2_sb[:, k % 2:k % 2 + 1], INV_D, None, OP.mult)
                        ve.tensor_mul(st6_sb[:], st1_sb[:, k % 2:k % 2 + 1], st1_sb[:, k % 2:k % 2 + 1])
                        ve.tensor_sub(st6_sb[:], st2_sb[:, k % 2:k % 2 + 1], st6_sb[:])
                        ve.tensor_scalar(st6_sb[:], st6_sb[:], LN_EPS, None, OP.add, inc=(sVST, 1, k + 1))
                        ve.wait_ge(sSQT, k + 1)
                        ve.reciprocal(st3_sb[:], st5_sb[:])
                        ve.tensor_mul(st2_sb[:, k % 2:k % 2 + 1], st6_sb[:], st3_sb[:])
                        ve.tensor_mul(st2_sb[:, k % 2:k % 2 + 1], st2_sb[:, k % 2:k % 2 + 1], st3_sb[:])
                        ve.tensor_scalar(st2_sb[:, k % 2:k % 2 + 1], st2_sb[:, k % 2:k % 2 + 1], -0.5, 1.5, OP.mult, OP.add)
                        ve.tensor_mul(st3_sb[:], st3_sb[:], st2_sb[:, k % 2:k % 2 + 1])
                        ve.tensor_mul(st4_sb[:], st1_sb[:, k % 2:k % 2 + 1], st3_sb[:])
                        ve.tensor_scalar(st4_sb[:], st4_sb[:], -1.0, None, OP.mult)
                        ve.tensor_scalar(y_sb[:], z_sb[:, (k % 2) * D:(k % 2) * D + D], st3_sb[:], st4_sb[:],
                                         OP.mult, OP.add)
                        ve.tensor_reduce(st5_sb[:], y_sb[:], AX, OP.max,
                                         apply_absolute_value=True)
                        ve.tensor_scalar(scale_all[:, k:k + 1], st5_sb[:], Q_EPS,
                                         INV127, OP.max, OP.mult)
                        ve.reciprocal(st6_sb[:], scale_all[:, k:k + 1])
                        ve.tensor_scalar(t_sb[:], y_sb[:], st6_sb[:], MAGIC, OP.mult, OP.add)
                        if k >= 1:
                            ve.wait_ge(sQ, k)
                        ve.tensor_scalar(q8_sb[:], t_sb[:], MAGIC, None, OP.subtract,
                                         inc=(sAF2, 1, k + 1))
                        if k >= 2:
                            ve.wait_ge(sYDMA, 16 * (k - 1))
                        ve.tensor_copy(yf16_sb[:, (k % 2) * D:(k % 2) * D + D], y_sb[:], inc=(sYD, 1, k + 1))

                    for tt in range(NT):
                        ve.wait_ge(sXI, 16 * (tt + 1))
                        if tt >= 2:
                            ve.wait_ge(sST, 2 * (tt - 1))
                        for half in range(2):
                            ve.wait_ge(sAR, 2 * tt + half + 1)
                            ve.tensor_add(
                                z_sb[:, (tt % 2) * D + half * 1024:(tt % 2) * D + half * 1024 + 1024],
                                path[(2 * tt + half) % 3][:],
                                xin_sb[:, (tt % 2) * D + half * 1024:(tt % 2) * D + half * 1024 + 1024],
                                inc=(sZD, 1, 2 * tt + half + 1))
                        if tt >= 1:
                            post(tt - 1)
                    post(NT - 1)
                    vchain[0] = ve.last
                    vchain[1] = ve._n

            with ExitStack() as phd:
                Ed = phd.enter_context
                wsb_sb = Ed(nc.sbuf_tensor("wsb_sb", [128, D], F32))
                bb_sb = Ed(nc.sbuf_tensor("bb_sb", [128, D], F32))
                wf_sb = Ed(nc.sbuf_tensor("wfsb", [128, NC_ * D], BF16))
                yin_sb = Ed(nc.sbuf_tensor("yin", [128, 2 * D], F16))
                f_sb = Ed(nc.sbuf_tensor("ff", [128, D], F32))
                f16_sb = Ed(nc.sbuf_tensor("f16", [128, D], F16))
                z2_sb = Ed(nc.sbuf_tensor("z2", [128, 2 * D], F32))
                o_sb = Ed(nc.sbuf_tensor("oo", [128, 2 * D], F32))
                su1_sb = Ed(nc.sbuf_tensor("su1", [128, 2], F32))
                su2_sb = Ed(nc.sbuf_tensor("su2", [128, 2], F32))
                su3_sb = Ed(nc.sbuf_tensor("su3", [128, 1], F32))
                su4_sb = Ed(nc.sbuf_tensor("su4", [128, 1], F32))
                su5_sb = Ed(nc.sbuf_tensor("su5", [128, 1], F32))
                su6_sb = Ed(nc.sbuf_tensor("su6", [128, 1], F32))
                pffnA = Ed(nc.psum_tensor("pffnA", [128, 2048], F32))
                pffnB = Ed(nc.psum_tensor("pffnB", [128, 2048], F32))
                blk3 = blkM
                pffn = [pffnA, pffnB]

                @blk3.gpsimd
                def _(g):
                    g.wait_ge(sWFC, 2)
                    g.dma_start(out=wsb_sb[:], in_=wsb_d).then_inc(sTQ2, 16)
                    g.wait_ge(sWFC, 4)
                    g.wait_ge(sTQ2, 16)
                    g.dma_start(out=bb_sb[:], in_=bb_d).then_inc(sTQ2, 16)
                    g.wait_ge(sCS, 32 * NC_)
                    wfsemd = [sWF, sWC, sYR, sAFF]
                    for j in range(4):
                        if j == 0:
                            g.wait_ge(sWFC, 8)
                        elif j == 1:
                            g.wait_ge(sWFC, 12)
                        else:
                            g.wait_ge(sAR, 2 * NT)
                            if j == 3:
                                g.wait_ge(sZD, 2 * NT - 2)
                        g.dma_start(
                            out=wf_sb[:, j * 4 * D:(j + 1) * 4 * D].rearrange(
                                "p (c d) -> p c d", c=4),
                            in_=wfb_d[j * 4:(j + 1) * 4].rearrange("c p d -> p c d"),
                        ).then_inc(wfsemd[j], 16)

                @blk3.sync
                def _(sy):
                    sy.wait_ge(sZD, 2 * NT)
                    sy.wait_ge(sYDMA, 16 * NT)

                    def owrite(tt):
                        sy.wait_ge(sOUTV, tt + 1)
                        sy.wait_ge(sOUT, 16 * tt)
                        sy.dma_start(
                            out=out_d[tt * 128:(tt + 1) * 128, :], in_=o_sb[:, (tt % 2) * D:(tt % 2) * D + D]
                        ).then_inc(sOUT, 16)

                    for tt in range(NT):
                        if tt >= 2:
                            sy.wait_ge(sZ2, tt - 1)
                        sy.wait_ge(sYI, 16 * tt)
                        sy.dma_start(out=yin_sb[:, (tt % 2) * D:(tt % 2) * D + D], in_=ysc_d[tt]).then_inc(sYI, 16)
                        if tt >= 2:
                            owrite(tt - 2)
                    owrite(NT - 2)
                    owrite(NT - 1)
                    sy.wait_ge(sOUT, 16 * NT)

                @blk3.scalar
                def _(sc):
                    for tt in range(NT):
                        if tt >= 1:
                            sc.wait_ge(sVS2, tt)
                            sc.activation(su5_sb[:], su6_sb[:], AF.Sqrt).then_inc(sSQ2, 1)
                        sc.wait_ge(sZ2, tt + 1)
                        if tt >= 2:
                            sc.wait_ge(sVS2, tt - 1)
                        if tt >= 1:
                            sc.wait_ge(sS2T, 2 * tt)
                        sc.activation(f_sb[:], z2_sb[:, (tt % 2) * D:(tt % 2) * D + D],
                                      AF.Copy, accum_out=su1_sb[:, tt % 2:tt % 2 + 1]).then_inc(sS2T, 1)
                        sc.wait_ge(sS2T, 2 * tt + 1)
                        sc.activation(f_sb[:], z2_sb[:, (tt % 2) * D:(tt % 2) * D + D],
                                      AF.Square, accum_out=su2_sb[:, tt % 2:tt % 2 + 1]).then_inc(sS2T, 1)
                    sc.wait_ge(sVS2, NT)
                    sc.activation(su5_sb[:], su6_sb[:], AF.Sqrt).then_inc(sSQ2, 1)

                @blk3.tensor
                def _(te):
                    te.wait_ge(sZD, 2 * NT)
                    for tt in range(NT):
                        if tt == 1:
                            te.wait_ge(sTRE, 2 * NT)
                        te.wait_ge(sTRE, 2 * tt + 2)
                        if tt >= 2:
                            te.wait_ge(sPF, tt - 1)
                        for c in range(NC_):
                            if tt == 0 and c % 4 == 0:
                                te.wait_ge((sWF, sWC, sYR, sAFF)[c // 4], 16)
                            for nt in range(4):
                                mm = te.matmul(
                                    pffn[tt % 2][:, nt * 512:(nt + 1) * 512],
                                    qT_sb[:, c * S + tt * 128: c * S + tt * 128 + 128],
                                    wf_sb[:, c * D + nt * 512: c * D + nt * 512 + 512],
                                    start=(c == 0), stop=(c == NC_ - 1),
                                )
                        mm.then_inc(sAR, 1)

                @blk3.vector
                def _(ve):
                    ve = ChainedEng(ve, sVC, vchain[0])
                    ve._n = vchain[1]
                    ve.wait_ge(sTQ2, 32)

                    def post2(k):
                        ve.wait_ge(sS2T, 2 * (k + 1))
                        ve.tensor_scalar(su1_sb[:, k % 2:k % 2 + 1], su1_sb[:, k % 2:k % 2 + 1], INV_D, None, OP.mult)
                        ve.tensor_scalar(su2_sb[:, k % 2:k % 2 + 1], su2_sb[:, k % 2:k % 2 + 1], INV_D, None, OP.mult)
                        ve.tensor_mul(su6_sb[:], su1_sb[:, k % 2:k % 2 + 1], su1_sb[:, k % 2:k % 2 + 1])
                        ve.tensor_sub(su6_sb[:], su2_sb[:, k % 2:k % 2 + 1], su6_sb[:])
                        ve.tensor_scalar(su6_sb[:], su6_sb[:], LN_EPS, None, OP.add, inc=(sVS2, 1, k + 1))
                        ve.wait_ge(sSQ2, k + 1)
                        ve.reciprocal(su3_sb[:], su5_sb[:])
                        ve.tensor_mul(su2_sb[:, k % 2:k % 2 + 1], su6_sb[:], su3_sb[:])
                        ve.tensor_mul(su2_sb[:, k % 2:k % 2 + 1], su2_sb[:, k % 2:k % 2 + 1], su3_sb[:])
                        ve.tensor_scalar(su2_sb[:, k % 2:k % 2 + 1], su2_sb[:, k % 2:k % 2 + 1], -0.5, 1.5, OP.mult, OP.add)
                        ve.tensor_mul(su3_sb[:], su3_sb[:], su2_sb[:, k % 2:k % 2 + 1])
                        ve.tensor_mul(su4_sb[:], su1_sb[:, k % 2:k % 2 + 1], su3_sb[:])
                        ve.tensor_scalar(su4_sb[:], su4_sb[:], -1.0, None, OP.mult)
                        if k >= 2:
                            ve.wait_ge(sOUT, 16 * (k - 1))
                        ve.tensor_scalar(o_sb[:, (k % 2) * D:(k % 2) * D + D], z2_sb[:, (k % 2) * D:(k % 2) * D + D],
                                         su3_sb[:], su4_sb[:], OP.mult, OP.add,
                                         inc=(sOUTV, 1, k + 1))

                    for tt in range(NT):
                        ve.wait_ge(sAR, 2 * NT + tt + 1)
                        ve.wait_ge(sYI, 16 * (tt + 1))
                        if tt >= 1:
                            ve.wait_ge(sS2T, 2 * tt)
                        if tt >= 2:
                            ve.wait_ge(sOUTV, tt - 1)
                        ve.scalar_tensor_tensor(
                            f_sb[:], pffn[tt % 2][:], scale_all[:, tt:tt + 1], wsb_sb[:],
                            OP.mult, OP.mult, inc=(sPF, 1, tt + 1),
                        )
                        ve.tensor_add(f16_sb[:], f_sb[:], bb_sb[:])
                        ve.tensor_add(z2_sb[:, (tt % 2) * D:(tt % 2) * D + D], f16_sb[:], yin_sb[:, (tt % 2) * D:(tt % 2) * D + D], inc=(sZ2, 1, tt + 1))
                        if tt >= 1:
                            post2(tt - 1)
                    post2(NT - 1)

    return nc
    return nc


_NC_CACHE = None


def kernel(L0_attn_x, L0_attn_q1_w, L0_attn_k1_w, L0_attn_q2_w, L0_attn_k2_w,
           L0_attn_v_w, L0_attn_o_w, L0_attn_lambda_param,
           L0_ffn_weight, L0_ffn_w_scale, L0_ffn_bias):
    global _NC_CACHE
    B = L0_attn_x.shape[0]

    def head_major(w):
        wt = np.ascontiguousarray(np.asarray(w).T)
        return wt.reshape(16, 128, 16, 128).transpose(2, 1, 0, 3)

    parts = [head_major(w) for w in (L0_attn_v_w, L0_attn_q1_w, L0_attn_k1_w,
                                     L0_attn_q2_w, L0_attn_k2_w)]
    wqkv = np.ascontiguousarray(np.stack(parts, axis=2))
    wqkv = wqkv.reshape(16, 128, 5 * 16 * 128)

    wo = np.ascontiguousarray(np.asarray(L0_attn_o_w).T)
    wffn = np.ascontiguousarray(np.asarray(L0_ffn_weight).T)
    wsb = np.ascontiguousarray(np.broadcast_to(
        np.asarray(L0_ffn_w_scale)[:, 0][None, :], (128, 2048)))
    bb = np.ascontiguousarray(np.broadcast_to(
        np.asarray(L0_ffn_bias)[None, :], (128, 2048)))
    lam = np.ascontiguousarray(np.broadcast_to(
        np.asarray(L0_attn_lambda_param)[None, :], (128, 16)))

    if _NC_CACHE is None:
        _NC_CACHE = build_program()
    nc = _NC_CACHE

    in_maps = []
    for b in range(B):
        xb = np.ascontiguousarray(np.asarray(L0_attn_x[b]))
        in_maps.append({
            "xT": np.ascontiguousarray(xb.T),
            "x": xb,
            "wqkv": wqkv, "wo": wo, "wffn": wffn,
            "wsb": wsb, "bb": bb, "lam": lam,
        })
    res = run_bass_kernel_spmd(nc, in_maps, core_ids=list(range(8)))
    out = np.stack([np.asarray(r["out"]) for r in res.results], axis=0)
    return out.astype(np.float32)
